# version 04
# baseline (speedup 1.0000x reference)
import contextlib
import numpy as np
import ml_dtypes
import concourse.bass as bass
import concourse.mybir as mybir
from concourse.bass_utils import run_bass_kernel_spmd

F32 = mybir.dt.float32
BF16 = mybir.dt.bfloat16
I32 = mybir.dt.int32
AF = mybir.ActivationFunctionType
ALU = mybir.AluOpType
AX = mybir.AxisListType
BF = ml_dtypes.bfloat16

B = 2
S = 4096
D = 2048
NCORE = 8
TOK = 1024
NT = TOK // 128
GSZ = 512
NH = 32
HD = 64
INTER0 = 5632
DFF1 = 8192
KT_PAD = (16, 32)
MASK_W = 16
MASK_START = (0, 16)
EXP_SHIFT = -20.0
SCALE_ATT = 0.125
NIC = D // 128

PAY1W = 256
PAY1_ROWS = 4 * D + 1
PAY2W = 1024
PAY2_ROWS = 2048 + 2048 + 1024 + 1024
PAY3W = 1024
PAY3_ROWS = 512 + 512


def core_positions(c):
    cc = c % 4
    return (cc, 7 - cc)


def core_token_rows(c):
    b = c // 4
    p0, p1 = core_positions(c)
    return [(b * S + p0 * GSZ, p0), (b * S + p1 * GSZ, p1)]



_ENG_Q = {"tensor": ("t", 1), "vector": ("v", 1), "scalar": ("s", 1),
          "gpsimd": ("g", 1)}
_DMA_Q = {"sync": "sd", "gpsimd": "gd"}
_SELF_GATED = ("sd", "gd", "cc")
_ALL_Q = ("t", "v", "s", "g", "sd", "gd", "cc")


class SB:
    def __init__(self):
        self.items = []
        self.cum = {q: 0 for q in _ALL_Q}
        self.w = {}
        self.r = {}
        self.floor = {q: 0 for q in _ALL_Q}

    def _dep(self, waits, q, val):
        waits[q] = max(waits.get(q, 0), val)

    def _add(self, engine, q, fn, reads, writes, inc):
        waits = {}
        for q2, v2 in self.floor.items():
            if v2 > 0:
                waits[q2] = v2
        if q in _SELF_GATED and self.cum[q] > waits.get(q, 0):
            waits[q] = self.cum[q]
        for k in reads:
            if k in self.w:
                self._dep(waits, *self.w[k])
        for k in writes:
            if k in self.w:
                self._dep(waits, *self.w[k])
            for rq, rv in self.r.get(k, {}).items():
                self._dep(waits, rq, rv)
        after = self.cum[q] + inc
        self.cum[q] = after
        for k in reads:
            d = self.r.setdefault(k, {})
            d[q] = max(d.get(q, 0), after)
        for k in writes:
            self.w[k] = (q, after)
            self.r[k] = {}
        self.items.append((engine, fn, waits, q, inc))

    def add(self, engine, fn, reads=(), writes=(), q=None, inc=None):
        if q is None:
            q, qinc = _ENG_Q[engine]
        else:
            qinc = 16 if q in _SELF_GATED else 1
        self._add(engine, q, fn, reads, writes, qinc if inc is None else inc)

    def dma(self, engine, out, in_, reads=(), writes=(), **kw):
        q = _DMA_Q[engine]
        self._add(engine, q, lambda e: e.dma_start(out=out, in_=in_, **kw),
                  reads, writes, 16)

    def cc(self, fn, reads=(), writes=()):
        self._add("gpsimd", "cc", fn, reads, writes, 16)

    def barrier(self):
        for q in _ALL_Q:
            self.floor[q] = self.cum[q]


def run_block(nc, sbd, name="main"):
    with contextlib.ExitStack() as st:
        sems = {q: st.enter_context(nc.semaphore(f"q_{q}_{name}"))
                for q in _ALL_Q}
        block = st.enter_context(nc.Block(name))
        totals = dict(sbd.cum)

        def mk(engname):
            def _(e):
                done = {}
                own = {}
                for (en, fn, waits, q, inc) in sbd.items:
                    if en != engname:
                        continue
                    for wq, wv in waits.items():
                        if wv <= 0 or done.get(wq, 0) >= wv:
                            continue
                        e.wait_ge(sems[wq], wv)
                        done[wq] = wv
                    fn(e).then_inc(sems[q], inc)
                    own[q] = own.get(q, 0) + inc
                for q2 in _ALL_Q:
                    if totals[q2] > 0 and done.get(q2, 0) < totals[q2]:
                        e.wait_ge(sems[q2], totals[q2])
            return _

        block.sync(mk("sync"))
        block.gpsimd(mk("gpsimd"))
        block.tensor(mk("tensor"))
        block.vector(mk("vector"))
        block.scalar(mk("scalar"))



def host_rope_tables():
    j = np.arange(HD, dtype=np.float64)
    inv = 1.0 / (10000.0 ** ((2.0 * (j % 32)) / HD))
    pos = np.arange(S, dtype=np.float64)
    th = pos[None, :] * inv[:, None]
    cos2 = np.concatenate([np.cos(th), np.cos(th)], 0).astype(np.float32)
    sin2 = np.concatenate([np.sin(th), np.sin(th)], 0).astype(np.float32)
    return cos2, sin2


def host_psgn():
    P = np.zeros((128, 128), dtype=np.float32)
    for blk in range(2):
        o = blk * 64
        for j in range(32):
            P[o + j + 32, o + j] = -1.0
            P[o + j, o + j + 32] = 1.0
    return P.astype(BF)


def host_masks(c):
    p0, p1 = core_positions(c)
    out = np.zeros((2 * MASK_W, 128, GSZ), dtype=np.float32)
    for slot, p in ((0, p0), (1, p1)):
        for w in range(MASK_W):
            kt = MASK_START[slot] + w
            m = out[slot * MASK_W + w]
            for s4 in range(4):
                diag = 4 * p + s4
                if kt < diag:
                    m[:, s4 * 128:(s4 + 1) * 128] = 1.0
                elif kt == diag:
                    kk = np.arange(128)[:, None]
                    qq = np.arange(128)[None, :]
                    m[:, s4 * 128:(s4 + 1) * 128] = (kk <= qq).astype(np.float32)
    return np.ascontiguousarray(out.astype(BF))


def host_band_vec(v, width=128):
    v = np.asarray(v, dtype=np.float32)
    n = v.shape[0]
    m = -(-n // width)
    out = np.zeros((width, m), dtype=np.float32)
    for j in range(m):
        seg = v[j * width:(j + 1) * width]
        out[:seg.shape[0], j] = seg
    return np.ascontiguousarray(out)



class Ctx:
    def __init__(self, name):
        self.nc = bass.Bass("TRN2", target_bir_lowering=False, name=name)
        self.sb = SB()
        self._fc = {}
        self._stack = None
        self._uid = 0

    def uid(self):
        self._uid += 1
        return self._uid

    def sbuf_p(self, name, shape, dtype=F32):
        return self.nc.alloc_sbuf_tensor(name, shape, dtype).ap()

    @contextlib.contextmanager
    def phase(self, name):
        assert self._stack is None, "phases cannot nest"
        self._stack = contextlib.ExitStack()
        self._pstack = contextlib.ExitStack()
        try:
            yield self
        finally:
            self.sb.barrier()
            self._pstack.close()
            self._pstack = None
            self._stack.close()
            self._stack = None

    def psum_flush(self):
        self.sb.barrier()
        self._pstack.close()
        self._pstack = contextlib.ExitStack()

    def sbuf(self, name, shape, dtype=F32):
        h = self._stack.enter_context(
            self.nc.sbuf_tensor(f"{name}_{self.uid()}", shape, dtype))
        return h.ap()

    def psum(self, name, dtype=F32):
        shape = [128, 512] if dtype == F32 else [128, 1024]
        h = self._pstack.enter_context(
            self.nc.psum_tensor(f"{name}_{self.uid()}", shape, dtype))
        return h.ap()

    def inp(self, name, shape, dtype=F32):
        return self.nc.dram_tensor(name, shape, dtype, kind="ExternalInput").ap()

    def out(self, name, shape, dtype=F32):
        return self.nc.dram_tensor(name, shape, dtype, kind="ExternalOutput").ap()

    def dram(self, name, shape, dtype=F32):
        return self.nc.dram_tensor(name, shape, dtype, kind="Internal").ap()

    def const(self, name, arr):
        return self.nc.inline_tensor(np.ascontiguousarray(arr),
                                     f"{name}_{self.uid()}").ap()

    def fconst(self, val):
        if val not in self._fc:
            i = len(self._fc)
            dram = self.const(f"fc{i}", np.full((128, 1), val, dtype=np.float32))
            tile = self.sbuf_p(f"fct{i}", [128, 1])
            key = f"fc{i}"
            self.sb.dma("sync", tile, dram, writes=[key])
            self._fc[val] = (tile, key)
        return self._fc[val]


def alloc_quant_tmp(ctx):
    return {
        "B": ctx.sbuf("q_B", [128, 2048], F32),
        "C": ctx.sbuf("q_C", [128, 2048], F32),
        "D": ctx.sbuf("q_D", [128, 2048], F32),
        "E": ctx.sbuf("q_E", [128, 2048], F32),
        "am": ctx.sbuf("q_am", [128, 128], F32),
        "sc": ctx.sbuf("q_sc", [128, 128], F32),
        "rin": ctx.sbuf("q_rin", [128, 128], F32),
    }


def emit_fp4_quant(ctx, src, dst, tmp, src_key, dst_key):
    sb = ctx.sb
    F = src.shape[-1]
    nb = F // 16
    src3 = src.rearrange("p (a b) -> p a b", b=16)
    am, sc, rin = tmp["am"][:, :nb], tmp["sc"][:, :nb], tmp["rin"][:, :nb]
    Bb, C = tmp["B"][:, :F], tmp["C"][:, :F]
    Dd, E = tmp["D"][:, :F], tmp["E"][:, :F]
    B3 = Bb.rearrange("p (a b) -> p a b", b=16)
    C3 = C.rearrange("p (a b) -> p a b", b=16)
    sb.add("vector", lambda e: e.tensor_reduce(
        am, src3, AX.X, ALU.max, apply_absolute_value=True),
        reads=[src_key], writes=["qam"])
    sb.add("vector", lambda e: e.tensor_scalar(am, am, 1e-30, None, ALU.max),
        reads=["qam"], writes=["qam"])
    sb.add("vector", lambda e: e.tensor_scalar(sc, am, 0.16666667, None, ALU.mult),
        reads=["qam"], writes=["qsc"])
    sb.add("vector", lambda e: e.reciprocal(rin, sc),
        reads=["qsc"], writes=["qrin"])
    sb.add("scalar", lambda e: e.activation(C, src, AF.Abs),
        reads=[src_key], writes=["qC"])
    rb = rin.unsqueeze(2).broadcast_to([128, nb, 16])
    sb.add("vector", lambda e: e.tensor_tensor(B3, C3, rb, ALU.mult),
        reads=["qC", "qrin"], writes=["qB"])
    sb.add("vector", lambda e: e.tensor_scalar(C, Bb, 2.0, None, ALU.is_ge),
        reads=["qB"], writes=["qC"])
    sb.add("vector", lambda e: e.tensor_scalar(Dd, C, -1.0, 2.0, ALU.mult, ALU.add),
        reads=["qC"], writes=["qD"])
    sb.add("vector", lambda e: e.tensor_scalar(E, C, 0.5, 0.5, ALU.mult, ALU.add),
        reads=["qC"], writes=["qE"])
    sb.add("vector", lambda e: e.tensor_scalar(C, Bb, 4.0, None, ALU.is_ge),
        reads=["qB"], writes=["qC"])
    sb.add("vector", lambda e: e.scalar_tensor_tensor(
        Dd, C, -0.5, Dd, ALU.mult, ALU.add), reads=["qC", "qD"], writes=["qD"])
    sb.add("vector", lambda e: e.scalar_tensor_tensor(
        E, C, 1.0, E, ALU.mult, ALU.add), reads=["qC", "qE"], writes=["qE"])
    sb.add("vector", lambda e: e.tensor_tensor(Bb, Bb, Dd, ALU.mult),
        reads=["qB", "qD"], writes=["qB"])
    sb.add("vector", lambda e: e.tensor_scalar(Dd, Bb, 0.5, None, ALU.is_gt),
        reads=["qB"], writes=["qD"])
    for bnd in (1.5, 2.5, 3.5):
        sb.add("vector", lambda e, bnd=bnd: e.scalar_tensor_tensor(
            Dd, Bb, bnd, Dd, ALU.is_gt, ALU.add), reads=["qB", "qD"], writes=["qD"])
    sb.add("vector", lambda e: e.tensor_tensor(Bb, Dd, E, ALU.mult),
        reads=["qD", "qE"], writes=["qB"])
    sb.add("scalar", lambda e: e.activation(C, src, AF.Sign),
        reads=[src_key], writes=["qC"])
    scb = sc.unsqueeze(2).broadcast_to([128, nb, 16])
    sb.add("vector", lambda e: e.tensor_tensor(C3, C3, scb, ALU.mult),
        reads=["qC", "qsc"], writes=["qC"])
    sb.add("vector", lambda e: e.tensor_tensor(dst, Bb, C, ALU.mult),
        reads=["qB", "qC"], writes=[dst_key])


def emit_ln(ctx, xr, scratch, stats, key):
    sb = ctx.sb
    s_sum, s_ssq = stats[:, 0:1], stats[:, 1:2]
    s_mean, s_var = stats[:, 2:3], stats[:, 3:4]
    s_sd, s_rstd, s_m2 = stats[:, 4:5], stats[:, 5:6], stats[:, 6:7]
    sb.add("vector", lambda e: e.reduce_sum(s_sum, xr, AX.X),
        reads=[key], writes=["lnst"])
    sb.add("scalar", lambda e: e.activation(scratch, xr, AF.Square, accum_out=s_ssq),
        reads=[key], writes=["lnscr", "lnst"])
    sb.add("vector", lambda e: e.tensor_scalar(s_mean, s_sum, 1.0 / D, None, ALU.mult),
        reads=["lnst"], writes=["lnst"])
    sb.add("vector", lambda e: e.tensor_tensor(s_m2, s_mean, s_mean, ALU.mult),
        reads=["lnst"], writes=["lnst"])
    sb.add("vector", lambda e: e.tensor_scalar(s_var, s_ssq, 1.0 / D, None, ALU.mult),
        reads=["lnst"], writes=["lnst"])
    sb.add("vector", lambda e: e.tensor_tensor(s_var, s_var, s_m2, ALU.subtract),
        reads=["lnst"], writes=["lnst"])
    epsap, epskey = ctx.fconst(1e-5)
    sb.add("scalar", lambda e: e.activation(s_sd, s_var, AF.Sqrt, bias=epsap),
        reads=["lnst", epskey], writes=["lnst"])
    sb.add("vector", lambda e: e.reciprocal(s_rstd, s_sd),
        reads=["lnst"], writes=["lnst"])
    sb.add("vector", lambda e: e.tensor_scalar(
        xr, xr, s_mean, s_rstd, ALU.subtract, ALU.mult),
        reads=[key, "lnst"], writes=[key])



def ph_wprep1(ctx, w1ts, w2ts, partials_out, pkey):
    sb = ctx.sb
    with ctx.phase("wprep1"):
        onescol = ctx.const("onescol", np.ones((128, 1), dtype=np.float32))
        tile = ctx.sbuf("w1tile", [128, 8192])
        ab = ctx.sbuf("w1ab", [128, 8192])
        red = ctx.sbuf("w1red", [128, 16])
        red2 = ctx.sbuf("w1red2", [128, 2])
        onest = ctx.sbuf("w1ones", [128, 1])
        ps = ctx.psum("w1ps")
        sb.dma("sync", onest, onescol, writes=["on"])
        w1v = w1ts.rearrange("(n p) d -> n p d", p=128)
        w2v = w2ts.rearrange("(n p) d -> n p d", p=128)
        col = 0
        for (v, n, width) in ((w1v, 2, 8192), (w2v, 8, 2048)):
            for i in range(n):
                sb.dma("sync", tile[:, :width], v[i], writes=["w1t"])
                sb.add("scalar", lambda e, w=width: e.activation(
                    ab[:, :w], tile[:, :w], AF.Abs), reads=["w1t"], writes=["w1a"])
                sb.add("vector", lambda e, c=col, w=width: e.tensor_reduce(
                    red[:, c:c + 1], ab[:, :w], AX.X, ALU.add),
                    reads=["w1a"], writes=["w1r"])
                col += 1
        sb.add("vector", lambda e: e.tensor_reduce(
            red2[:, 0:1], red[:, 0:2], AX.X, ALU.add), reads=["w1r"], writes=["w1r2"])
        sb.add("vector", lambda e: e.tensor_reduce(
            red2[:, 1:2], red[:, 2:10], AX.X, ALU.add), reads=["w1r"], writes=["w1r2"])
        sb.add("tensor", lambda e: e.matmul(ps[0:1, 0:2], onest, red2,
                                            start=True, stop=True),
               reads=["w1r2", "on"], writes=["w1ps"])
        pw = partials_out.shape[1]
        outt = ctx.sbuf("w1outt", [1, 256])
        sb.add("vector", lambda e: e.memset(outt, 0.0), writes=["w1o"])
        sb.add("vector", lambda e: e.tensor_copy(outt[:, 0:2], ps[0:1, 0:2]),
               reads=["w1ps"], writes=["w1o"])
        sb.dma("sync", partials_out, outt[:, 0:pw], reads=["w1o"], writes=[pkey])


def ph_wquant(ctx, srcs, qt_outs, ntiles, okeys):
    sb = ctx.sb
    with ctx.phase("wquant"):
        ident = ctx.const("identf", np.eye(128, dtype=np.float32))
        identt = ctx.sbuf("widentt", [128, 128], F32)
        sb.dma("sync", identt, ident, writes=["wid"])
        tmp = alloc_quant_tmp(ctx)
        wt = ctx.sbuf("wqt_in", [128, 2048])
        qw = ctx.sbuf("wqt_q", [128, 2048], F32)
        oqt = ctx.sbuf("wqt_o", [128, 16 * 128], F32)
        pst = ctx.psum("wqt_ps")
        for (src, qt_out, okey) in zip(srcs, qt_outs, okeys):
            wv = src.rearrange("(t p) d -> t p d", p=128)
            ov = qt_out.rearrange("(a p) o -> p a o", p=128)
            for t in range(ntiles):
                sb.dma("sync", wt, wv[t], writes=["wqw"])
                emit_fp4_quant(ctx, wt, qw, tmp, "wqw", "wqq")
                for ic in range(16):
                    sb.add("tensor", lambda e, ic=ic: e.transpose(
                        pst[:, 0:128], qw[:, ic * 128:(ic + 1) * 128], identt),
                        reads=["wqq", "wid"], writes=["wqps"])
                    sb.add("scalar", lambda e, ic=ic: e.copy(
                        oqt[:, ic * 128:(ic + 1) * 128], pst[:, 0:128]),
                        reads=["wqps"], writes=["wqo"])
                sb.dma("sync", ov[:, :, t * 128:(t + 1) * 128],
                       oqt.rearrange("p (a o) -> p a o", o=128),
                       reads=["wqo"], writes=[okey])


def ph_tern(ctx, partials_all, w1ts, w2ts, w1q_out, w2q_out, pkey, okeys):
    sb = ctx.sb
    with ctx.phase("tern"):
        ones8 = ctx.const("ones8", np.ones((8, 1), dtype=np.float32))
        ones1r = ctx.const("ones1r", np.ones((1, 128), dtype=np.float32))
        part_t = ctx.sbuf("tpart", [8, 16])
        on8 = ctx.sbuf("ton8", [8, 1])
        on1r = ctx.sbuf("ton1", [1, 128])
        scal = ctx.sbuf("tscal", [1, 2])
        thr = ctx.sbuf("tthr", [128, 2])
        wt = ctx.sbuf("twt", [128, 2048])
        ab = ctx.sbuf("tab", [128, 2048])
        sg = ctx.sbuf("tsg", [128, 2048])
        tq = ctx.sbuf("ttq", [128, 2048], BF16)
        ps = ctx.psum("tps")
        sb.dma("sync", part_t, partials_all, reads=[pkey], writes=["tpt"])
        sb.dma("sync", on8, ones8, writes=["to8"])
        sb.dma("sync", on1r, ones1r, writes=["to1"])
        sb.add("tensor", lambda e: e.matmul(ps[0:1, 0:2], on8, part_t[:, 0:2],
                                            start=True, stop=True),
               reads=["tpt", "to8"], writes=["tpsk"])
        sb.add("vector", lambda e: e.tensor_copy(scal, ps[0:1, 0:2]),
               reads=["tpsk"], writes=["tsc"])
        sb.add("vector", lambda e: e.tensor_scalar(
            scal, scal, 1.0 / 16777216.0, None, ALU.mult),
            reads=["tsc"], writes=["tsc"])
        sb.add("vector", lambda e: e.tensor_scalar(scal, scal, 1e-8, None, ALU.max),
            reads=["tsc"], writes=["tsc"])
        sb.add("vector", lambda e: e.tensor_scalar(scal, scal, 0.5, None, ALU.mult),
            reads=["tsc"], writes=["tsc"])
        sb.add("tensor", lambda e: e.matmul(ps[0:128, 2:4], on1r, scal,
                                            start=True, stop=True),
               reads=["tsc", "to1"], writes=["tpsk"])
        sb.add("vector", lambda e: e.tensor_copy(thr, ps[0:128, 2:4]),
               reads=["tpsk"], writes=["tth"])
        for (src, o_t, nt_, width, ti, okey) in (
                (w1ts, w1q_out, 2, 8192, 0, okeys[0]),
                (w2ts, w2q_out, 8, 2048, 1, okeys[1])):
            v = src.rearrange("(n p) d -> n p d", p=128)
            o = o_t.rearrange("(n p) d -> n p d", p=128)
            nch = width // 2048
            for i in range(nt_):
                for ch in range(nch):
                    cs = slice(ch * 2048, (ch + 1) * 2048)
                    sb.dma("sync", wt, v[i][:, cs], writes=["twq"])
                    sb.add("scalar", lambda e: e.activation(ab, wt, AF.Abs),
                           reads=["twq"], writes=["tta"])
                    sb.add("scalar", lambda e: e.activation(sg, wt, AF.Sign),
                           reads=["twq"], writes=["tts"])
                    sb.add("vector", lambda e, ti=ti: e.tensor_scalar(
                        ab, ab, thr[:, ti:ti + 1], None, ALU.is_gt),
                        reads=["tta", "tth"], writes=["tta"])
                    sb.add("vector", lambda e: e.tensor_tensor(
                        tq, ab, sg, ALU.mult), reads=["tta", "tts"], writes=["ttr"])
                    sb.dma("sync", o[i][:, cs], tq, reads=["ttr"], writes=[okey])


def _wcols(w, j0, j1):
    if isinstance(w, tuple):
        _, v3, row0, rows = w
        out = []
        j = j0
        while j < j1:
            r = j // 256
            hi = min(j1, (r + 1) * 256)
            out.append((j - j0, v3[r, row0:row0 + rows, j - r * 256:hi - r * 256]))
            j = hi
        return out
    return [(0, w[:, j0:j1])]


def _wshape1(w):
    if isinstance(w, tuple):
        return 8 * 256
    return w.shape[1]


def ph_qkv(ctx, layer, x_in, wq, wk, wv, cos_in, sin_in,
           qT_o, kT_o, v_o, in_keys, out_keys):
    l0 = (layer == 0)
    KD = _wshape1(wk) if not isinstance(wk, tuple) else kT_o.shape[0]
    sb = ctx.sb
    with ctx.phase(f"qkv{layer}"):
        ident = ctx.const("identb", np.eye(128, dtype=BF))
        psgn = ctx.const("psgn", host_psgn())
        xt = ctx.sbuf("kx", [128, 2048])
        xqb = ctx.sbuf("kxq", [128, 2048], BF16)
        xqT = ctx.sbuf("kxqT", [128, NIC * TOK], BF16)
        identt = ctx.sbuf("kid", [128, 128], BF16)
        psgnt = ctx.sbuf("kpg", [128, 128], BF16)
        cost = ctx.sbuf("kcos", [128, TOK])
        sint = ctx.sbuf("ksin", [128, TOK])
        wslab = [ctx.sbuf(f"kws{i}", [128, NIC * 128], BF16) for i in range(2)]
        wvfull = ctx.sbuf("kwv", [128, NIC * min(KD, 2048)], BF16)
        qs = ctx.sbuf("kqs", [128, 512])
        qb = ctx.sbuf("kqb", [128, 512], BF16)
        ra = ctx.sbuf("kra", [128, 512])
        rb_ = ctx.sbuf("krb", [128, 512])
        rout = ctx.sbuf("kro", [128, 512])
        routb = ctx.sbuf("krob", [128, 512], BF16)
        vout = ctx.sbuf("kvo", [128, 512])
        tmp = alloc_quant_tmp(ctx) if l0 else None
        pp = ctx.psum("kpp")
        pr = ctx.psum("kpr")
        pt = ctx.psum("kpt", BF16)

        sb.dma("sync", identt, ident, writes=["kid"])
        sb.dma("sync", psgnt, psgn, writes=["kpg"])
        sb.dma("sync", cost, cos_in, writes=["kcos"])
        sb.dma("sync", sint, sin_in, writes=["ksin"])

        xv = x_in.rearrange("(n p) d -> n p d", p=128)
        xqTv = xqT.rearrange("p (a t) -> p a t", t=TOK)
        for i in range(NT):
            sb.dma("sync", xt, xv[i], reads=[in_keys.get("x", "_")], writes=["kx"])
            if l0:
                emit_fp4_quant(ctx, xt, xqb, tmp, "kx", "kxq")
            else:
                sb.add("vector", lambda e: e.tensor_copy(xqb, xt),
                       reads=["kx"], writes=["kxq"])
            for ic in range(NIC):
                sb.add("tensor", lambda e, ic=ic: e.transpose(
                    pt[:, 0:128], xqb[:, ic * 128:(ic + 1) * 128], identt),
                    reads=["kxq", "kid"], writes=["kpt"])
                sb.add("scalar", lambda e, ic=ic, i=i: e.copy(
                    xqTv[:, ic, i * 128:(i + 1) * 128], pt[:, 0:128]),
                    reads=["kpt"], writes=["kxT"])

        for (w, o_t, od, wkey, okey, is_q) in (
                (wq, qT_o, D, in_keys.get("wq", "_"), out_keys["q"], True),
                (wk, kT_o, KD, in_keys.get("wk", "_"), out_keys["k"], False)):
            ov = o_t.rearrange("(a p) t -> p a t", p=128)
            for dc in range(od // 128):
                buf = wslab[dc % 2]
                kwsl = f"kwsl{dc % 2}"
                for (doff, src) in _wcols(w, dc * 128, (dc + 1) * 128):
                    wsub = src.shape[1]
                    sb.dma("gpsimd",
                           buf.rearrange("p (a o) -> p a o", o=128)
                           [:, :, doff:doff + wsub],
                           src.rearrange("(a p) o -> p a o", p=128),
                           reads=[wkey], writes=[kwsl])
                for tb in range(2):
                    for ic in range(NIC):
                        sb.add("tensor", lambda e, ic=ic, tb=tb, buf=buf: e.matmul(
                            pp[:, 0:512], buf[:, ic * 128:(ic + 1) * 128],
                            xqTv[:, ic, tb * 512:(tb + 1) * 512],
                            start=(ic == 0), stop=(ic == NIC - 1)),
                            reads=[kwsl, "kxT"], writes=["kpp"])
                    sb.add("vector", lambda e: e.tensor_copy(qs, pp[:, 0:512]),
                           reads=["kpp"], writes=["kqs"])
                    sb.add("scalar", lambda e: e.copy(qb, qs),
                           reads=["kqs"], writes=["kqb"])
                    sb.add("tensor", lambda e: e.matmul(
                        pr[:, 0:512], psgnt, qb, start=True, stop=True),
                        reads=["kqb", "kpg"], writes=["kpr"])
                    sb.add("vector", lambda e, tb=tb: e.tensor_tensor(
                        ra, qs, cost[:, tb * 512:(tb + 1) * 512], ALU.mult),
                        reads=["kqs", "kcos"], writes=["kra"])
                    sb.add("vector", lambda e, tb=tb: e.tensor_tensor(
                        rb_, pr[:, 0:512], sint[:, tb * 512:(tb + 1) * 512],
                        ALU.mult), reads=["kpr", "ksin"], writes=["krb"])
                    if is_q:
                        sb.add("vector", lambda e: e.tensor_tensor(
                            routb, ra, rb_, ALU.add),
                            reads=["kra", "krb"], writes=["kro"])
                        sb.dma("sync", ov[:, dc, tb * 512:(tb + 1) * 512], routb,
                               reads=["kro"], writes=[okey])
                    else:
                        sb.add("vector", lambda e: e.tensor_tensor(
                            rout, ra, rb_, ALU.add),
                            reads=["kra", "krb"], writes=["kro"])
                        sb.dma("sync", ov[:, dc, tb * 512:(tb + 1) * 512], rout,
                               reads=["kro"], writes=[okey])

        wvfv = wvfull.rearrange("p (a o) -> p a o", o=min(KD, 2048))
        for (doff, src) in _wcols(wv, 0, KD):
            wsub = src.shape[1]
            sb.dma("gpsimd", wvfv[:, :, doff:doff + wsub],
                   src.rearrange("(a p) o -> p a o", p=128),
                   reads=[in_keys.get("wv", "_")], writes=["kwvf"])
        vv = v_o.rearrange("(n p) d -> n p d", p=128)
        nvb = max(1, KD // 512)
        vw = min(512, KD)
        for i in range(NT):
            for ob in range(nvb):
                for ic in range(NIC):
                    sb.add("tensor", lambda e, ic=ic, i=i, ob=ob: e.matmul(
                        pp[:, 0:vw], xqTv[:, ic, i * 128:(i + 1) * 128],
                        wvfv[:, ic, ob * 512:ob * 512 + vw],
                        start=(ic == 0), stop=(ic == NIC - 1)),
                        reads=["kwvf", "kxT"], writes=["kpp"])
                sb.add("scalar", lambda e: e.copy(vout[:, 0:vw], pp[:, 0:vw]),
                       reads=["kpp"], writes=["kvo"])
                sb.dma("sync", vv[i][:, ob * 512:ob * 512 + vw], vout[:, 0:vw],
                       reads=["kvo"], writes=[out_keys["v"]])


def ph_attn(ctx, layer, qT_in, kTf, vf, masks, x_in, wo, x1_o, in_keys, okey):
    l0 = (layer == 0)
    KD = kTf.shape[0]
    VC = 130
    sb = ctx.sb
    kq = in_keys.get("q", "_")
    kk = in_keys.get("k", "_")
    kv_ = in_keys.get("v", "_")
    kwo = in_keys.get("wo", "_")
    kxk = in_keys.get("x", "_")
    with ctx.phase(f"attn{layer}"):
        ident = ctx.const("identb2", np.eye(128, dtype=BF))
        qTt = ctx.sbuf("aqT", [128, 16 * GSZ], BF16)
        maskt = ctx.sbuf("amask", [128, MASK_W * GSZ], BF16)
        kthp = ctx.sbuf("akthp", [128, S], BF16)
        vhp = ctx.sbuf("avhp", [128, 32 * VC], BF16)
        pT = ctx.sbuf("apT", [128, GSZ], BF16)
        aos = ctx.sbuf("aaos", [128, 4 * 2048])
        aoq = ctx.sbuf("aaoq", [128, 2048], BF16)
        oqT = ctx.sbuf("aoqT", [128, 16 * GSZ], BF16)
        wslb = [ctx.sbuf(f"awsl{i}", [128, NIC * 512], BF16) for i in range(2)]
        rden = ctx.sbuf("arden", [128, 1])
        identt = ctx.sbuf("aid", [128, 128], BF16)
        lnscr = ctx.sbuf("alnscr", [128, 2048])
        stats = ctx.sbuf("astats", [128, 8])
        tmp = alloc_quant_tmp(ctx) if l0 else None
        ps_s = [ctx.psum("aps_s0"), ctx.psum("aps_s1")]
        ps_o = [ctx.psum(f"aps_o{i}") for i in range(4)]
        pp = ctx.psum("app")
        pt = ctx.psum("apt", BF16)

        sb.dma("sync", identt, ident, writes=["aid"])
        vhpv = vhp.rearrange("p (a c) -> p a c", c=VC)
        sb.add("vector", lambda e: e.memset(vhpv[:, :, 64:65], 1.0),
               writes=["avhp"])
        sb.add("vector", lambda e: e.memset(vhpv[:, :, 129:130], 1.0),
               writes=["avhp"])

        qTv = qT_in.rearrange("(a p) t -> p a t", p=128)
        x1v = x1_o.rearrange("(n p) d -> n p d", p=128)
        xv = x_in.rearrange("(n p) d -> n p d", p=128)
        qTtv = qTt.rearrange("p (a t) -> p a t", t=GSZ)
        oqTv = oqT.rearrange("p (a t) -> p a t", t=GSZ)
        wslv = [w.rearrange("p (a o) -> p a o", o=512) for w in wslb]

        shap, shkey = ctx.fconst(EXP_SHIFT)
        scap, sckey = ctx.fconst(SCALE_ATT)
        for slot in range(2):
            ktn = KT_PAD[slot]
            sb.dma("sync", qTtv, qTv[:, :, slot * GSZ:(slot + 1) * GSZ],
                   reads=[kq], writes=["aqT"])
            sb.dma("sync", maskt.rearrange("p (m q) -> p m q", q=GSZ),
                   masks[slot * MASK_W:(slot + 1) * MASK_W]
                   .rearrange("m p q -> p m q"),
                   reads=[in_keys.get("masks", "_")], writes=["amask"])
            for hp in range(16 if l0 else 8):
                if l0:
                    sb.dma("gpsimd", kthp[:, 0:ktn * 128],
                           kTf[hp * 128:(hp + 1) * 128, 0:ktn * 128],
                           reads=[kk], writes=["akthp"])
                    for half in range(2):
                        sb.dma("gpsimd",
                               vhpv[:, 0:ktn, 65 * half:65 * half + 64],
                               vf[0:ktn * 128,
                                  hp * 128 + 64 * half:hp * 128 + 64 * half + 64]
                               .rearrange("(t p) c -> p t c", p=128),
                               reads=[kv_], writes=["avhp"])
                else:
                    for half in range(2):
                        sb.dma("gpsimd", kthp[64 * half:64 * half + 64, 0:ktn * 128],
                               kTf[hp * 64:(hp + 1) * 64, 0:ktn * 128],
                               reads=[kk], writes=["akthp"])
                    sb.dma("gpsimd", vhpv[:, 0:ktn, 0:64],
                           vf[0:ktn * 128, hp * 64:(hp + 1) * 64]
                           .rearrange("(t p) c -> p t c", p=128),
                           reads=[kv_], writes=["avhp"])
                for h2 in range(2 if l0 else 4):
                    qh = hp * (2 if l0 else 4) + h2
                    krow = 64 * (qh % 2)
                    vcol = (65 * (qh % 2)) if l0 else 0
                    for kt in range(ktn):
                        pss = ps_s[kt % 2]
                        kss = f"apss{kt % 2}"
                        sb.add("tensor", lambda e, kt=kt, qh=qh, krow=krow,
                               pss=pss: e.matmul(
                            pss[:, 0:GSZ],
                            kthp[krow:krow + 64, kt * 128:(kt + 1) * 128],
                            qTtv[64 * (qh % 2):64 * (qh % 2) + 64, qh // 2, :],
                            start=True, stop=True),
                            reads=["akthp", "aqT"], writes=[kss])
                        sb.add("scalar", lambda e, pss=pss, shap=shap, scap=scap:
                               e.activation(pT, pss[:, 0:GSZ], AF.Exp,
                                            bias=shap, scale=scap),
                               reads=[kss, shkey, sckey], writes=["apT"])
                        wi = kt - MASK_START[slot]
                        if 0 <= wi < MASK_W:
                            sb.add("vector", lambda e, wi=wi: e.tensor_tensor(
                                pT, pT, maskt[:, wi * GSZ:(wi + 1) * GSZ],
                                ALU.mult),
                                reads=["apT", "amask"], writes=["apT"])
                        for s4 in range(4):
                            sb.add("tensor", lambda e, s4=s4, kt=kt, vcol=vcol,
                                   ktn=ktn: e.matmul(
                                ps_o[s4][:, 0:65],
                                pT[:, s4 * 128:(s4 + 1) * 128],
                                vhpv[:, kt, vcol:vcol + 65],
                                start=(kt == 0), stop=(kt == ktn - 1)),
                                reads=["apT", "avhp"], writes=[f"apso{s4}"])
                    for s4 in range(4):
                        sb.add("vector", lambda e, s4=s4: e.reciprocal(
                            rden, ps_o[s4][:, 64:65]),
                            reads=[f"apso{s4}"], writes=["arden"])
                        sb.add("vector", lambda e, s4=s4, qh=qh: e.tensor_scalar(
                            aos[:, s4 * 2048 + qh * 64: s4 * 2048 + qh * 64 + 64],
                            ps_o[s4][:, 0:64], rden, None, ALU.mult),
                            reads=[f"apso{s4}", "arden"], writes=["aaos"])

            for s4 in range(4):
                if l0:
                    emit_fp4_quant(ctx, aos[:, s4 * 2048:(s4 + 1) * 2048], aoq,
                                   tmp, "aaos", "aaoq")
                else:
                    sb.add("vector", lambda e, s4=s4: e.tensor_copy(
                        aoq, aos[:, s4 * 2048:(s4 + 1) * 2048]),
                        reads=["aaos"], writes=["aaoq"])
                for ic in range(NIC):
                    sb.add("tensor", lambda e, ic=ic: e.transpose(
                        pt[:, 0:128], aoq[:, ic * 128:(ic + 1) * 128], identt),
                        reads=["aaoq", "aid"], writes=["apt"])
                    sb.add("scalar", lambda e, ic=ic, s4=s4: e.copy(
                        oqTv[:, ic, s4 * 128:(s4 + 1) * 128], pt[:, 0:128]),
                        reads=["apt"], writes=["aoqT"])
            for tt in range(4):
                sb.dma("sync", aos[:, tt * 2048:(tt + 1) * 2048],
                       xv[slot * 4 + tt], reads=[kxk], writes=["aaos"])
            for ob in range(4):
                buf = wslv[ob % 2]
                kwsl = f"awosl{ob % 2}"
                for (doff, src) in _wcols(wo, ob * 512, (ob + 1) * 512):
                    wsub = src.shape[1]
                    sb.dma("gpsimd", buf[:, :, doff:doff + wsub],
                           src.rearrange("(a p) o -> p a o", p=128),
                           reads=[kwo], writes=[kwsl])
                for tt in range(4):
                    for ic in range(NIC):
                        sb.add("tensor", lambda e, ic=ic, tt=tt, buf=buf: e.matmul(
                            pp[:, 0:512], oqTv[:, ic, tt * 128:(tt + 1) * 128],
                            buf[:, ic, :], start=(ic == 0), stop=(ic == NIC - 1)),
                            reads=["aoqT", kwsl], writes=["app"])
                    sb.add("vector", lambda e, tt=tt, ob=ob: e.tensor_tensor(
                        aos[:, tt * 2048 + ob * 512: tt * 2048 + (ob + 1) * 512],
                        aos[:, tt * 2048 + ob * 512: tt * 2048 + (ob + 1) * 512],
                        pp[:, 0:512], ALU.add),
                        reads=["app", "aaos"], writes=["aaos"])
            for tt in range(4):
                xr = aos[:, tt * 2048:(tt + 1) * 2048]
                emit_ln(ctx, xr, lnscr, stats, "aaos")
                sb.dma("sync", x1v[slot * 4 + tt], xr, reads=["aaos"],
                       writes=[okey])


def ph_ffn(ctx, layer, x_in, wg, wu, wd, vec1, vec2, xo, in_keys, okey):
    l0 = (layer == 0)
    MID = INTER0 if l0 else DFF1
    NMC = MID // 128
    PASS_MC = 22 if l0 else 16
    NPASS = NMC // PASS_MC
    sb = ctx.sb
    kx = in_keys.get("x", "_")
    kwg = in_keys.get("wg", "_")
    kwu = in_keys.get("wu", "_")
    kwd = in_keys.get("wd", "_")
    kvec = in_keys.get("vec", "_")
    with ctx.phase(f"ffn{layer}"):
        ident = ctx.const("identb3", np.eye(128, dtype=BF))
        ones1 = ctx.const("ones1f", np.ones((1, 128), dtype=np.float32))
        xT = ctx.sbuf("fxT", [128, NIC * TOK], BF16)
        x2pre = ctx.sbuf("fx2", [128, NT * 2048])
        hfull = ctx.sbuf("fh", [128, PASS_MC * TOK], BF16)
        gsl = [ctx.sbuf(f"fg{i}", [128, NIC * 128], BF16) for i in range(2)]
        usl = [ctx.sbuf(f"fu{i}", [128, NIC * 128], BF16) for i in range(2)]
        dslb = [ctx.sbuf(f"fd{i}", [128, 512], BF16) for i in range(2)]
        bcast = ctx.sbuf("fbc", [128, 2048])
        rowt = ctx.sbuf("frow", [1, 2048])
        hg = ctx.sbuf("fhg", [128, 512])
        hu = ctx.sbuf("fhu", [128, 512])
        xbt = ctx.sbuf("fxb", [128, 2048], BF16)
        xtile = ctx.sbuf("fxt", [128, 2048])
        identt = ctx.sbuf("fid", [128, 128], BF16)
        onest = ctx.sbuf("fon", [1, 128])
        gvec = ctx.sbuf("fgv", [128, 2 * NMC])
        lnscr = ctx.sbuf("flnscr", [128, 2048])
        stats = ctx.sbuf("fstats", [128, 8])
        ps_g = ctx.psum("fps_g")
        ps_u = ctx.psum("fps_u")
        pt = ctx.psum("fpt", BF16)
        ps_t = [ctx.psum(f"fps_t{i}") for i in range(4)]

        sb.dma("sync", identt, ident, writes=["fid"])
        sb.dma("sync", onest, ones1, writes=["fon"])
        if l0:
            sb.dma("sync", gvec[:, 0:NMC], vec1[0], reads=[kvec], writes=["fgv"])
            sb.dma("sync", gvec[:, NMC:2 * NMC], vec1[1], reads=[kvec],
                   writes=["fgv"])
        else:
            sb.dma("sync", gvec[:, 0:NMC], vec1, reads=[kvec], writes=["fgv"])
        sb.dma("sync", rowt, vec2, reads=[kvec], writes=["frow"])

        for j in range(4):
            sb.add("tensor", lambda e, j=j: e.matmul(
                ps_g[:, 0:512], onest, rowt[:, j * 512:(j + 1) * 512],
                start=True, stop=True), reads=["frow", "fon"], writes=["fpsg"])
            sb.add("vector", lambda e, j=j: e.tensor_copy(
                bcast[:, j * 512:(j + 1) * 512], ps_g[:, 0:512]),
                reads=["fpsg"], writes=["fbc"])

        xv = x_in.rearrange("(n p) d -> n p d", p=128)
        xov = xo.rearrange("(n p) d -> n p d", p=128)
        xTv = xT.rearrange("p (a t) -> p a t", t=TOK)
        hv = hfull.rearrange("p (a t) -> p a t", t=TOK)
        x2v = x2pre.rearrange("p (a d) -> p a d", d=2048)

        for i in range(NT):
            sb.dma("sync", xtile, xv[i], reads=[kx], writes=["fxt"])
            if l0:
                sb.add("vector", lambda e, i=i: e.tensor_copy(x2v[:, i, :], xtile),
                       reads=["fxt"], writes=["fx2"])
            else:
                sb.add("vector", lambda e, i=i: e.tensor_tensor(
                    x2v[:, i, :], xtile, bcast, ALU.add),
                    reads=["fxt", "fbc"], writes=["fx2"])
            sb.add("vector", lambda e: e.tensor_copy(xbt, xtile),
                   reads=["fxt"], writes=["fxb"])
            for ic in range(NIC):
                sb.add("tensor", lambda e, ic=ic: e.transpose(
                    pt[:, 0:128], xbt[:, ic * 128:(ic + 1) * 128], identt),
                    reads=["fxb", "fid"], writes=["fpt"])
                sb.add("scalar", lambda e, ic=ic, i=i: e.copy(
                    xTv[:, ic, i * 128:(i + 1) * 128], pt[:, 0:128]),
                    reads=["fpt"], writes=["fxT"])

        wgv = wg.rearrange("(a p) o -> p a o", p=128)
        if l0:
            wuv = wu.rearrange("(a p) o -> p a o", p=128)
        geng = "gpsimd" if l0 else "sync"

        for hp_ in range(NPASS):
            for mcl in range(PASS_MC):
                mc = hp_ * PASS_MC + mcl
                gb = gsl[mcl % 2]
                kg = f"fgsl{mcl % 2}"
                sb.dma(geng, gb.rearrange("p (a o) -> p a o", o=128),
                       wgv[:, :, mc * 128:(mc + 1) * 128],
                       reads=[kwg], writes=[kg])
                if l0:
                    ub = usl[mcl % 2]
                    ku = f"fusl{mcl % 2}"
                    sb.dma("gpsimd", ub.rearrange("p (a o) -> p a o", o=128),
                           wuv[:, :, mc * 128:(mc + 1) * 128],
                           reads=[kwu], writes=[ku])
                for tb in range(2):
                    for ic in range(NIC):
                        sb.add("tensor", lambda e, ic=ic, tb=tb, gb=gb: e.matmul(
                            ps_g[:, 0:512], gb[:, ic * 128:(ic + 1) * 128],
                            xTv[:, ic, tb * 512:(tb + 1) * 512],
                            start=(ic == 0), stop=(ic == NIC - 1)),
                            reads=[kg, "fxT"], writes=["fpsg"])
                    if l0:
                        for ic in range(NIC):
                            sb.add("tensor", lambda e, ic=ic, tb=tb, ub=ub: e.matmul(
                                ps_u[:, 0:512], ub[:, ic * 128:(ic + 1) * 128],
                                xTv[:, ic, tb * 512:(tb + 1) * 512],
                                start=(ic == 0), stop=(ic == NIC - 1)),
                                reads=[ku, "fxT"], writes=["fpsu"])
                        sb.add("scalar", lambda e, mc=mc: e.activation(
                            hg, ps_g[:, 0:512], AF.Silu, scale=gvec[:, mc:mc + 1]),
                            reads=["fpsg", "fgv"], writes=["fhg"])
                        sb.add("vector", lambda e, mc=mc: e.tensor_scalar(
                            hu, ps_u[:, 0:512], gvec[:, NMC + mc:NMC + mc + 1],
                            None, ALU.mult), reads=["fpsu", "fgv"], writes=["fhu"])
                        sb.add("vector", lambda e, mcl=mcl, tb=tb: e.tensor_tensor(
                            hv[:, mcl, tb * 512:(tb + 1) * 512], hg, hu, ALU.mult),
                            reads=["fhg", "fhu"], writes=["fh"])
                    else:
                        sb.add("scalar", lambda e, mc=mc, mcl=mcl, tb=tb:
                               e.activation(
                            hv[:, mcl, tb * 512:(tb + 1) * 512], ps_g[:, 0:512],
                            AF.Gelu, bias=gvec[:, mc:mc + 1]),
                            reads=["fpsg", "fgv"], writes=["fh"])
            for ttg in range(2):
                for ob in range(4):
                    for mcl in range(PASS_MC):
                        mc = hp_ * PASS_MC + mcl
                        db = dslb[mcl % 2]
                        kd = f"fdsl{mcl % 2}"
                        sb.dma(geng, db,
                               wd[mc * 128:(mc + 1) * 128,
                                  ob * 512:(ob + 1) * 512],
                               reads=[kwd], writes=[kd])
                        for tt in range(4):
                            ti = ttg * 4 + tt
                            sb.add("tensor", lambda e, mcl=mcl, ti=ti, tt=tt,
                                   db=db: e.matmul(
                                ps_t[tt][:, 0:512],
                                hv[:, mcl, ti * 128:(ti + 1) * 128],
                                db, start=(mcl == 0), stop=(mcl == PASS_MC - 1)),
                                reads=["fh", kd], writes=[f"fpst{tt}"])
                    for tt in range(4):
                        ti = ttg * 4 + tt
                        if l0:
                            sb.add("vector", lambda e, tt=tt, ob=ob: e.tensor_tensor(
                                hg, ps_t[tt][:, 0:512],
                                bcast[:, ob * 512:(ob + 1) * 512],
                                ALU.mult), reads=[f"fpst{tt}", "fbc"],
                                writes=["fhg"])
                            sb.add("vector", lambda e, ti=ti, ob=ob: e.tensor_tensor(
                                x2v[:, ti, ob * 512:(ob + 1) * 512],
                                x2v[:, ti, ob * 512:(ob + 1) * 512], hg, ALU.add),
                                reads=["fhg", "fx2"], writes=["fx2"])
                        else:
                            sb.add("vector", lambda e, ti=ti, tt=tt, ob=ob:
                                   e.tensor_tensor(
                                x2v[:, ti, ob * 512:(ob + 1) * 512],
                                x2v[:, ti, ob * 512:(ob + 1) * 512],
                                ps_t[tt][:, 0:512], ALU.add),
                                reads=[f"fpst{tt}", "fx2"], writes=["fx2"])

        for i in range(NT):
            xr = x2v[:, i, :]
            emit_ln(ctx, xr, lnscr, stats, "fx2")
            sb.dma("sync", xov[i], xr, reads=["fx2"], writes=[okey])



def _prefetch_consts(ctx):
    ctx.fconst(1e-5)
    ctx.fconst(EXP_SHIFT)
    ctx.fconst(SCALE_ATT)


def build_launch_a():
    ctx = Ctx("la")
    x_in = ctx.inp("x_own", [TOK, D])
    wq0 = ctx.inp("wq0", [D, D])
    wk0 = ctx.inp("wk0", [D, D])
    wv0 = ctx.inp("wv0", [D, D])
    w1ts = ctx.inp("w1ts", [256, DFF1])
    w2ts = ctx.inp("w2ts", [1024, D])
    cos_in = ctx.inp("cos_own", [128, TOK])
    sin_in = ctx.inp("sin_own", [128, TOK])
    psums_o = ctx.out("psums", [1, 16])
    qT_o = ctx.out("qT", [D, TOK], BF16)
    kT_o = ctx.out("kT", [D, TOK])
    v_o = ctx.out("v", [TOK, D])
    wqqt = ctx.dram("wqqt", [D, D])
    wkqt = ctx.dram("wkqt", [D, D])
    wvqt = ctx.dram("wvqt", [D, D])
    _prefetch_consts(ctx)
    ph_wprep1(ctx, w1ts, w2ts, psums_o, "pso")
    ph_wquant(ctx, [wq0, wk0, wv0], [wqqt, wkqt, wvqt], 16,
              ["wqk", "wkk", "wvk"])
    ph_qkv(ctx, 0, x_in, wqqt, wkqt, wvqt, cos_in, sin_in,
           qT_o, kT_o, v_o,
           {"wq": "wqk", "wk": "wkk", "wv": "wvk"},
           {"q": "qo", "k": "ko", "v": "vo"})
    run_block(ctx.nc, ctx.sb, "la")
    return ctx.nc


def build_launch_b():
    ctx = Ctx("lb")
    x_in = ctx.inp("x_own", [TOK, D])
    qT0 = ctx.inp("qT0", [D, TOK], BF16)
    kTf0 = ctx.inp("kTf0", [D, S])
    vf0 = ctx.inp("vf0", [S, D])
    masks = ctx.inp("masks", [2 * MASK_W, 128, GSZ], BF16)
    wo0 = ctx.inp("wo0", [D, D])
    gateT = ctx.inp("gateT", [D, INTER0])
    upT = ctx.inp("upT", [D, INTER0])
    downT = ctx.inp("downT", [INTER0, D])
    gsb = ctx.inp("gsb", [128, INTER0 // 128])
    usb = ctx.inp("usb", [128, INTER0 // 128])
    dsr = ctx.inp("dsr", [1, D])
    partials_all = ctx.inp("partials_all", [8, 16])
    w1ts = ctx.inp("w1ts", [256, DFF1])
    w2ts = ctx.inp("w2ts", [1024, D])
    wq1 = ctx.inp("wq1", [D, D])
    wk1 = ctx.inp("wk1", [D, 512])
    wv1 = ctx.inp("wv1", [D, 512])
    cos_in = ctx.inp("cos_own", [128, TOK])
    sin_in = ctx.inp("sin_own", [128, TOK])
    x2_o = ctx.out("x2", [TOK, D])
    qT1_o = ctx.out("qT1", [D, TOK], BF16)
    kT1_o = ctx.out("kT1", [512, TOK])
    v1_o = ctx.out("v1", [TOK, 512])
    w1q_o = ctx.out("w1q_s", [256, DFF1], BF16)
    w2q_o = ctx.out("w2q_s", [1024, D], BF16)
    woqt = ctx.dram("woqt", [D, D])
    x1b = ctx.dram("x1b", [TOK, D])
    _prefetch_consts(ctx)
    ph_wquant(ctx, [wo0], [woqt], 16, ["wok"])
    ph_attn(ctx, 0, qT0, kTf0, vf0, masks, x_in, woqt, x1b,
            {"q": "_q", "k": "_k", "v": "_v", "wo": "wok", "x": "_x",
             "masks": "_m"}, "x1k")
    ph_ffn(ctx, 0, x1b, gateT, upT, downT, (gsb, usb), dsr, x2_o,
           {"x": "x1k"}, "x2k")
    ph_tern(ctx, partials_all, w1ts, w2ts, w1q_o, w2q_o, "_pa",
            ["w1k", "w2k"])
    ph_qkv(ctx, 1, x2_o, wq1, wk1, wv1, cos_in, sin_in,
           qT1_o, kT1_o, v1_o, {"x": "x2k"},
           {"q": "q1o", "k": "k1o", "v": "v1o"})
    run_block(ctx.nc, ctx.sb, "lb")
    return ctx.nc


def build_launch_c():
    ctx = Ctx("lc")
    x2_in = ctx.inp("x2_own", [TOK, D])
    qT1 = ctx.inp("qT1", [D, TOK], BF16)
    kTf1 = ctx.inp("kTf1", [512, S])
    vf1 = ctx.inp("vf1", [S, 512])
    masks = ctx.inp("masks", [2 * MASK_W, 128, GSZ], BF16)
    wo1 = ctx.inp("wo1", [D, D])
    w1qT = ctx.inp("w1qT", [D, DFF1], BF16)
    w2qT = ctx.inp("w2qT", [DFF1, D], BF16)
    b1b = ctx.inp("b1b", [128, DFF1 // 128])
    b2r = ctx.inp("b2r", [1, D])
    xo = ctx.out("xo", [TOK, D])
    x3b = ctx.dram("x3b", [TOK, D])
    _prefetch_consts(ctx)
    ph_attn(ctx, 1, qT1, kTf1, vf1, masks, x2_in, wo1, x3b,
            {}, "x3k")
    ph_ffn(ctx, 1, x3b, w1qT, None, w2qT, b1b, b2r, xo,
           {"x": "x3k"}, "xok")
    run_block(ctx.nc, ctx.sb, "lc")
    return ctx.nc



_NC_CACHE = {}


def _run(name, builder, in_maps):
    if name not in _NC_CACHE:
        _NC_CACHE[name] = builder()
    res = run_bass_kernel_spmd(_NC_CACHE[name], in_maps, core_ids=list(range(NCORE)))
    return res.results


def _order_kv(rq, kd, kname, vname):
    f32 = np.float32
    kTfs = [np.zeros((kd, S), dtype=f32) for _ in range(B)]
    vfs = [np.zeros((S, kd), dtype=f32) for _ in range(B)]
    for c in range(NCORE):
        b = c // 4
        (r0, p0), (r1, p1) = core_token_rows(c)
        kT = np.asarray(rq[c][kname])
        v = np.asarray(rq[c][vname])
        for slot, p in ((0, p0), (1, p1)):
            kTfs[b][:, p * GSZ:(p + 1) * GSZ] = kT[:, slot * GSZ:(slot + 1) * GSZ]
            vfs[b][p * GSZ:(p + 1) * GSZ] = v[slot * GSZ:(slot + 1) * GSZ]
    return kTfs, vfs


def kernel_3launch(L0_attn_hidden_states, L0_attn_Wq, L0_attn_Wk, L0_attn_Wv, L0_attn_Wo,
           L0_ffn_gate_w, L0_ffn_up_w, L0_ffn_down_w, L0_ffn_gate_s, L0_ffn_up_s,
           L0_ffn_down_s, L1_attn_wq, L1_attn_wk, L1_attn_wv, L1_attn_wo,
           L1_ffn_w1, L1_ffn_b1, L1_ffn_w2, L1_ffn_b2):
    f32 = np.float32
    x = np.ascontiguousarray(np.asarray(L0_attn_hidden_states, f32).reshape(B * S, D))
    w1T = np.ascontiguousarray(np.asarray(L1_ffn_w1, f32).T)
    w2T = np.ascontiguousarray(np.asarray(L1_ffn_w2, f32).T)
    gateT = np.ascontiguousarray(np.asarray(L0_ffn_gate_w, f32).T)
    upT = np.ascontiguousarray(np.asarray(L0_ffn_up_w, f32).T)
    downT = np.ascontiguousarray(np.asarray(L0_ffn_down_w, f32).T)
    wq0 = np.ascontiguousarray(np.asarray(L0_attn_Wq, f32))
    wk0 = np.ascontiguousarray(np.asarray(L0_attn_Wk, f32))
    wv0 = np.ascontiguousarray(np.asarray(L0_attn_Wv, f32))
    wo0 = np.ascontiguousarray(np.asarray(L0_attn_Wo, f32))
    cos2, sin2 = host_rope_tables()

    def own_rows(c, arr2d):
        (r0, _), (r1, _) = core_token_rows(c)
        return np.ascontiguousarray(
            np.concatenate([arr2d[r0:r0 + GSZ], arr2d[r1:r1 + GSZ]], 0))

    def rope_own(c):
        (_, p0), (_, p1) = core_token_rows(c)
        cosn = np.concatenate([cos2[:, p0 * GSZ:(p0 + 1) * GSZ],
                               cos2[:, p1 * GSZ:(p1 + 1) * GSZ]], 1)
        sinn = np.concatenate([sin2[:, p0 * GSZ:(p0 + 1) * GSZ],
                               sin2[:, p1 * GSZ:(p1 + 1) * GSZ]], 1)
        return np.ascontiguousarray(cosn), np.ascontiguousarray(sinn)

    def scatter_rows(res, key):
        full = np.zeros((B * S, D), dtype=f32)
        for c in range(NCORE):
            (r0, _), (r1, _) = core_token_rows(c)
            full[r0:r0 + GSZ] = np.asarray(res[c][key])[:GSZ]
            full[r1:r1 + GSZ] = np.asarray(res[c][key])[GSZ:]
        return full

    ima = []
    for c in range(NCORE):
        cosn, sinn = rope_own(c)
        ima.append({"x_own": own_rows(c, x), "wq0": wq0, "wk0": wk0, "wv0": wv0,
                    "w1ts": np.ascontiguousarray(w1T[c * 256:(c + 1) * 256]),
                    "w2ts": np.ascontiguousarray(w2T[c * 1024:(c + 1) * 1024]),
                    "cos_own": cosn, "sin_own": sinn})
    ra = _run("la", build_launch_a, ima)
    partials_all = np.ascontiguousarray(
        np.concatenate([np.asarray(ra[c]["psums"]) for c in range(NCORE)], 0))
    kTfs0, vfs0 = _order_kv(ra, D, "kT", "v")

    imb = []
    for c in range(NCORE):
        b = c // 4
        cosn, sinn = rope_own(c)
        imb.append({"x_own": ima[c]["x_own"], "qT0": np.asarray(ra[c]["qT"]),
                    "kTf0": kTfs0[b], "vf0": vfs0[b], "masks": host_masks(c),
                    "wo0": wo0, "gateT": gateT, "upT": upT, "downT": downT,
                    "gsb": host_band_vec(L0_ffn_gate_s),
                    "usb": host_band_vec(L0_ffn_up_s),
                    "dsr": np.ascontiguousarray(np.asarray(L0_ffn_down_s, f32)[None, :]),
                    "partials_all": partials_all,
                    "w1ts": ima[c]["w1ts"], "w2ts": ima[c]["w2ts"],
                    "wq1": np.ascontiguousarray(np.asarray(L1_attn_wq, f32)),
                    "wk1": np.ascontiguousarray(np.asarray(L1_attn_wk, f32)),
                    "wv1": np.ascontiguousarray(np.asarray(L1_attn_wv, f32)),
                    "cos_own": cosn, "sin_own": sinn})
    rb = _run("lb", build_launch_b, imb)
    kTfs1, vfs1 = _order_kv(rb, 512, "kT1", "v1")
    w1qT = np.ascontiguousarray(np.concatenate(
        [np.asarray(rb[c]["w1q_s"]) for c in range(NCORE)], 0))
    w2qT = np.ascontiguousarray(np.concatenate(
        [np.asarray(rb[c]["w2q_s"]) for c in range(NCORE)], 0))

    imc = []
    for c in range(NCORE):
        b = c // 4
        imc.append({"x2_own": np.ascontiguousarray(np.asarray(rb[c]["x2"])),
                    "qT1": np.asarray(rb[c]["qT1"]),
                    "kTf1": kTfs1[b], "vf1": vfs1[b], "masks": host_masks(c),
                    "wo1": np.ascontiguousarray(np.asarray(L1_attn_wo, f32)),
                    "w1qT": w1qT, "w2qT": w2qT,
                    "b1b": host_band_vec(L1_ffn_b1),
                    "b2r": np.ascontiguousarray(np.asarray(L1_ffn_b2, f32)[None, :])})
    rc = _run("lc", build_launch_c, imc)
    x4 = scatter_rows(rc, "xo")
    return x4.reshape(B, S, D)



GRP8 = [[0, 1, 2, 3, 4, 5, 6, 7]]
GRP4 = [[0, 1, 2, 3], [4, 5, 6, 7]]
PAY2KV = 4096
PAYT = 2048
PAY3KV = 1024


def build_fused():
    ctx = Ctx("fz")
    sb = ctx.sb
    nc = ctx.nc
    x_in = ctx.inp("x_own", [TOK, D])
    wq_s = ctx.inp("wq_s", [256, D])
    wk_s = ctx.inp("wk_s", [256, D])
    wv_s = ctx.inp("wv_s", [256, D])
    wo_s = ctx.inp("wo_s", [256, D])
    w1ts = ctx.inp("w1ts", [256, DFF1])
    w2ts = ctx.inp("w2ts", [1024, D])
    gateT = ctx.inp("gateT", [D, INTER0])
    upT = ctx.inp("upT", [D, INTER0])
    downT = ctx.inp("downT", [INTER0, D])
    gsb = ctx.inp("gsb", [128, INTER0 // 128])
    usb = ctx.inp("usb", [128, INTER0 // 128])
    dsr = ctx.inp("dsr", [1, D])
    wq1 = ctx.inp("wq1", [D, D])
    wk1 = ctx.inp("wk1", [D, 512])
    wv1 = ctx.inp("wv1", [D, 512])
    wo1 = ctx.inp("wo1", [D, D])
    b1b = ctx.inp("b1b", [128, DFF1 // 128])
    b2r = ctx.inp("b2r", [1, D])
    masks = ctx.inp("masks", [2 * MASK_W, 128, GSZ], BF16)
    cos_in = ctx.inp("cos_own", [128, TOK])
    sin_in = ctx.inp("sin_own", [128, TOK])
    xo = ctx.out("xo", [TOK, D])

    pay1_in = ctx.dram("pay1_in", [PAY1_ROWS, PAY1W])
    pay1_out = ctx.dram("pay1_out", [8 * PAY1_ROWS, PAY1W])
    pay2_in = ctx.dram("pay2_in", [PAY2KV, PAY2W])
    pay2_out = ctx.dram("pay2_out", [4 * PAY2KV, PAY2W])
    payt_in = ctx.dram("payt_in", [PAYT, PAY2W])
    payt_out = ctx.dram("payt_out", [8 * PAYT, PAY2W])
    pay3_in = ctx.dram("pay3_in", [PAY3KV, PAY3W])
    pay3_out = ctx.dram("pay3_out", [4 * PAY3KV, PAY3W])
    qT0b = ctx.dram("qT0b", [D, TOK], BF16)
    qT1b = ctx.dram("qT1b", [D, TOK], BF16)
    kTf0 = ctx.dram("kTf0", [D, S])
    vf0 = ctx.dram("vf0", [S, D])
    kTf1 = ctx.dram("kTf1", [512, S])
    vf1 = ctx.dram("vf1", [S, 512])
    w1qTb = ctx.dram("w1qTb", [D, DFF1 // 2])
    w2qTb = ctx.dram("w2qTb", [DFF1, D // 2])
    x1b = ctx.dram("x1b", [TOK, D])
    x2b = ctx.dram("x2b", [TOK, D])
    x3b = ctx.dram("x3b", [TOK, D])

    _prefetch_consts(ctx)

    ph_wprep1(ctx, w1ts, w2ts, pay1_in[4 * D:4 * D + 1, :], "p1k")
    ph_wquant(ctx, [wq_s, wk_s, wv_s, wo_s],
              [pay1_in[0:D, :], pay1_in[D:2 * D, :], pay1_in[2 * D:3 * D, :],
               pay1_in[3 * D:4 * D, :]], 2, ["p1k", "p1k", "p1k", "p1k"])
    sb.cc(lambda e: e.collective_compute(
        "AllGather", ALU.bypass, GRP8,
        [pay1_in.opt()], [pay1_out.opt()]), reads=["p1k"], writes=["p1o"])

    p1v3 = pay1_out.rearrange("(r n) w -> r n w", n=PAY1_ROWS)
    wq_full = ("gath", p1v3, 0, D)
    wk_full = ("gath", p1v3, D, D)
    wv_full = ("gath", p1v3, 2 * D, D)
    wo_full = ("gath", p1v3, 3 * D, D)
    partials_all = p1v3[:, 4 * D, 0:16]

    w1q_view = payt_in[0:1024, :].bitcast(BF16).rearrange(
        "(a b) c -> a (b c)", b=4)
    w2q_view = payt_in[1024:2048, :].bitcast(BF16)
    ph_tern(ctx, partials_all, w1ts, w2ts, w1q_view, w2q_view, "p1o",
            ["ptk", "ptk"])
    sb.cc(lambda e: e.collective_compute(
        "AllGather", ALU.bypass, GRP8,
        [payt_in.opt()], [payt_out.opt()]), reads=["ptk"], writes=["pto"])
    with ctx.phase("fixw"):
        for r in range(8):
            sb.add("gpsimd", lambda e, r=r: e.dma_start(
                out=w1qTb[r * 256:(r + 1) * 256, :],
                in_=payt_out[r * PAYT: r * PAYT + 1024, :]
                .rearrange("(a b) c -> a (b c)", b=4)),
                reads=["pto"], writes=["w1f"], q="gd")
            sb.add("gpsimd", lambda e, r=r: e.dma_start(
                out=w2qTb[r * 1024:(r + 1) * 1024, :],
                in_=payt_out[r * PAYT + 1024: (r + 1) * PAYT, :]),
                reads=["pto"], writes=["w2f"], q="gd")

    ph_qkv(ctx, 0, x_in, wq_full, wk_full, wv_full, cos_in, sin_in,
           qT0b, pay2_in[0:2048, :],
           pay2_in[2048:4096, :].rearrange("(a b) c -> a (b c)", b=2),
           {"wq": "p1o", "wk": "p1o", "wv": "p1o"},
           {"q": "q0k", "k": "p2k", "v": "p2k"})
    sb.cc(lambda e: e.collective_compute(
        "AllGather", ALU.bypass, GRP4,
        [pay2_in.opt()], [pay2_out.opt()]), reads=["p2k"], writes=["p2o"])
    with ctx.phase("fix0"):
        for r4 in range(4):
            for slot in range(2):
                p = r4 if slot == 0 else 7 - r4
                sb.add("gpsimd", lambda e, r4=r4, slot=slot, p=p: e.dma_start(
                    out=kTf0[:, p * GSZ:(p + 1) * GSZ],
                    in_=pay2_out[r4 * PAY2KV: r4 * PAY2KV + 2048,
                                 slot * GSZ:(slot + 1) * GSZ]),
                    reads=["p2o"], writes=["ktf0"], q="gd")
                sb.add("gpsimd", lambda e, r4=r4, slot=slot, p=p: e.dma_start(
                    out=vf0[p * GSZ:(p + 1) * GSZ, :]
                    .rearrange("(a b) c -> a (b c)", b=2),
                    in_=pay2_out[r4 * PAY2KV + 2048 + slot * 1024:
                                 r4 * PAY2KV + 2048 + (slot + 1) * 1024, :]),
                    reads=["p2o"], writes=["vf0"], q="gd")

    ph_attn(ctx, 0, qT0b, kTf0, vf0, masks, x_in, wo_full, x1b,
            {"q": "q0k", "k": "ktf0", "v": "vf0", "wo": "p1o",
             "x": "_x", "masks": "_m"}, "x1k")
    ph_ffn(ctx, 0, x1b, gateT, upT, downT, (gsb, usb), dsr, x2b,
           {"x": "x1k"}, "x2k")
    ph_qkv(ctx, 1, x2b, wq1, wk1, wv1, cos_in, sin_in,
           qT1b, pay3_in[0:512, :],
           pay3_in[512:1024, :].rearrange("a (b c) -> (a b) c", b=2),
           {"x": "x2k"}, {"q": "q1k", "k": "p3k", "v": "p3k"})
    sb.cc(lambda e: e.collective_compute(
        "AllGather", ALU.bypass, GRP4,
        [pay3_in.opt()], [pay3_out.opt()]), reads=["p3k"], writes=["p3o"])
    with ctx.phase("fix1"):
        for r4 in range(4):
            for slot in range(2):
                p = r4 if slot == 0 else 7 - r4
                sb.add("gpsimd", lambda e, r4=r4, slot=slot, p=p: e.dma_start(
                    out=kTf1[:, p * GSZ:(p + 1) * GSZ],
                    in_=pay3_out[r4 * PAY3KV: r4 * PAY3KV + 512,
                                 slot * GSZ:(slot + 1) * GSZ]),
                    reads=["p3o"], writes=["ktf1"], q="gd")
                sb.add("gpsimd", lambda e, r4=r4, slot=slot, p=p: e.dma_start(
                    out=vf1[p * GSZ:(p + 1) * GSZ, :],
                    in_=pay3_out[r4 * PAY3KV + 512 + slot * 256:
                                 r4 * PAY3KV + 512 + (slot + 1) * 256, :]
                    .rearrange("a (b c) -> (a b) c", b=2)),
                    reads=["p3o"], writes=["vf1"], q="gd")

    ph_attn(ctx, 1, qT1b, kTf1, vf1, masks, x2b, wo1, x3b,
            {"q": "q1k", "k": "ktf1", "v": "vf1", "x": "x2k"}, "x3k")
    ph_ffn(ctx, 1, x3b, w1qTb.bitcast(BF16), None, w2qTb.bitcast(BF16),
           b1b, b2r, xo, {"x": "x3k", "wg": "w1f", "wd": "w2f"}, "xok")
    run_block(nc, sb, "fz")
    return nc


def kernel_fused_inputs(ins):
    f32 = np.float32
    x = np.ascontiguousarray(np.asarray(ins["L0_attn_hidden_states"], f32)
                             .reshape(B * S, D))
    w1T = np.ascontiguousarray(np.asarray(ins["L1_ffn_w1"], f32).T)
    w2T = np.ascontiguousarray(np.asarray(ins["L1_ffn_w2"], f32).T)
    cos2, sin2 = host_rope_tables()
    ims = []
    for c in range(NCORE):
        (r0, _), (r1, _) = core_token_rows(c)
        (_, p0), (_, p1) = core_token_rows(c)
        cosn = np.ascontiguousarray(np.concatenate(
            [cos2[:, p0 * GSZ:(p0 + 1) * GSZ], cos2[:, p1 * GSZ:(p1 + 1) * GSZ]], 1))
        sinn = np.ascontiguousarray(np.concatenate(
            [sin2[:, p0 * GSZ:(p0 + 1) * GSZ], sin2[:, p1 * GSZ:(p1 + 1) * GSZ]], 1))
        ims.append({
            "x_own": np.ascontiguousarray(
                np.concatenate([x[r0:r0 + GSZ], x[r1:r1 + GSZ]], 0)),
            "wq_s": np.ascontiguousarray(np.asarray(ins["L0_attn_Wq"], f32)[c * 256:(c + 1) * 256]),
            "wk_s": np.ascontiguousarray(np.asarray(ins["L0_attn_Wk"], f32)[c * 256:(c + 1) * 256]),
            "wv_s": np.ascontiguousarray(np.asarray(ins["L0_attn_Wv"], f32)[c * 256:(c + 1) * 256]),
            "wo_s": np.ascontiguousarray(np.asarray(ins["L0_attn_Wo"], f32)[c * 256:(c + 1) * 256]),
            "w1ts": np.ascontiguousarray(w1T[c * 256:(c + 1) * 256]),
            "w2ts": np.ascontiguousarray(w2T[c * 1024:(c + 1) * 1024]),
            "gateT": np.ascontiguousarray(np.asarray(ins["L0_ffn_gate_w"], f32).T),
            "upT": np.ascontiguousarray(np.asarray(ins["L0_ffn_up_w"], f32).T),
            "downT": np.ascontiguousarray(np.asarray(ins["L0_ffn_down_w"], f32).T),
            "gsb": host_band_vec(ins["L0_ffn_gate_s"]),
            "usb": host_band_vec(ins["L0_ffn_up_s"]),
            "dsr": np.ascontiguousarray(np.asarray(ins["L0_ffn_down_s"], f32)[None, :]),
            "wq1": np.ascontiguousarray(np.asarray(ins["L1_attn_wq"], f32)),
            "wk1": np.ascontiguousarray(np.asarray(ins["L1_attn_wk"], f32)),
            "wv1": np.ascontiguousarray(np.asarray(ins["L1_attn_wv"], f32)),
            "wo1": np.ascontiguousarray(np.asarray(ins["L1_attn_wo"], f32)),
            "b1b": host_band_vec(ins["L1_ffn_b1"]),
            "b2r": np.ascontiguousarray(np.asarray(ins["L1_ffn_b2"], f32)[None, :]),
            "masks": host_masks(c),
            "cos_own": cosn, "sin_own": sinn,
        })
    return ims


def kernel_fused(**ins):
    ims = kernel_fused_inputs(ins)
    if "fz" not in _NC_CACHE:
        _NC_CACHE["fz"] = build_fused()
    res = run_bass_kernel_spmd(_NC_CACHE["fz"], ims, core_ids=list(range(NCORE)))
    f32 = np.float32
    x4 = np.zeros((B * S, D), dtype=f32)
    for c in range(NCORE):
        (r0, _), (r1, _) = core_token_rows(c)
        x4[r0:r0 + GSZ] = np.asarray(res.results[c]["xo"])[:GSZ]
        x4[r1:r1 + GSZ] = np.asarray(res.results[c]["xo"])[GSZ:]
    return x4.reshape(B, S, D)


def kernel(**ins):
    return kernel_3launch(**ins)


# version 05
# speedup vs baseline: 1.0991x; 1.0991x over previous
import contextlib
import numpy as np
import ml_dtypes
import concourse.bass as bass
import concourse.mybir as mybir
from concourse.bass_utils import run_bass_kernel_spmd

F32 = mybir.dt.float32
BF16 = mybir.dt.bfloat16
I32 = mybir.dt.int32
AF = mybir.ActivationFunctionType
ALU = mybir.AluOpType
AX = mybir.AxisListType
BF = ml_dtypes.bfloat16

B = 2
S = 4096
D = 2048
NCORE = 8
TOK = 1024
NT = TOK // 128
GSZ = 512
NH = 32
HD = 64
INTER0 = 5632
DFF1 = 8192
KT_PAD = (16, 32)
MASK_W = 16
MASK_START = (0, 16)
EXP_SHIFT = -20.0
SCALE_ATT = 0.125
NIC = D // 128

PAY1W = 256
PAY1_ROWS = 4 * D + 1
PAY2W = 1024
PAY2_ROWS = 2048 + 2048 + 1024 + 1024
PAY3W = 1024
PAY3_ROWS = 512 + 512


def core_positions(c):
    cc = c % 4
    return (cc, 7 - cc)


def core_token_rows(c):
    b = c // 4
    p0, p1 = core_positions(c)
    return [(b * S + p0 * GSZ, p0), (b * S + p1 * GSZ, p1)]



_ENG_Q = {"tensor": ("t", 1), "vector": ("v", 1), "scalar": ("s", 1),
          "gpsimd": ("g", 1)}
_DMA_Q = {"sync": "sd", "gpsimd": "gd", "scalar": "sd2"}
_SELF_GATED = ("sd", "gd", "sd2", "cc")
_ALL_Q = ("t", "v", "s", "g", "sd", "gd", "sd2", "cc")


class SB:
    def __init__(self):
        self.items = []
        self.cum = {q: 0 for q in _ALL_Q}
        self.w = {}
        self.r = {}
        self.floor = {q: 0 for q in _ALL_Q}

    def _dep(self, waits, q, val):
        waits[q] = max(waits.get(q, 0), val)

    def _add(self, engine, q, fn, reads, writes, inc):
        waits = {}
        for q2, v2 in self.floor.items():
            if v2 > 0:
                waits[q2] = v2
        if q in _SELF_GATED and self.cum[q] > waits.get(q, 0):
            waits[q] = self.cum[q]
        for k in reads:
            if k in self.w:
                self._dep(waits, *self.w[k])
        for k in writes:
            if k in self.w:
                self._dep(waits, *self.w[k])
            for rq, rv in self.r.get(k, {}).items():
                self._dep(waits, rq, rv)
        after = self.cum[q] + inc
        self.cum[q] = after
        for k in reads:
            d = self.r.setdefault(k, {})
            d[q] = max(d.get(q, 0), after)
        for k in writes:
            self.w[k] = (q, after)
            self.r[k] = {}
        self.items.append((engine, fn, waits, q, inc))

    def add(self, engine, fn, reads=(), writes=(), q=None, inc=None):
        if q is None:
            q, qinc = _ENG_Q[engine]
        else:
            qinc = 16 if q in _SELF_GATED else 1
        self._add(engine, q, fn, reads, writes, qinc if inc is None else inc)

    def dma(self, engine, out, in_, reads=(), writes=(), **kw):
        q = _DMA_Q[engine]
        self._add(engine, q, lambda e: e.dma_start(out=out, in_=in_, **kw),
                  reads, writes, 16)

    def cc(self, fn, reads=(), writes=()):
        self._add("gpsimd", "cc", fn, reads, writes, 16)

    def barrier(self):
        for q in _ALL_Q:
            self.floor[q] = self.cum[q]


def run_block(nc, sbd, name="main"):
    with contextlib.ExitStack() as st:
        sems = {q: st.enter_context(nc.semaphore(f"q_{q}_{name}"))
                for q in _ALL_Q}
        block = st.enter_context(nc.Block(name))
        totals = dict(sbd.cum)

        def mk(engname):
            def _(e):
                done = {}
                own = {}
                for (en, fn, waits, q, inc) in sbd.items:
                    if en != engname:
                        continue
                    for wq, wv in waits.items():
                        if wv <= 0 or done.get(wq, 0) >= wv:
                            continue
                        e.wait_ge(sems[wq], wv)
                        done[wq] = wv
                    fn(e).then_inc(sems[q], inc)
                    own[q] = own.get(q, 0) + inc
                for q2 in _ALL_Q:
                    if totals[q2] > 0 and done.get(q2, 0) < totals[q2]:
                        e.wait_ge(sems[q2], totals[q2])
            return _

        block.sync(mk("sync"))
        block.gpsimd(mk("gpsimd"))
        block.tensor(mk("tensor"))
        block.vector(mk("vector"))
        block.scalar(mk("scalar"))



def host_rope_tables():
    j = np.arange(HD, dtype=np.float64)
    inv = 1.0 / (10000.0 ** ((2.0 * (j % 32)) / HD))
    pos = np.arange(S, dtype=np.float64)
    th = pos[None, :] * inv[:, None]
    cos2 = np.concatenate([np.cos(th), np.cos(th)], 0).astype(np.float32)
    sin2 = np.concatenate([np.sin(th), np.sin(th)], 0).astype(np.float32)
    return cos2, sin2


def host_psgn():
    P = np.zeros((128, 128), dtype=np.float32)
    for blk in range(2):
        o = blk * 64
        for j in range(32):
            P[o + j + 32, o + j] = -1.0
            P[o + j, o + j + 32] = 1.0
    return P.astype(BF)


def host_masks(c):
    p0, p1 = core_positions(c)
    out = np.zeros((2 * MASK_W, 128, GSZ), dtype=np.float32)
    for slot, p in ((0, p0), (1, p1)):
        for w in range(MASK_W):
            kt = MASK_START[slot] + w
            m = out[slot * MASK_W + w]
            for s4 in range(4):
                diag = 4 * p + s4
                if kt < diag:
                    m[:, s4 * 128:(s4 + 1) * 128] = 1.0
                elif kt == diag:
                    kk = np.arange(128)[:, None]
                    qq = np.arange(128)[None, :]
                    m[:, s4 * 128:(s4 + 1) * 128] = (kk <= qq).astype(np.float32)
    return np.ascontiguousarray(out.astype(BF))


def host_band_vec(v, width=128):
    v = np.asarray(v, dtype=np.float32)
    n = v.shape[0]
    m = -(-n // width)
    out = np.zeros((width, m), dtype=np.float32)
    for j in range(m):
        seg = v[j * width:(j + 1) * width]
        out[:seg.shape[0], j] = seg
    return np.ascontiguousarray(out)



class Ctx:
    def __init__(self, name):
        self.nc = bass.Bass("TRN2", target_bir_lowering=False, name=name)
        self.sb = SB()
        self._fc = {}
        self._stack = None
        self._uid = 0

    def uid(self):
        self._uid += 1
        return self._uid

    def sbuf_p(self, name, shape, dtype=F32):
        return self.nc.alloc_sbuf_tensor(name, shape, dtype).ap()

    @contextlib.contextmanager
    def phase(self, name):
        assert self._stack is None, "phases cannot nest"
        self._stack = contextlib.ExitStack()
        self._pstack = contextlib.ExitStack()
        try:
            yield self
        finally:
            self.sb.barrier()
            self._pstack.close()
            self._pstack = None
            self._stack.close()
            self._stack = None

    def psum_flush(self):
        self.sb.barrier()
        self._pstack.close()
        self._pstack = contextlib.ExitStack()

    def sbuf(self, name, shape, dtype=F32):
        h = self._stack.enter_context(
            self.nc.sbuf_tensor(f"{name}_{self.uid()}", shape, dtype))
        return h.ap()

    def psum(self, name, dtype=F32):
        shape = [128, 512] if dtype == F32 else [128, 1024]
        h = self._pstack.enter_context(
            self.nc.psum_tensor(f"{name}_{self.uid()}", shape, dtype))
        return h.ap()

    def inp(self, name, shape, dtype=F32):
        return self.nc.dram_tensor(name, shape, dtype, kind="ExternalInput").ap()

    def out(self, name, shape, dtype=F32):
        return self.nc.dram_tensor(name, shape, dtype, kind="ExternalOutput").ap()

    def dram(self, name, shape, dtype=F32):
        return self.nc.dram_tensor(name, shape, dtype, kind="Internal").ap()

    def const(self, name, arr):
        return self.nc.inline_tensor(np.ascontiguousarray(arr),
                                     f"{name}_{self.uid()}").ap()

    def fconst(self, val):
        if val not in self._fc:
            i = len(self._fc)
            dram = self.const(f"fc{i}", np.full((128, 1), val, dtype=np.float32))
            tile = self.sbuf_p(f"fct{i}", [128, 1])
            key = f"fc{i}"
            self.sb.dma("sync", tile, dram, writes=[key])
            self._fc[val] = (tile, key)
        return self._fc[val]


def alloc_quant_tmp(ctx):
    return {
        "B": ctx.sbuf("q_B", [128, 2048], F32),
        "C": ctx.sbuf("q_C", [128, 2048], F32),
        "D": ctx.sbuf("q_D", [128, 2048], F32),
        "E": ctx.sbuf("q_E", [128, 2048], F32),
        "am": ctx.sbuf("q_am", [128, 128], F32),
        "sc": ctx.sbuf("q_sc", [128, 128], F32),
        "rin": ctx.sbuf("q_rin", [128, 128], F32),
    }


def emit_fp4_quant(ctx, src, dst, tmp, src_key, dst_key):
    sb = ctx.sb
    F = src.shape[-1]
    nb = F // 16
    src3 = src.rearrange("p (a b) -> p a b", b=16)
    am, sc, rin = tmp["am"][:, :nb], tmp["sc"][:, :nb], tmp["rin"][:, :nb]
    Bb, C = tmp["B"][:, :F], tmp["C"][:, :F]
    Dd, E = tmp["D"][:, :F], tmp["E"][:, :F]
    B3 = Bb.rearrange("p (a b) -> p a b", b=16)
    C3 = C.rearrange("p (a b) -> p a b", b=16)
    sb.add("vector", lambda e: e.tensor_reduce(
        am, src3, AX.X, ALU.max, apply_absolute_value=True),
        reads=[src_key], writes=["qam"])
    sb.add("vector", lambda e: e.tensor_scalar(am, am, 1e-30, None, ALU.max),
        reads=["qam"], writes=["qam"])
    sb.add("vector", lambda e: e.tensor_scalar(sc, am, 0.16666667, None, ALU.mult),
        reads=["qam"], writes=["qsc"])
    sb.add("vector", lambda e: e.reciprocal(rin, sc),
        reads=["qsc"], writes=["qrin"])
    sb.add("scalar", lambda e: e.activation(C, src, AF.Abs),
        reads=[src_key], writes=["qC"])
    rb = rin.unsqueeze(2).broadcast_to([128, nb, 16])
    sb.add("vector", lambda e: e.tensor_tensor(B3, C3, rb, ALU.mult),
        reads=["qC", "qrin"], writes=["qB"])
    sb.add("vector", lambda e: e.tensor_scalar(C, Bb, 2.0, None, ALU.is_ge),
        reads=["qB"], writes=["qC"])
    sb.add("vector", lambda e: e.tensor_scalar(Dd, C, -1.0, 2.0, ALU.mult, ALU.add),
        reads=["qC"], writes=["qD"])
    sb.add("vector", lambda e: e.tensor_scalar(E, C, 0.5, 0.5, ALU.mult, ALU.add),
        reads=["qC"], writes=["qE"])
    sb.add("vector", lambda e: e.tensor_scalar(C, Bb, 4.0, None, ALU.is_ge),
        reads=["qB"], writes=["qC"])
    sb.add("vector", lambda e: e.scalar_tensor_tensor(
        Dd, C, -0.5, Dd, ALU.mult, ALU.add), reads=["qC", "qD"], writes=["qD"])
    sb.add("vector", lambda e: e.scalar_tensor_tensor(
        E, C, 1.0, E, ALU.mult, ALU.add), reads=["qC", "qE"], writes=["qE"])
    sb.add("vector", lambda e: e.tensor_tensor(Bb, Bb, Dd, ALU.mult),
        reads=["qB", "qD"], writes=["qB"])
    sb.add("vector", lambda e: e.tensor_scalar(Dd, Bb, 0.5, None, ALU.is_gt),
        reads=["qB"], writes=["qD"])
    for bnd in (1.5, 2.5, 3.5):
        sb.add("vector", lambda e, bnd=bnd: e.scalar_tensor_tensor(
            Dd, Bb, bnd, Dd, ALU.is_gt, ALU.add), reads=["qB", "qD"], writes=["qD"])
    sb.add("vector", lambda e: e.tensor_tensor(Bb, Dd, E, ALU.mult),
        reads=["qD", "qE"], writes=["qB"])
    sb.add("scalar", lambda e: e.activation(C, src, AF.Sign),
        reads=[src_key], writes=["qC"])
    scb = sc.unsqueeze(2).broadcast_to([128, nb, 16])
    sb.add("vector", lambda e: e.tensor_tensor(C3, C3, scb, ALU.mult),
        reads=["qC", "qsc"], writes=["qC"])
    sb.add("vector", lambda e: e.tensor_tensor(dst, Bb, C, ALU.mult),
        reads=["qB", "qC"], writes=[dst_key])


def emit_ln(ctx, xr, scratch, stats, key):
    sb = ctx.sb
    s_sum, s_ssq = stats[:, 0:1], stats[:, 1:2]
    s_mean, s_var = stats[:, 2:3], stats[:, 3:4]
    s_sd, s_rstd, s_m2 = stats[:, 4:5], stats[:, 5:6], stats[:, 6:7]
    sb.add("vector", lambda e: e.reduce_sum(s_sum, xr, AX.X),
        reads=[key], writes=["lnst"])
    sb.add("scalar", lambda e: e.activation(scratch, xr, AF.Square, accum_out=s_ssq),
        reads=[key], writes=["lnscr", "lnst"])
    sb.add("vector", lambda e: e.tensor_scalar(s_mean, s_sum, 1.0 / D, None, ALU.mult),
        reads=["lnst"], writes=["lnst"])
    sb.add("vector", lambda e: e.tensor_tensor(s_m2, s_mean, s_mean, ALU.mult),
        reads=["lnst"], writes=["lnst"])
    sb.add("vector", lambda e: e.tensor_scalar(s_var, s_ssq, 1.0 / D, None, ALU.mult),
        reads=["lnst"], writes=["lnst"])
    sb.add("vector", lambda e: e.tensor_tensor(s_var, s_var, s_m2, ALU.subtract),
        reads=["lnst"], writes=["lnst"])
    epsap, epskey = ctx.fconst(1e-5)
    sb.add("scalar", lambda e: e.activation(s_sd, s_var, AF.Sqrt, bias=epsap),
        reads=["lnst", epskey], writes=["lnst"])
    sb.add("vector", lambda e: e.reciprocal(s_rstd, s_sd),
        reads=["lnst"], writes=["lnst"])
    sb.add("vector", lambda e: e.tensor_scalar(
        xr, xr, s_mean, s_rstd, ALU.subtract, ALU.mult),
        reads=[key, "lnst"], writes=[key])



def ph_wprep1(ctx, w1ts, w2ts, partials_out, pkey):
    sb = ctx.sb
    with ctx.phase("wprep1"):
        onescol = ctx.const("onescol", np.ones((128, 1), dtype=np.float32))
        tile = ctx.sbuf("w1tile", [128, 8192])
        ab = ctx.sbuf("w1ab", [128, 8192])
        red = ctx.sbuf("w1red", [128, 16])
        red2 = ctx.sbuf("w1red2", [128, 2])
        onest = ctx.sbuf("w1ones", [128, 1])
        ps = ctx.psum("w1ps")
        sb.dma("sync", onest, onescol, writes=["on"])
        w1v = w1ts.rearrange("(n p) d -> n p d", p=128)
        w2v = w2ts.rearrange("(n p) d -> n p d", p=128)
        col = 0
        for (v, n, width) in ((w1v, 2, 8192), (w2v, 8, 2048)):
            for i in range(n):
                sb.dma("sync", tile[:, :width], v[i], writes=["w1t"])
                sb.add("scalar", lambda e, w=width: e.activation(
                    ab[:, :w], tile[:, :w], AF.Abs), reads=["w1t"], writes=["w1a"])
                sb.add("vector", lambda e, c=col, w=width: e.tensor_reduce(
                    red[:, c:c + 1], ab[:, :w], AX.X, ALU.add),
                    reads=["w1a"], writes=["w1r"])
                col += 1
        sb.add("vector", lambda e: e.tensor_reduce(
            red2[:, 0:1], red[:, 0:2], AX.X, ALU.add), reads=["w1r"], writes=["w1r2"])
        sb.add("vector", lambda e: e.tensor_reduce(
            red2[:, 1:2], red[:, 2:10], AX.X, ALU.add), reads=["w1r"], writes=["w1r2"])
        sb.add("tensor", lambda e: e.matmul(ps[0:1, 0:2], onest, red2,
                                            start=True, stop=True),
               reads=["w1r2", "on"], writes=["w1ps"])
        pw = partials_out.shape[1]
        outt = ctx.sbuf("w1outt", [1, 256])
        sb.add("vector", lambda e: e.memset(outt, 0.0), writes=["w1o"])
        sb.add("vector", lambda e: e.tensor_copy(outt[:, 0:2], ps[0:1, 0:2]),
               reads=["w1ps"], writes=["w1o"])
        sb.dma("sync", partials_out, outt[:, 0:pw], reads=["w1o"], writes=[pkey])


def ph_wquant(ctx, srcs, qt_outs, ntiles, okeys):
    sb = ctx.sb
    with ctx.phase("wquant"):
        ident = ctx.const("identf", np.eye(128, dtype=np.float32))
        identt = ctx.sbuf("widentt", [128, 128], F32)
        sb.dma("sync", identt, ident, writes=["wid"])
        tmp = alloc_quant_tmp(ctx)
        wt = ctx.sbuf("wqt_in", [128, 2048])
        qw = ctx.sbuf("wqt_q", [128, 2048], F32)
        oqt = ctx.sbuf("wqt_o", [128, 16 * 128], F32)
        pst = ctx.psum("wqt_ps")
        for (src, qt_out, okey) in zip(srcs, qt_outs, okeys):
            wv = src.rearrange("(t p) d -> t p d", p=128)
            ov = qt_out.rearrange("(a p) o -> p a o", p=128)
            for t in range(ntiles):
                sb.dma("sync", wt, wv[t], writes=["wqw"])
                emit_fp4_quant(ctx, wt, qw, tmp, "wqw", "wqq")
                for ic in range(16):
                    sb.add("tensor", lambda e, ic=ic: e.transpose(
                        pst[:, 0:128], qw[:, ic * 128:(ic + 1) * 128], identt),
                        reads=["wqq", "wid"], writes=["wqps"])
                    sb.add("scalar", lambda e, ic=ic: e.copy(
                        oqt[:, ic * 128:(ic + 1) * 128], pst[:, 0:128]),
                        reads=["wqps"], writes=["wqo"])
                sb.dma("sync", ov[:, :, t * 128:(t + 1) * 128],
                       oqt.rearrange("p (a o) -> p a o", o=128),
                       reads=["wqo"], writes=[okey])


def ph_tern(ctx, partials_all, w1ts, w2ts, w1q_out, w2q_out, pkey, okeys):
    sb = ctx.sb
    with ctx.phase("tern"):
        ones8 = ctx.const("ones8", np.ones((8, 1), dtype=np.float32))
        ones1r = ctx.const("ones1r", np.ones((1, 128), dtype=np.float32))
        part_t = ctx.sbuf("tpart", [8, 16])
        on8 = ctx.sbuf("ton8", [8, 1])
        on1r = ctx.sbuf("ton1", [1, 128])
        scal = ctx.sbuf("tscal", [1, 2])
        thr = ctx.sbuf("tthr", [128, 2])
        wt = ctx.sbuf("twt", [128, 2048])
        ab = ctx.sbuf("tab", [128, 2048])
        sg = ctx.sbuf("tsg", [128, 2048])
        tq = ctx.sbuf("ttq", [128, 2048], BF16)
        ps = ctx.psum("tps")
        sb.dma("sync", part_t, partials_all, reads=[pkey], writes=["tpt"])
        sb.dma("sync", on8, ones8, writes=["to8"])
        sb.dma("sync", on1r, ones1r, writes=["to1"])
        sb.add("tensor", lambda e: e.matmul(ps[0:1, 0:2], on8, part_t[:, 0:2],
                                            start=True, stop=True),
               reads=["tpt", "to8"], writes=["tpsk"])
        sb.add("vector", lambda e: e.tensor_copy(scal, ps[0:1, 0:2]),
               reads=["tpsk"], writes=["tsc"])
        sb.add("vector", lambda e: e.tensor_scalar(
            scal, scal, 1.0 / 16777216.0, None, ALU.mult),
            reads=["tsc"], writes=["tsc"])
        sb.add("vector", lambda e: e.tensor_scalar(scal, scal, 1e-8, None, ALU.max),
            reads=["tsc"], writes=["tsc"])
        sb.add("vector", lambda e: e.tensor_scalar(scal, scal, 0.5, None, ALU.mult),
            reads=["tsc"], writes=["tsc"])
        sb.add("tensor", lambda e: e.matmul(ps[0:128, 2:4], on1r, scal,
                                            start=True, stop=True),
               reads=["tsc", "to1"], writes=["tpsk"])
        sb.add("vector", lambda e: e.tensor_copy(thr, ps[0:128, 2:4]),
               reads=["tpsk"], writes=["tth"])
        for (src, o_t, nt_, width, ti, okey) in (
                (w1ts, w1q_out, 2, 8192, 0, okeys[0]),
                (w2ts, w2q_out, 8, 2048, 1, okeys[1])):
            v = src.rearrange("(n p) d -> n p d", p=128)
            o = o_t.rearrange("(n p) d -> n p d", p=128)
            nch = width // 2048
            for i in range(nt_):
                for ch in range(nch):
                    cs = slice(ch * 2048, (ch + 1) * 2048)
                    sb.dma("sync", wt, v[i][:, cs], writes=["twq"])
                    sb.add("scalar", lambda e: e.activation(ab, wt, AF.Abs),
                           reads=["twq"], writes=["tta"])
                    sb.add("scalar", lambda e: e.activation(sg, wt, AF.Sign),
                           reads=["twq"], writes=["tts"])
                    sb.add("vector", lambda e, ti=ti: e.tensor_scalar(
                        ab, ab, thr[:, ti:ti + 1], None, ALU.is_gt),
                        reads=["tta", "tth"], writes=["tta"])
                    sb.add("vector", lambda e: e.tensor_tensor(
                        tq, ab, sg, ALU.mult), reads=["tta", "tts"], writes=["ttr"])
                    sb.dma("sync", o[i][:, cs], tq, reads=["ttr"], writes=[okey])


def _wcols(w, j0, j1):
    if isinstance(w, tuple):
        _, v3, row0, rows = w
        out = []
        j = j0
        while j < j1:
            r = j // 256
            hi = min(j1, (r + 1) * 256)
            out.append((j - j0, v3[r, row0:row0 + rows, j - r * 256:hi - r * 256]))
            j = hi
        return out
    return [(0, w[:, j0:j1])]


def _wshape1(w):
    if isinstance(w, tuple):
        return 8 * 256
    return w.shape[1]


def ph_qkv(ctx, layer, x_in, wq, wk, wv, cos_in, sin_in,
           qT_o, kT_o, v_o, in_keys, out_keys):
    l0 = (layer == 0)
    KD = _wshape1(wk) if not isinstance(wk, tuple) else kT_o.shape[0]
    sb = ctx.sb
    with ctx.phase(f"qkv{layer}"):
        ident = ctx.const("identb", np.eye(128, dtype=BF))
        psgn = ctx.const("psgn", host_psgn())
        xt = ctx.sbuf("kx", [128, 2048])
        xqb = ctx.sbuf("kxq", [128, 2048], BF16)
        xqT = ctx.sbuf("kxqT", [128, NIC * TOK], BF16)
        identt = ctx.sbuf("kid", [128, 128], BF16)
        psgnt = ctx.sbuf("kpg", [128, 128], BF16)
        cost = ctx.sbuf("kcos", [128, TOK])
        sint = ctx.sbuf("ksin", [128, TOK])
        wslab = [ctx.sbuf(f"kws{i}", [128, NIC * 128], BF16) for i in range(2)]
        wvfull = ctx.sbuf("kwv", [128, NIC * min(KD, 2048)], BF16)
        qs = ctx.sbuf("kqs", [128, 512])
        qb = ctx.sbuf("kqb", [128, 512], BF16)
        ra = ctx.sbuf("kra", [128, 512])
        rb_ = ctx.sbuf("krb", [128, 512])
        rout = ctx.sbuf("kro", [128, 512])
        routb = ctx.sbuf("krob", [128, 512], BF16)
        vout = ctx.sbuf("kvo", [128, 512])
        tmp = alloc_quant_tmp(ctx) if l0 else None
        pp = ctx.psum("kpp")
        pr = ctx.psum("kpr")
        pt = ctx.psum("kpt", BF16)

        sb.dma("sync", identt, ident, writes=["kid"])
        sb.dma("sync", psgnt, psgn, writes=["kpg"])
        sb.dma("sync", cost, cos_in, writes=["kcos"])
        sb.dma("sync", sint, sin_in, writes=["ksin"])

        xv = x_in.rearrange("(n p) d -> n p d", p=128)
        xqTv = xqT.rearrange("p (a t) -> p a t", t=TOK)
        for i in range(NT):
            sb.dma("sync", xt, xv[i], reads=[in_keys.get("x", "_")], writes=["kx"])
            if l0:
                emit_fp4_quant(ctx, xt, xqb, tmp, "kx", "kxq")
            else:
                sb.add("vector", lambda e: e.tensor_copy(xqb, xt),
                       reads=["kx"], writes=["kxq"])
            for ic in range(NIC):
                sb.add("tensor", lambda e, ic=ic: e.transpose(
                    pt[:, 0:128], xqb[:, ic * 128:(ic + 1) * 128], identt),
                    reads=["kxq", "kid"], writes=["kpt"])
                sb.add("scalar", lambda e, ic=ic, i=i: e.copy(
                    xqTv[:, ic, i * 128:(i + 1) * 128], pt[:, 0:128]),
                    reads=["kpt"], writes=["kxT"])

        for (w, o_t, od, wkey, okey, is_q) in (
                (wq, qT_o, D, in_keys.get("wq", "_"), out_keys["q"], True),
                (wk, kT_o, KD, in_keys.get("wk", "_"), out_keys["k"], False)):
            ov = o_t.rearrange("(a p) t -> p a t", p=128)
            for dc in range(od // 128):
                buf = wslab[dc % 2]
                kwsl = f"kwsl{dc % 2}"
                for (doff, src) in _wcols(w, dc * 128, (dc + 1) * 128):
                    wsub = src.shape[1]
                    sb.dma("gpsimd",
                           buf.rearrange("p (a o) -> p a o", o=128)
                           [:, :, doff:doff + wsub],
                           src.rearrange("(a p) o -> p a o", p=128),
                           reads=[wkey], writes=[kwsl])
                for tb in range(2):
                    for ic in range(NIC):
                        sb.add("tensor", lambda e, ic=ic, tb=tb, buf=buf: e.matmul(
                            pp[:, 0:512], buf[:, ic * 128:(ic + 1) * 128],
                            xqTv[:, ic, tb * 512:(tb + 1) * 512],
                            start=(ic == 0), stop=(ic == NIC - 1)),
                            reads=[kwsl, "kxT"], writes=["kpp"])
                    sb.add("vector", lambda e: e.tensor_copy(qs, pp[:, 0:512]),
                           reads=["kpp"], writes=["kqs"])
                    sb.add("scalar", lambda e: e.copy(qb, qs),
                           reads=["kqs"], writes=["kqb"])
                    sb.add("tensor", lambda e: e.matmul(
                        pr[:, 0:512], psgnt, qb, start=True, stop=True),
                        reads=["kqb", "kpg"], writes=["kpr"])
                    sb.add("vector", lambda e, tb=tb: e.tensor_tensor(
                        ra, qs, cost[:, tb * 512:(tb + 1) * 512], ALU.mult),
                        reads=["kqs", "kcos"], writes=["kra"])
                    sb.add("vector", lambda e, tb=tb: e.tensor_tensor(
                        rb_, pr[:, 0:512], sint[:, tb * 512:(tb + 1) * 512],
                        ALU.mult), reads=["kpr", "ksin"], writes=["krb"])
                    if is_q:
                        sb.add("vector", lambda e: e.tensor_tensor(
                            routb, ra, rb_, ALU.add),
                            reads=["kra", "krb"], writes=["kro"])
                        sb.dma("sync", ov[:, dc, tb * 512:(tb + 1) * 512], routb,
                               reads=["kro"], writes=[okey])
                    else:
                        sb.add("vector", lambda e: e.tensor_tensor(
                            rout, ra, rb_, ALU.add),
                            reads=["kra", "krb"], writes=["kro"])
                        sb.dma("sync", ov[:, dc, tb * 512:(tb + 1) * 512], rout,
                               reads=["kro"], writes=[okey])

        wvfv = wvfull.rearrange("p (a o) -> p a o", o=min(KD, 2048))
        for (doff, src) in _wcols(wv, 0, KD):
            wsub = src.shape[1]
            sb.dma("gpsimd", wvfv[:, :, doff:doff + wsub],
                   src.rearrange("(a p) o -> p a o", p=128),
                   reads=[in_keys.get("wv", "_")], writes=["kwvf"])
        vv = v_o.rearrange("(n p) d -> n p d", p=128)
        nvb = max(1, KD // 512)
        vw = min(512, KD)
        for i in range(NT):
            for ob in range(nvb):
                for ic in range(NIC):
                    sb.add("tensor", lambda e, ic=ic, i=i, ob=ob: e.matmul(
                        pp[:, 0:vw], xqTv[:, ic, i * 128:(i + 1) * 128],
                        wvfv[:, ic, ob * 512:ob * 512 + vw],
                        start=(ic == 0), stop=(ic == NIC - 1)),
                        reads=["kwvf", "kxT"], writes=["kpp"])
                sb.add("scalar", lambda e: e.copy(vout[:, 0:vw], pp[:, 0:vw]),
                       reads=["kpp"], writes=["kvo"])
                sb.dma("sync", vv[i][:, ob * 512:ob * 512 + vw], vout[:, 0:vw],
                       reads=["kvo"], writes=[out_keys["v"]])


def ph_attn(ctx, layer, qT_in, kTf, vf, masks, x_in, wo, x1_o, in_keys, okey):
    l0 = (layer == 0)
    KD = kTf.shape[0]
    VC = 130
    sb = ctx.sb
    kq = in_keys.get("q", "_")
    kk = in_keys.get("k", "_")
    kv_ = in_keys.get("v", "_")
    kwo = in_keys.get("wo", "_")
    kxk = in_keys.get("x", "_")
    with ctx.phase(f"attn{layer}"):
        ident = ctx.const("identb2", np.eye(128, dtype=BF))
        qTt = ctx.sbuf("aqT", [128, 16 * GSZ], BF16)
        maskt = ctx.sbuf("amask", [128, MASK_W * GSZ], BF16)
        kthp2 = [ctx.sbuf(f"akthp{i}", [128, S], BF16) for i in range(2)]
        vhp2 = [ctx.sbuf(f"avhp{i}", [128, 32 * VC], BF16) for i in range(2)]
        pT2 = [ctx.sbuf(f"apT{i}", [128, GSZ], BF16) for i in range(2)]
        aos = ctx.sbuf("aaos", [128, 4 * 2048])
        aoq = ctx.sbuf("aaoq", [128, 2048], BF16)
        oqT = ctx.sbuf("aoqT", [128, 16 * GSZ], BF16)
        wslb = [ctx.sbuf(f"awsl{i}", [128, NIC * 512], BF16) for i in range(2)]
        rden = ctx.sbuf("arden", [128, 1])
        identt = ctx.sbuf("aid", [128, 128], BF16)
        lnscr = ctx.sbuf("alnscr", [128, 2048])
        stats = ctx.sbuf("astats", [128, 8])
        tmp = alloc_quant_tmp(ctx) if l0 else None
        ps_s = [ctx.psum("aps_s0"), ctx.psum("aps_s1")]
        ps_o = [ctx.psum(f"aps_o{i}") for i in range(4)]
        pp = ctx.psum("app")
        pt = ctx.psum("apt", BF16)

        sb.dma("sync", identt, ident, writes=["aid"])
        vhpv2 = [v.rearrange("p (a c) -> p a c", c=VC) for v in vhp2]
        for i in range(2):
            sb.add("vector", lambda e, i=i: e.memset(vhpv2[i][:, :, 64:65], 1.0),
                   writes=[f"avhp{i}"])
            sb.add("vector", lambda e, i=i: e.memset(vhpv2[i][:, :, 129:130], 1.0),
                   writes=[f"avhp{i}"])

        qTv = qT_in.rearrange("(a p) t -> p a t", p=128)
        x1v = x1_o.rearrange("(n p) d -> n p d", p=128)
        xv = x_in.rearrange("(n p) d -> n p d", p=128)
        qTtv = qTt.rearrange("p (a t) -> p a t", t=GSZ)
        oqTv = oqT.rearrange("p (a t) -> p a t", t=GSZ)
        wslv = [w.rearrange("p (a o) -> p a o", o=512) for w in wslb]

        shap, shkey = ctx.fconst(EXP_SHIFT)
        scap, sckey = ctx.fconst(SCALE_ATT)
        for slot in range(2):
            ktn = KT_PAD[slot]
            sb.dma("sync", qTtv, qTv[:, :, slot * GSZ:(slot + 1) * GSZ],
                   reads=[kq], writes=["aqT"])
            sb.dma("sync", maskt.rearrange("p (m q) -> p m q", q=GSZ),
                   masks[slot * MASK_W:(slot + 1) * MASK_W]
                   .rearrange("m p q -> p m q"),
                   reads=[in_keys.get("masks", "_")], writes=["amask"])
            for hp in range(16 if l0 else 8):
                bi = hp % 2
                kthp = kthp2[bi]
                vhpv = vhpv2[bi]
                kkt, kvh = f"akthp{bi}", f"avhp{bi}"
                if l0:
                    sb.dma("gpsimd", kthp[:, 0:ktn * 128],
                           kTf[hp * 128:(hp + 1) * 128, 0:ktn * 128],
                           reads=[kk], writes=[kkt])
                    for half in range(2):
                        sb.dma("gpsimd",
                               vhpv[:, 0:ktn, 65 * half:65 * half + 64],
                               vf[0:ktn * 128,
                                  hp * 128 + 64 * half:hp * 128 + 64 * half + 64]
                               .rearrange("(t p) c -> p t c", p=128),
                               reads=[kv_], writes=[kvh])
                else:
                    for half in range(2):
                        sb.dma("gpsimd", kthp[64 * half:64 * half + 64, 0:ktn * 128],
                               kTf[hp * 64:(hp + 1) * 64, 0:ktn * 128],
                               reads=[kk], writes=[kkt])
                    sb.dma("gpsimd", vhpv[:, 0:ktn, 0:64],
                           vf[0:ktn * 128, hp * 64:(hp + 1) * 64]
                           .rearrange("(t p) c -> p t c", p=128),
                           reads=[kv_], writes=[kvh])
                for h2 in range(2 if l0 else 4):
                    qh = hp * (2 if l0 else 4) + h2
                    krow = 64 * (qh % 2)
                    vcol = (65 * (qh % 2)) if l0 else 0
                    for kt in range(ktn):
                        pss = ps_s[kt % 2]
                        kss = f"apss{kt % 2}"
                        pT = pT2[kt % 2]
                        kpt = f"apT{kt % 2}"
                        sb.add("tensor", lambda e, kt=kt, qh=qh, krow=krow,
                               pss=pss, kthp=kthp: e.matmul(
                            pss[:, 0:GSZ],
                            kthp[krow:krow + 64, kt * 128:(kt + 1) * 128],
                            qTtv[64 * (qh % 2):64 * (qh % 2) + 64, qh // 2, :],
                            start=True, stop=True),
                            reads=[kkt, "aqT"], writes=[kss])
                        sb.add("scalar", lambda e, pss=pss, pT=pT, shap=shap,
                               scap=scap:
                               e.activation(pT, pss[:, 0:GSZ], AF.Exp,
                                            bias=shap, scale=scap),
                               reads=[kss, shkey, sckey], writes=[kpt])
                        wi = kt - MASK_START[slot]
                        if 0 <= wi < MASK_W:
                            sb.add("vector", lambda e, wi=wi, pT=pT: e.tensor_tensor(
                                pT, pT, maskt[:, wi * GSZ:(wi + 1) * GSZ],
                                ALU.mult),
                                reads=[kpt, "amask"], writes=[kpt])
                        for s4 in range(4):
                            sb.add("tensor", lambda e, s4=s4, kt=kt, vcol=vcol,
                                   ktn=ktn, pT=pT, vhpv=vhpv: e.matmul(
                                ps_o[s4][:, 0:65],
                                pT[:, s4 * 128:(s4 + 1) * 128],
                                vhpv[:, kt, vcol:vcol + 65],
                                start=(kt == 0), stop=(kt == ktn - 1)),
                                reads=[kpt, kvh], writes=[f"apso{s4}"])
                    for s4 in range(4):
                        sb.add("vector", lambda e, s4=s4: e.reciprocal(
                            rden, ps_o[s4][:, 64:65]),
                            reads=[f"apso{s4}"], writes=["arden"])
                        sb.add("vector", lambda e, s4=s4, qh=qh: e.tensor_scalar(
                            aos[:, s4 * 2048 + qh * 64: s4 * 2048 + qh * 64 + 64],
                            ps_o[s4][:, 0:64], rden, None, ALU.mult),
                            reads=[f"apso{s4}", "arden"], writes=["aaos"])

            for s4 in range(4):
                if l0:
                    emit_fp4_quant(ctx, aos[:, s4 * 2048:(s4 + 1) * 2048], aoq,
                                   tmp, "aaos", "aaoq")
                else:
                    sb.add("vector", lambda e, s4=s4: e.tensor_copy(
                        aoq, aos[:, s4 * 2048:(s4 + 1) * 2048]),
                        reads=["aaos"], writes=["aaoq"])
                for ic in range(NIC):
                    sb.add("tensor", lambda e, ic=ic: e.transpose(
                        pt[:, 0:128], aoq[:, ic * 128:(ic + 1) * 128], identt),
                        reads=["aaoq", "aid"], writes=["apt"])
                    sb.add("scalar", lambda e, ic=ic, s4=s4: e.copy(
                        oqTv[:, ic, s4 * 128:(s4 + 1) * 128], pt[:, 0:128]),
                        reads=["apt"], writes=["aoqT"])
            for tt in range(4):
                sb.dma("sync", aos[:, tt * 2048:(tt + 1) * 2048],
                       xv[slot * 4 + tt], reads=[kxk], writes=["aaos"])
            for ob in range(4):
                buf = wslv[ob % 2]
                kwsl = f"awosl{ob % 2}"
                for (doff, src) in _wcols(wo, ob * 512, (ob + 1) * 512):
                    wsub = src.shape[1]
                    sb.dma("gpsimd", buf[:, :, doff:doff + wsub],
                           src.rearrange("(a p) o -> p a o", p=128),
                           reads=[kwo], writes=[kwsl])
                for tt in range(4):
                    for ic in range(NIC):
                        sb.add("tensor", lambda e, ic=ic, tt=tt, buf=buf: e.matmul(
                            pp[:, 0:512], oqTv[:, ic, tt * 128:(tt + 1) * 128],
                            buf[:, ic, :], start=(ic == 0), stop=(ic == NIC - 1)),
                            reads=["aoqT", kwsl], writes=["app"])
                    sb.add("vector", lambda e, tt=tt, ob=ob: e.tensor_tensor(
                        aos[:, tt * 2048 + ob * 512: tt * 2048 + (ob + 1) * 512],
                        aos[:, tt * 2048 + ob * 512: tt * 2048 + (ob + 1) * 512],
                        pp[:, 0:512], ALU.add),
                        reads=["app", "aaos"], writes=["aaos"])
            for tt in range(4):
                xr = aos[:, tt * 2048:(tt + 1) * 2048]
                emit_ln(ctx, xr, lnscr, stats, "aaos")
                sb.dma("sync", x1v[slot * 4 + tt], xr, reads=["aaos"],
                       writes=[okey])


def ph_ffn(ctx, layer, x_in, wg, wu, wd, vec1, vec2, xo, in_keys, okey):
    l0 = (layer == 0)
    MID = INTER0 if l0 else DFF1
    NMC = MID // 128
    PASS_MC = 22 if l0 else 16
    NPASS = NMC // PASS_MC
    sb = ctx.sb
    kx = in_keys.get("x", "_")
    kwg = in_keys.get("wg", "_")
    kwu = in_keys.get("wu", "_")
    kwd = in_keys.get("wd", "_")
    kvec = in_keys.get("vec", "_")
    with ctx.phase(f"ffn{layer}"):
        ident = ctx.const("identb3", np.eye(128, dtype=BF))
        ones1 = ctx.const("ones1f", np.ones((1, 128), dtype=np.float32))
        xT = ctx.sbuf("fxT", [128, NIC * TOK], BF16)
        x2pre = ctx.sbuf("fx2", [128, NT * 2048])
        hfull = ctx.sbuf("fh", [128, PASS_MC * TOK], BF16)
        gsl = [ctx.sbuf(f"fg{i}", [128, NIC * 128], BF16) for i in range(2)]
        usl = [ctx.sbuf(f"fu{i}", [128, NIC * 128], BF16) for i in range(2)]
        dslb = [ctx.sbuf(f"fd{i}", [128, 512], BF16) for i in range(2)]
        dstg = [ctx.sbuf(f"fds{i}", [128, 512]) for i in range(2)] if l0 else None
        bcast = ctx.sbuf("fbc", [128, 2048])
        rowt = ctx.sbuf("frow", [1, 2048])
        hg = ctx.sbuf("fhg", [128, 512])
        hu = ctx.sbuf("fhu", [128, 512])
        xbt = ctx.sbuf("fxb", [128, 2048], BF16)
        xtile = ctx.sbuf("fxt", [128, 2048])
        identt = ctx.sbuf("fid", [128, 128], BF16)
        onest = ctx.sbuf("fon", [1, 128])
        gvec = ctx.sbuf("fgv", [128, 2 * NMC])
        lnscr = ctx.sbuf("flnscr", [128, 2048])
        stats = ctx.sbuf("fstats", [128, 8])
        ps_g = ctx.psum("fps_g")
        ps_u = ctx.psum("fps_u")
        pt = ctx.psum("fpt", BF16)

        sb.dma("sync", identt, ident, writes=["fid"])
        sb.dma("sync", onest, ones1, writes=["fon"])
        if l0:
            sb.dma("sync", gvec[:, 0:NMC], vec1[0], reads=[kvec], writes=["fgv"])
            sb.dma("sync", gvec[:, NMC:2 * NMC], vec1[1], reads=[kvec],
                   writes=["fgv"])
        else:
            sb.dma("sync", gvec[:, 0:NMC], vec1, reads=[kvec], writes=["fgv"])
        sb.dma("sync", rowt, vec2, reads=[kvec], writes=["frow"])

        for j in range(4):
            sb.add("tensor", lambda e, j=j, psg=ps_g: e.matmul(
                psg[:, 0:512], onest, rowt[:, j * 512:(j + 1) * 512],
                start=True, stop=True), reads=["frow", "fon"], writes=["fpsg"])
            sb.add("vector", lambda e, j=j, psg=ps_g: e.tensor_copy(
                bcast[:, j * 512:(j + 1) * 512], psg[:, 0:512]),
                reads=["fpsg"], writes=["fbc"])

        xv = x_in.rearrange("(n p) d -> n p d", p=128)
        xov = xo.rearrange("(n p) d -> n p d", p=128)
        xTv = xT.rearrange("p (a t) -> p a t", t=TOK)
        hv = hfull.rearrange("p (a t) -> p a t", t=TOK)
        x2v = x2pre.rearrange("p (a d) -> p a d", d=2048)

        for i in range(NT):
            sb.dma("sync", xtile, xv[i], reads=[kx], writes=["fxt"])
            if l0:
                sb.add("vector", lambda e, i=i: e.tensor_copy(x2v[:, i, :], xtile),
                       reads=["fxt"], writes=["fx2"])
            else:
                sb.add("vector", lambda e, i=i: e.tensor_tensor(
                    x2v[:, i, :], xtile, bcast, ALU.add),
                    reads=["fxt", "fbc"], writes=["fx2"])
            sb.add("vector", lambda e: e.tensor_copy(xbt, xtile),
                   reads=["fxt"], writes=["fxb"])
            for ic in range(NIC):
                sb.add("tensor", lambda e, ic=ic: e.transpose(
                    pt[:, 0:128], xbt[:, ic * 128:(ic + 1) * 128], identt),
                    reads=["fxb", "fid"], writes=["fpt"])
                sb.add("scalar", lambda e, ic=ic, i=i: e.copy(
                    xTv[:, ic, i * 128:(i + 1) * 128], pt[:, 0:128]),
                    reads=["fpt"], writes=["fxT"])

        wgv = wg.rearrange("(a p) o -> p a o", p=128)
        if l0:
            wuv = wu.rearrange("(a p) o -> p a o", p=128)
        geng = "gpsimd" if l0 else "sync"

        for hp_ in range(NPASS):
            for mcl in range(PASS_MC):
                mc = hp_ * PASS_MC + mcl
                gb = gsl[mcl % 2]
                kg = f"fgsl{mcl % 2}"
                g_eng = geng if l0 else ("sync" if mcl % 2 == 0 else "scalar")
                sb.dma(g_eng, gb.rearrange("p (a o) -> p a o", o=128),
                       wgv[:, :, mc * 128:(mc + 1) * 128],
                       reads=[kwg], writes=[kg])
                if l0:
                    ub = usl[mcl % 2]
                    ku = f"fusl{mcl % 2}"
                    sb.dma("gpsimd", ub.rearrange("p (a o) -> p a o", o=128),
                           wuv[:, :, mc * 128:(mc + 1) * 128],
                           reads=[kwu], writes=[ku])
                for tb in range(2):
                    for ic in range(NIC):
                        sb.add("tensor", lambda e, ic=ic, tb=tb, gb=gb,
                               psg=ps_g: e.matmul(
                            psg[:, 0:512], gb[:, ic * 128:(ic + 1) * 128],
                            xTv[:, ic, tb * 512:(tb + 1) * 512],
                            start=(ic == 0), stop=(ic == NIC - 1)),
                            reads=[kg, "fxT"], writes=["fpsg"])
                    if l0:
                        for ic in range(NIC):
                            sb.add("tensor", lambda e, ic=ic, tb=tb, ub=ub,
                                   psu=ps_u: e.matmul(
                                psu[:, 0:512], ub[:, ic * 128:(ic + 1) * 128],
                                xTv[:, ic, tb * 512:(tb + 1) * 512],
                                start=(ic == 0), stop=(ic == NIC - 1)),
                                reads=[ku, "fxT"], writes=["fpsu"])
                        sb.add("scalar", lambda e, mc=mc, psg=ps_g: e.activation(
                            hg, psg[:, 0:512], AF.Silu, scale=gvec[:, mc:mc + 1]),
                            reads=["fpsg", "fgv"], writes=["fhg"])
                        sb.add("vector", lambda e, mc=mc, psu=ps_u: e.tensor_scalar(
                            hu, psu[:, 0:512], gvec[:, NMC + mc:NMC + mc + 1],
                            None, ALU.mult), reads=["fpsu", "fgv"], writes=["fhu"])
                        sb.add("vector", lambda e, mcl=mcl, tb=tb: e.tensor_tensor(
                            hv[:, mcl, tb * 512:(tb + 1) * 512], hg, hu, ALU.mult),
                            reads=["fhg", "fhu"], writes=["fh"])
                    else:
                        sb.add("scalar", lambda e, mc=mc, mcl=mcl, tb=tb,
                               psg=ps_g:
                               e.activation(
                            hv[:, mcl, tb * 512:(tb + 1) * 512], psg[:, 0:512],
                            AF.Gelu, bias=gvec[:, mc:mc + 1]),
                            reads=["fpsg", "fgv"], writes=["fh"])
            ctx.psum_flush()
            ps8 = [ctx.psum(f"fp8_{i}") for i in range(8)]
            for ob in range(4):
                for mcl in range(PASS_MC):
                    mc = hp_ * PASS_MC + mcl
                    bi = mcl % 2
                    qeng = "sync" if bi == 0 else "scalar"
                    db = dslb[bi]
                    kd = f"fdsl{bi}"
                    wsrc = wd[mc * 128:(mc + 1) * 128, ob * 512:(ob + 1) * 512]
                    if l0:
                        stg = dstg[bi]
                        kstg = f"fstg{bi}"
                        sb.dma(qeng, stg, wsrc, reads=[kwd], writes=[kstg])
                        sb.add("scalar", lambda e, stg=stg, db=db: e.copy(db, stg),
                               reads=[kstg], writes=[kd])
                    else:
                        sb.dma(qeng, db, wsrc, reads=[kwd], writes=[kd])
                    for ti in range(8):
                        sb.add("tensor", lambda e, mcl=mcl, ti=ti, db=db,
                               p8=ps8[ti]: e.matmul(
                            p8[:, 0:512],
                            hv[:, mcl, ti * 128:(ti + 1) * 128],
                            db, start=(mcl == 0), stop=(mcl == PASS_MC - 1)),
                            reads=["fh", kd], writes=[f"fp8_{ti}"])
                for ti in range(8):
                    if l0:
                        sb.add("vector", lambda e, ti=ti, ob=ob,
                               p8=ps8[ti]: e.tensor_tensor(
                            hg, p8[:, 0:512],
                            bcast[:, ob * 512:(ob + 1) * 512],
                            ALU.mult), reads=[f"fp8_{ti}", "fbc"],
                            writes=["fhg"])
                        sb.add("vector", lambda e, ti=ti, ob=ob: e.tensor_tensor(
                            x2v[:, ti, ob * 512:(ob + 1) * 512],
                            x2v[:, ti, ob * 512:(ob + 1) * 512], hg, ALU.add),
                            reads=["fhg", "fx2"], writes=["fx2"])
                    else:
                        sb.add("vector", lambda e, ti=ti, ob=ob,
                               p8=ps8[ti]: e.tensor_tensor(
                            x2v[:, ti, ob * 512:(ob + 1) * 512],
                            x2v[:, ti, ob * 512:(ob + 1) * 512],
                            p8[:, 0:512], ALU.add),
                            reads=[f"fp8_{ti}", "fx2"], writes=["fx2"])
            ctx.psum_flush()
            if hp_ < NPASS - 1:
                ps_g = ctx.psum("fps_g")
                if l0:
                    ps_u = ctx.psum("fps_u")

        for i in range(NT):
            xr = x2v[:, i, :]
            emit_ln(ctx, xr, lnscr, stats, "fx2")
            sb.dma("sync", xov[i], xr, reads=["fx2"], writes=[okey])



def _prefetch_consts(ctx):
    ctx.fconst(1e-5)
    ctx.fconst(EXP_SHIFT)
    ctx.fconst(SCALE_ATT)


def build_launch_a():
    ctx = Ctx("la")
    x_in = ctx.inp("x_own", [TOK, D])
    wq0 = ctx.inp("wq0", [D, D])
    wk0 = ctx.inp("wk0", [D, D])
    wv0 = ctx.inp("wv0", [D, D])
    w1ts = ctx.inp("w1ts", [256, DFF1])
    w2ts = ctx.inp("w2ts", [1024, D])
    cos_in = ctx.inp("cos_own", [128, TOK])
    sin_in = ctx.inp("sin_own", [128, TOK])
    psums_o = ctx.out("psums", [1, 16])
    qT_o = ctx.out("qT", [D, TOK], BF16)
    kT_o = ctx.out("kT", [D, TOK])
    v_o = ctx.out("v", [TOK, D])
    wqqt = ctx.dram("wqqt", [D, D])
    wkqt = ctx.dram("wkqt", [D, D])
    wvqt = ctx.dram("wvqt", [D, D])
    _prefetch_consts(ctx)
    ph_wprep1(ctx, w1ts, w2ts, psums_o, "pso")
    ph_wquant(ctx, [wq0, wk0, wv0], [wqqt, wkqt, wvqt], 16,
              ["wqk", "wkk", "wvk"])
    ph_qkv(ctx, 0, x_in, wqqt, wkqt, wvqt, cos_in, sin_in,
           qT_o, kT_o, v_o,
           {"wq": "wqk", "wk": "wkk", "wv": "wvk"},
           {"q": "qo", "k": "ko", "v": "vo"})
    run_block(ctx.nc, ctx.sb, "la")
    return ctx.nc


def build_launch_b():
    ctx = Ctx("lb")
    x_in = ctx.inp("x_own", [TOK, D])
    qT0 = ctx.inp("qT0", [D, TOK], BF16)
    kTf0 = ctx.inp("kTf0", [D, S])
    vf0 = ctx.inp("vf0", [S, D])
    masks = ctx.inp("masks", [2 * MASK_W, 128, GSZ], BF16)
    wo0 = ctx.inp("wo0", [D, D])
    gateT = ctx.inp("gateT", [D, INTER0])
    upT = ctx.inp("upT", [D, INTER0])
    downT = ctx.inp("downT", [INTER0, D])
    gsb = ctx.inp("gsb", [128, INTER0 // 128])
    usb = ctx.inp("usb", [128, INTER0 // 128])
    dsr = ctx.inp("dsr", [1, D])
    partials_all = ctx.inp("partials_all", [8, 16])
    w1ts = ctx.inp("w1ts", [256, DFF1])
    w2ts = ctx.inp("w2ts", [1024, D])
    wq1 = ctx.inp("wq1", [D, D])
    wk1 = ctx.inp("wk1", [D, 512])
    wv1 = ctx.inp("wv1", [D, 512])
    cos_in = ctx.inp("cos_own", [128, TOK])
    sin_in = ctx.inp("sin_own", [128, TOK])
    x2_o = ctx.out("x2", [TOK, D])
    qT1_o = ctx.out("qT1", [D, TOK], BF16)
    kT1_o = ctx.out("kT1", [512, TOK])
    v1_o = ctx.out("v1", [TOK, 512])
    w1q_o = ctx.out("w1q_s", [256, DFF1], BF16)
    w2q_o = ctx.out("w2q_s", [1024, D], BF16)
    woqt = ctx.dram("woqt", [D, D])
    x1b = ctx.dram("x1b", [TOK, D])
    _prefetch_consts(ctx)
    ph_wquant(ctx, [wo0], [woqt], 16, ["wok"])
    ph_attn(ctx, 0, qT0, kTf0, vf0, masks, x_in, woqt, x1b,
            {"q": "_q", "k": "_k", "v": "_v", "wo": "wok", "x": "_x",
             "masks": "_m"}, "x1k")
    ph_ffn(ctx, 0, x1b, gateT, upT, downT, (gsb, usb), dsr, x2_o,
           {"x": "x1k"}, "x2k")
    ph_tern(ctx, partials_all, w1ts, w2ts, w1q_o, w2q_o, "_pa",
            ["w1k", "w2k"])
    ph_qkv(ctx, 1, x2_o, wq1, wk1, wv1, cos_in, sin_in,
           qT1_o, kT1_o, v1_o, {"x": "x2k"},
           {"q": "q1o", "k": "k1o", "v": "v1o"})
    run_block(ctx.nc, ctx.sb, "lb")
    return ctx.nc


def build_launch_c():
    ctx = Ctx("lc")
    x2_in = ctx.inp("x2_own", [TOK, D])
    qT1 = ctx.inp("qT1", [D, TOK], BF16)
    kTf1 = ctx.inp("kTf1", [512, S])
    vf1 = ctx.inp("vf1", [S, 512])
    masks = ctx.inp("masks", [2 * MASK_W, 128, GSZ], BF16)
    wo1 = ctx.inp("wo1", [D, D])
    w1qT = ctx.inp("w1qT", [D, DFF1], BF16)
    w2qT = ctx.inp("w2qT", [DFF1, D], BF16)
    b1b = ctx.inp("b1b", [128, DFF1 // 128])
    b2r = ctx.inp("b2r", [1, D])
    xo = ctx.out("xo", [TOK, D])
    x3b = ctx.dram("x3b", [TOK, D])
    _prefetch_consts(ctx)
    ph_attn(ctx, 1, qT1, kTf1, vf1, masks, x2_in, wo1, x3b,
            {}, "x3k")
    ph_ffn(ctx, 1, x3b, w1qT, None, w2qT, b1b, b2r, xo,
           {"x": "x3k"}, "xok")
    run_block(ctx.nc, ctx.sb, "lc")
    return ctx.nc



_NC_CACHE = {}


def _run(name, builder, in_maps):
    if name not in _NC_CACHE:
        _NC_CACHE[name] = builder()
    res = run_bass_kernel_spmd(_NC_CACHE[name], in_maps, core_ids=list(range(NCORE)))
    return res.results


def _order_kv(rq, kd, kname, vname):
    f32 = np.float32
    kTfs = [np.zeros((kd, S), dtype=f32) for _ in range(B)]
    vfs = [np.zeros((S, kd), dtype=f32) for _ in range(B)]
    for c in range(NCORE):
        b = c // 4
        (r0, p0), (r1, p1) = core_token_rows(c)
        kT = np.asarray(rq[c][kname])
        v = np.asarray(rq[c][vname])
        for slot, p in ((0, p0), (1, p1)):
            kTfs[b][:, p * GSZ:(p + 1) * GSZ] = kT[:, slot * GSZ:(slot + 1) * GSZ]
            vfs[b][p * GSZ:(p + 1) * GSZ] = v[slot * GSZ:(slot + 1) * GSZ]
    return kTfs, vfs


def kernel_3launch(L0_attn_hidden_states, L0_attn_Wq, L0_attn_Wk, L0_attn_Wv, L0_attn_Wo,
           L0_ffn_gate_w, L0_ffn_up_w, L0_ffn_down_w, L0_ffn_gate_s, L0_ffn_up_s,
           L0_ffn_down_s, L1_attn_wq, L1_attn_wk, L1_attn_wv, L1_attn_wo,
           L1_ffn_w1, L1_ffn_b1, L1_ffn_w2, L1_ffn_b2):
    f32 = np.float32
    x = np.ascontiguousarray(np.asarray(L0_attn_hidden_states, f32).reshape(B * S, D))
    w1T = np.ascontiguousarray(np.asarray(L1_ffn_w1, f32).T)
    w2T = np.ascontiguousarray(np.asarray(L1_ffn_w2, f32).T)
    gateT = np.ascontiguousarray(np.asarray(L0_ffn_gate_w, f32).T)
    upT = np.ascontiguousarray(np.asarray(L0_ffn_up_w, f32).T)
    downT = np.ascontiguousarray(np.asarray(L0_ffn_down_w, f32).T)
    wq0 = np.ascontiguousarray(np.asarray(L0_attn_Wq, f32))
    wk0 = np.ascontiguousarray(np.asarray(L0_attn_Wk, f32))
    wv0 = np.ascontiguousarray(np.asarray(L0_attn_Wv, f32))
    wo0 = np.ascontiguousarray(np.asarray(L0_attn_Wo, f32))
    cos2, sin2 = host_rope_tables()

    def own_rows(c, arr2d):
        (r0, _), (r1, _) = core_token_rows(c)
        return np.ascontiguousarray(
            np.concatenate([arr2d[r0:r0 + GSZ], arr2d[r1:r1 + GSZ]], 0))

    def rope_own(c):
        (_, p0), (_, p1) = core_token_rows(c)
        cosn = np.concatenate([cos2[:, p0 * GSZ:(p0 + 1) * GSZ],
                               cos2[:, p1 * GSZ:(p1 + 1) * GSZ]], 1)
        sinn = np.concatenate([sin2[:, p0 * GSZ:(p0 + 1) * GSZ],
                               sin2[:, p1 * GSZ:(p1 + 1) * GSZ]], 1)
        return np.ascontiguousarray(cosn), np.ascontiguousarray(sinn)

    def scatter_rows(res, key):
        full = np.zeros((B * S, D), dtype=f32)
        for c in range(NCORE):
            (r0, _), (r1, _) = core_token_rows(c)
            full[r0:r0 + GSZ] = np.asarray(res[c][key])[:GSZ]
            full[r1:r1 + GSZ] = np.asarray(res[c][key])[GSZ:]
        return full

    ima = []
    for c in range(NCORE):
        cosn, sinn = rope_own(c)
        ima.append({"x_own": own_rows(c, x), "wq0": wq0, "wk0": wk0, "wv0": wv0,
                    "w1ts": np.ascontiguousarray(w1T[c * 256:(c + 1) * 256]),
                    "w2ts": np.ascontiguousarray(w2T[c * 1024:(c + 1) * 1024]),
                    "cos_own": cosn, "sin_own": sinn})
    ra = _run("la", build_launch_a, ima)
    partials_all = np.ascontiguousarray(
        np.concatenate([np.asarray(ra[c]["psums"]) for c in range(NCORE)], 0))
    kTfs0, vfs0 = _order_kv(ra, D, "kT", "v")

    imb = []
    for c in range(NCORE):
        b = c // 4
        cosn, sinn = rope_own(c)
        imb.append({"x_own": ima[c]["x_own"], "qT0": np.asarray(ra[c]["qT"]),
                    "kTf0": kTfs0[b], "vf0": vfs0[b], "masks": host_masks(c),
                    "wo0": wo0, "gateT": gateT, "upT": upT, "downT": downT,
                    "gsb": host_band_vec(L0_ffn_gate_s),
                    "usb": host_band_vec(L0_ffn_up_s),
                    "dsr": np.ascontiguousarray(np.asarray(L0_ffn_down_s, f32)[None, :]),
                    "partials_all": partials_all,
                    "w1ts": ima[c]["w1ts"], "w2ts": ima[c]["w2ts"],
                    "wq1": np.ascontiguousarray(np.asarray(L1_attn_wq, f32)),
                    "wk1": np.ascontiguousarray(np.asarray(L1_attn_wk, f32)),
                    "wv1": np.ascontiguousarray(np.asarray(L1_attn_wv, f32)),
                    "cos_own": cosn, "sin_own": sinn})
    rb = _run("lb", build_launch_b, imb)
    kTfs1, vfs1 = _order_kv(rb, 512, "kT1", "v1")
    w1qT = np.ascontiguousarray(np.concatenate(
        [np.asarray(rb[c]["w1q_s"]) for c in range(NCORE)], 0))
    w2qT = np.ascontiguousarray(np.concatenate(
        [np.asarray(rb[c]["w2q_s"]) for c in range(NCORE)], 0))

    imc = []
    for c in range(NCORE):
        b = c // 4
        imc.append({"x2_own": np.ascontiguousarray(np.asarray(rb[c]["x2"])),
                    "qT1": np.asarray(rb[c]["qT1"]),
                    "kTf1": kTfs1[b], "vf1": vfs1[b], "masks": host_masks(c),
                    "wo1": np.ascontiguousarray(np.asarray(L1_attn_wo, f32)),
                    "w1qT": w1qT, "w2qT": w2qT,
                    "b1b": host_band_vec(L1_ffn_b1),
                    "b2r": np.ascontiguousarray(np.asarray(L1_ffn_b2, f32)[None, :])})
    rc = _run("lc", build_launch_c, imc)
    x4 = scatter_rows(rc, "xo")
    return x4.reshape(B, S, D)



GRP8 = [[0, 1, 2, 3, 4, 5, 6, 7]]
GRP4 = [[0, 1, 2, 3], [4, 5, 6, 7]]
PAY2KV = 4096
PAYT = 2048
PAY3KV = 1024


def build_fused():
    ctx = Ctx("fz")
    sb = ctx.sb
    nc = ctx.nc
    x_in = ctx.inp("x_own", [TOK, D])
    wq_s = ctx.inp("wq_s", [256, D])
    wk_s = ctx.inp("wk_s", [256, D])
    wv_s = ctx.inp("wv_s", [256, D])
    wo_s = ctx.inp("wo_s", [256, D])
    w1ts = ctx.inp("w1ts", [256, DFF1])
    w2ts = ctx.inp("w2ts", [1024, D])
    gateT = ctx.inp("gateT", [D, INTER0])
    upT = ctx.inp("upT", [D, INTER0])
    downT = ctx.inp("downT", [INTER0, D])
    gsb = ctx.inp("gsb", [128, INTER0 // 128])
    usb = ctx.inp("usb", [128, INTER0 // 128])
    dsr = ctx.inp("dsr", [1, D])
    wq1 = ctx.inp("wq1", [D, D])
    wk1 = ctx.inp("wk1", [D, 512])
    wv1 = ctx.inp("wv1", [D, 512])
    wo1 = ctx.inp("wo1", [D, D])
    b1b = ctx.inp("b1b", [128, DFF1 // 128])
    b2r = ctx.inp("b2r", [1, D])
    masks = ctx.inp("masks", [2 * MASK_W, 128, GSZ], BF16)
    cos_in = ctx.inp("cos_own", [128, TOK])
    sin_in = ctx.inp("sin_own", [128, TOK])
    xo = ctx.out("xo", [TOK, D])

    pay1_in = ctx.dram("pay1_in", [PAY1_ROWS, PAY1W])
    pay1_out = ctx.dram("pay1_out", [8 * PAY1_ROWS, PAY1W])
    pay2_in = ctx.dram("pay2_in", [PAY2KV, PAY2W])
    pay2_out = ctx.dram("pay2_out", [4 * PAY2KV, PAY2W])
    payt_in = ctx.dram("payt_in", [PAYT, PAY2W])
    payt_out = ctx.dram("payt_out", [8 * PAYT, PAY2W])
    pay3_in = ctx.dram("pay3_in", [PAY3KV, PAY3W])
    pay3_out = ctx.dram("pay3_out", [4 * PAY3KV, PAY3W])
    qT0b = ctx.dram("qT0b", [D, TOK], BF16)
    qT1b = ctx.dram("qT1b", [D, TOK], BF16)
    kTf0 = ctx.dram("kTf0", [D, S])
    vf0 = ctx.dram("vf0", [S, D])
    kTf1 = ctx.dram("kTf1", [512, S])
    vf1 = ctx.dram("vf1", [S, 512])
    w1qTb = ctx.dram("w1qTb", [D, DFF1 // 2])
    w2qTb = ctx.dram("w2qTb", [DFF1, D // 2])
    x1b = ctx.dram("x1b", [TOK, D])
    x2b = ctx.dram("x2b", [TOK, D])
    x3b = ctx.dram("x3b", [TOK, D])

    _prefetch_consts(ctx)

    ph_wprep1(ctx, w1ts, w2ts, pay1_in[4 * D:4 * D + 1, :], "p1k")
    ph_wquant(ctx, [wq_s, wk_s, wv_s, wo_s],
              [pay1_in[0:D, :], pay1_in[D:2 * D, :], pay1_in[2 * D:3 * D, :],
               pay1_in[3 * D:4 * D, :]], 2, ["p1k", "p1k", "p1k", "p1k"])
    sb.cc(lambda e: e.collective_compute(
        "AllGather", ALU.bypass, GRP8,
        [pay1_in.opt()], [pay1_out.opt()]), reads=["p1k"], writes=["p1o"])

    p1v3 = pay1_out.rearrange("(r n) w -> r n w", n=PAY1_ROWS)
    wq_full = ("gath", p1v3, 0, D)
    wk_full = ("gath", p1v3, D, D)
    wv_full = ("gath", p1v3, 2 * D, D)
    wo_full = ("gath", p1v3, 3 * D, D)
    partials_all = p1v3[:, 4 * D, 0:16]

    w1q_view = payt_in[0:1024, :].bitcast(BF16).rearrange(
        "(a b) c -> a (b c)", b=4)
    w2q_view = payt_in[1024:2048, :].bitcast(BF16)
    ph_tern(ctx, partials_all, w1ts, w2ts, w1q_view, w2q_view, "p1o",
            ["ptk", "ptk"])
    sb.cc(lambda e: e.collective_compute(
        "AllGather", ALU.bypass, GRP8,
        [payt_in.opt()], [payt_out.opt()]), reads=["ptk"], writes=["pto"])
    with ctx.phase("fixw"):
        for r in range(8):
            sb.add("gpsimd", lambda e, r=r: e.dma_start(
                out=w1qTb[r * 256:(r + 1) * 256, :],
                in_=payt_out[r * PAYT: r * PAYT + 1024, :]
                .rearrange("(a b) c -> a (b c)", b=4)),
                reads=["pto"], writes=["w1f"], q="gd")
            sb.add("gpsimd", lambda e, r=r: e.dma_start(
                out=w2qTb[r * 1024:(r + 1) * 1024, :],
                in_=payt_out[r * PAYT + 1024: (r + 1) * PAYT, :]),
                reads=["pto"], writes=["w2f"], q="gd")

    ph_qkv(ctx, 0, x_in, wq_full, wk_full, wv_full, cos_in, sin_in,
           qT0b, pay2_in[0:2048, :],
           pay2_in[2048:4096, :].rearrange("(a b) c -> a (b c)", b=2),
           {"wq": "p1o", "wk": "p1o", "wv": "p1o"},
           {"q": "q0k", "k": "p2k", "v": "p2k"})
    sb.cc(lambda e: e.collective_compute(
        "AllGather", ALU.bypass, GRP4,
        [pay2_in.opt()], [pay2_out.opt()]), reads=["p2k"], writes=["p2o"])
    with ctx.phase("fix0"):
        for r4 in range(4):
            for slot in range(2):
                p = r4 if slot == 0 else 7 - r4
                sb.add("gpsimd", lambda e, r4=r4, slot=slot, p=p: e.dma_start(
                    out=kTf0[:, p * GSZ:(p + 1) * GSZ],
                    in_=pay2_out[r4 * PAY2KV: r4 * PAY2KV + 2048,
                                 slot * GSZ:(slot + 1) * GSZ]),
                    reads=["p2o"], writes=["ktf0"], q="gd")
                sb.add("gpsimd", lambda e, r4=r4, slot=slot, p=p: e.dma_start(
                    out=vf0[p * GSZ:(p + 1) * GSZ, :]
                    .rearrange("(a b) c -> a (b c)", b=2),
                    in_=pay2_out[r4 * PAY2KV + 2048 + slot * 1024:
                                 r4 * PAY2KV + 2048 + (slot + 1) * 1024, :]),
                    reads=["p2o"], writes=["vf0"], q="gd")

    ph_attn(ctx, 0, qT0b, kTf0, vf0, masks, x_in, wo_full, x1b,
            {"q": "q0k", "k": "ktf0", "v": "vf0", "wo": "p1o",
             "x": "_x", "masks": "_m"}, "x1k")
    ph_ffn(ctx, 0, x1b, gateT, upT, downT, (gsb, usb), dsr, x2b,
           {"x": "x1k"}, "x2k")
    ph_qkv(ctx, 1, x2b, wq1, wk1, wv1, cos_in, sin_in,
           qT1b, pay3_in[0:512, :],
           pay3_in[512:1024, :].rearrange("a (b c) -> (a b) c", b=2),
           {"x": "x2k"}, {"q": "q1k", "k": "p3k", "v": "p3k"})
    sb.cc(lambda e: e.collective_compute(
        "AllGather", ALU.bypass, GRP4,
        [pay3_in.opt()], [pay3_out.opt()]), reads=["p3k"], writes=["p3o"])
    with ctx.phase("fix1"):
        for r4 in range(4):
            for slot in range(2):
                p = r4 if slot == 0 else 7 - r4
                sb.add("gpsimd", lambda e, r4=r4, slot=slot, p=p: e.dma_start(
                    out=kTf1[:, p * GSZ:(p + 1) * GSZ],
                    in_=pay3_out[r4 * PAY3KV: r4 * PAY3KV + 512,
                                 slot * GSZ:(slot + 1) * GSZ]),
                    reads=["p3o"], writes=["ktf1"], q="gd")
                sb.add("gpsimd", lambda e, r4=r4, slot=slot, p=p: e.dma_start(
                    out=vf1[p * GSZ:(p + 1) * GSZ, :],
                    in_=pay3_out[r4 * PAY3KV + 512 + slot * 256:
                                 r4 * PAY3KV + 512 + (slot + 1) * 256, :]
                    .rearrange("a (b c) -> (a b) c", b=2)),
                    reads=["p3o"], writes=["vf1"], q="gd")

    ph_attn(ctx, 1, qT1b, kTf1, vf1, masks, x2b, wo1, x3b,
            {"q": "q1k", "k": "ktf1", "v": "vf1", "x": "x2k"}, "x3k")
    ph_ffn(ctx, 1, x3b, w1qTb.bitcast(BF16), None, w2qTb.bitcast(BF16),
           b1b, b2r, xo, {"x": "x3k", "wg": "w1f", "wd": "w2f"}, "xok")
    run_block(nc, sb, "fz")
    return nc


def kernel_fused_inputs(ins):
    f32 = np.float32
    x = np.ascontiguousarray(np.asarray(ins["L0_attn_hidden_states"], f32)
                             .reshape(B * S, D))
    w1T = np.ascontiguousarray(np.asarray(ins["L1_ffn_w1"], f32).T)
    w2T = np.ascontiguousarray(np.asarray(ins["L1_ffn_w2"], f32).T)
    cos2, sin2 = host_rope_tables()
    ims = []
    for c in range(NCORE):
        (r0, _), (r1, _) = core_token_rows(c)
        (_, p0), (_, p1) = core_token_rows(c)
        cosn = np.ascontiguousarray(np.concatenate(
            [cos2[:, p0 * GSZ:(p0 + 1) * GSZ], cos2[:, p1 * GSZ:(p1 + 1) * GSZ]], 1))
        sinn = np.ascontiguousarray(np.concatenate(
            [sin2[:, p0 * GSZ:(p0 + 1) * GSZ], sin2[:, p1 * GSZ:(p1 + 1) * GSZ]], 1))
        ims.append({
            "x_own": np.ascontiguousarray(
                np.concatenate([x[r0:r0 + GSZ], x[r1:r1 + GSZ]], 0)),
            "wq_s": np.ascontiguousarray(np.asarray(ins["L0_attn_Wq"], f32)[c * 256:(c + 1) * 256]),
            "wk_s": np.ascontiguousarray(np.asarray(ins["L0_attn_Wk"], f32)[c * 256:(c + 1) * 256]),
            "wv_s": np.ascontiguousarray(np.asarray(ins["L0_attn_Wv"], f32)[c * 256:(c + 1) * 256]),
            "wo_s": np.ascontiguousarray(np.asarray(ins["L0_attn_Wo"], f32)[c * 256:(c + 1) * 256]),
            "w1ts": np.ascontiguousarray(w1T[c * 256:(c + 1) * 256]),
            "w2ts": np.ascontiguousarray(w2T[c * 1024:(c + 1) * 1024]),
            "gateT": np.ascontiguousarray(np.asarray(ins["L0_ffn_gate_w"], f32).T),
            "upT": np.ascontiguousarray(np.asarray(ins["L0_ffn_up_w"], f32).T),
            "downT": np.ascontiguousarray(np.asarray(ins["L0_ffn_down_w"], f32).T),
            "gsb": host_band_vec(ins["L0_ffn_gate_s"]),
            "usb": host_band_vec(ins["L0_ffn_up_s"]),
            "dsr": np.ascontiguousarray(np.asarray(ins["L0_ffn_down_s"], f32)[None, :]),
            "wq1": np.ascontiguousarray(np.asarray(ins["L1_attn_wq"], f32)),
            "wk1": np.ascontiguousarray(np.asarray(ins["L1_attn_wk"], f32)),
            "wv1": np.ascontiguousarray(np.asarray(ins["L1_attn_wv"], f32)),
            "wo1": np.ascontiguousarray(np.asarray(ins["L1_attn_wo"], f32)),
            "b1b": host_band_vec(ins["L1_ffn_b1"]),
            "b2r": np.ascontiguousarray(np.asarray(ins["L1_ffn_b2"], f32)[None, :]),
            "masks": host_masks(c),
            "cos_own": cosn, "sin_own": sinn,
        })
    return ims


def kernel_fused(**ins):
    ims = kernel_fused_inputs(ins)
    if "fz" not in _NC_CACHE:
        _NC_CACHE["fz"] = build_fused()
    res = run_bass_kernel_spmd(_NC_CACHE["fz"], ims, core_ids=list(range(NCORE)))
    f32 = np.float32
    x4 = np.zeros((B * S, D), dtype=f32)
    for c in range(NCORE):
        (r0, _), (r1, _) = core_token_rows(c)
        x4[r0:r0 + GSZ] = np.asarray(res.results[c]["xo"])[:GSZ]
        x4[r1:r1 + GSZ] = np.asarray(res.results[c]["xo"])[GSZ:]
    return x4.reshape(B, S, D)


def kernel(**ins):
    return kernel_3launch(**ins)


# version 06 (final)
# speedup vs baseline: 1.2883x; 1.1721x over previous
import contextlib
import numpy as np
import ml_dtypes
import concourse.bass as bass
import concourse.mybir as mybir
from concourse.bass_utils import run_bass_kernel_spmd

F32 = mybir.dt.float32
BF16 = mybir.dt.bfloat16
I32 = mybir.dt.int32
AF = mybir.ActivationFunctionType
ALU = mybir.AluOpType
AX = mybir.AxisListType
BF = ml_dtypes.bfloat16

B = 2
S = 4096
D = 2048
NCORE = 8
TOK = 1024
NT = TOK // 128
GSZ = 512
NH = 32
HD = 64
INTER0 = 5632
DFF1 = 8192
KT_PAD = (16, 32)
MASK_W = 16
MASK_START = (0, 16)
EXP_SHIFT = -20.0
SCALE_ATT = 0.125
NIC = D // 128

PAY1W = 256
PAY1_ROWS = 4 * D + 1
PAY2W = 1024
PAY2_ROWS = 2048 + 2048 + 1024 + 1024
PAY3W = 1024
PAY3_ROWS = 512 + 512


def core_positions(c):
    cc = c % 4
    return (cc, 7 - cc)


def core_token_rows(c):
    b = c // 4
    p0, p1 = core_positions(c)
    return [(b * S + p0 * GSZ, p0), (b * S + p1 * GSZ, p1)]



_ENG_Q = {"tensor": ("t", 1), "vector": ("v", 1), "scalar": ("s", 1),
          "gpsimd": ("g", 1)}
_DMA_Q = {"sync": "sd", "gpsimd": "gd", "scalar": "sd2"}
_SELF_GATED = ("sd", "gd", "sd2", "cc")
_ALL_Q = ("t", "v", "s", "g", "sd", "gd", "sd2", "cc")


class SB:
    def __init__(self):
        self.items = []
        self.cum = {q: 0 for q in _ALL_Q}
        self.w = {}
        self.r = {}
        self.floor = {q: 0 for q in _ALL_Q}

    def _dep(self, waits, q, val):
        waits[q] = max(waits.get(q, 0), val)

    def _add(self, engine, q, fn, reads, writes, inc):
        waits = {}
        for q2, v2 in self.floor.items():
            if v2 > 0:
                waits[q2] = v2
        if q in _SELF_GATED and self.cum[q] > waits.get(q, 0):
            waits[q] = self.cum[q]
        for k in reads:
            if k in self.w:
                self._dep(waits, *self.w[k])
        for k in writes:
            if k in self.w:
                self._dep(waits, *self.w[k])
            for rq, rv in self.r.get(k, {}).items():
                self._dep(waits, rq, rv)
        after = self.cum[q] + inc
        self.cum[q] = after
        for k in reads:
            d = self.r.setdefault(k, {})
            d[q] = max(d.get(q, 0), after)
        for k in writes:
            self.w[k] = (q, after)
            self.r[k] = {}
        self.items.append((engine, fn, waits, q, inc))

    def add(self, engine, fn, reads=(), writes=(), q=None, inc=None):
        if q is None:
            q, qinc = _ENG_Q[engine]
        else:
            qinc = 16 if q in _SELF_GATED else 1
        self._add(engine, q, fn, reads, writes, qinc if inc is None else inc)

    def dma(self, engine, out, in_, reads=(), writes=(), **kw):
        q = _DMA_Q[engine]
        self._add(engine, q, lambda e: e.dma_start(out=out, in_=in_, **kw),
                  reads, writes, 16)

    def cc(self, fn, reads=(), writes=()):
        self._add("gpsimd", "cc", fn, reads, writes, 16)

    def barrier(self):
        for q in _ALL_Q:
            self.floor[q] = self.cum[q]


def run_block(nc, sbd, name="main"):
    with contextlib.ExitStack() as st:
        sems = {q: st.enter_context(nc.semaphore(f"q_{q}_{name}"))
                for q in _ALL_Q}
        block = st.enter_context(nc.Block(name))
        totals = dict(sbd.cum)

        def mk(engname):
            def _(e):
                done = {}
                own = {}
                for (en, fn, waits, q, inc) in sbd.items:
                    if en != engname:
                        continue
                    for wq, wv in waits.items():
                        if wv <= 0 or done.get(wq, 0) >= wv:
                            continue
                        e.wait_ge(sems[wq], wv)
                        done[wq] = wv
                    fn(e).then_inc(sems[q], inc)
                    own[q] = own.get(q, 0) + inc
                for q2 in _ALL_Q:
                    if totals[q2] > 0 and done.get(q2, 0) < totals[q2]:
                        e.wait_ge(sems[q2], totals[q2])
            return _

        block.sync(mk("sync"))
        block.gpsimd(mk("gpsimd"))
        block.tensor(mk("tensor"))
        block.vector(mk("vector"))
        block.scalar(mk("scalar"))



def host_rope_tables():
    j = np.arange(HD, dtype=np.float64)
    inv = 1.0 / (10000.0 ** ((2.0 * (j % 32)) / HD))
    pos = np.arange(S, dtype=np.float64)
    th = pos[None, :] * inv[:, None]
    cos2 = np.concatenate([np.cos(th), np.cos(th)], 0).astype(np.float32)
    sin2 = np.concatenate([np.sin(th), np.sin(th)], 0).astype(np.float32)
    return cos2, sin2


def host_psgn():
    P = np.zeros((128, 128), dtype=np.float32)
    for blk in range(2):
        o = blk * 64
        for j in range(32):
            P[o + j + 32, o + j] = -1.0
            P[o + j, o + j + 32] = 1.0
    return P.astype(BF)


def host_masks(c):
    p0, p1 = core_positions(c)
    out = np.zeros((2 * MASK_W, 128, GSZ), dtype=np.float32)
    for slot, p in ((0, p0), (1, p1)):
        for w in range(MASK_W):
            kt = MASK_START[slot] + w
            m = out[slot * MASK_W + w]
            for s4 in range(4):
                diag = 4 * p + s4
                if kt < diag:
                    m[:, s4 * 128:(s4 + 1) * 128] = 1.0
                elif kt == diag:
                    kk = np.arange(128)[:, None]
                    qq = np.arange(128)[None, :]
                    m[:, s4 * 128:(s4 + 1) * 128] = (kk <= qq).astype(np.float32)
    return np.ascontiguousarray(out.astype(BF))


def host_band_vec(v, width=128):
    v = np.asarray(v, dtype=np.float32)
    n = v.shape[0]
    m = -(-n // width)
    out = np.zeros((width, m), dtype=np.float32)
    for j in range(m):
        seg = v[j * width:(j + 1) * width]
        out[:seg.shape[0], j] = seg
    return np.ascontiguousarray(out)



class Ctx:
    def __init__(self, name):
        self.nc = bass.Bass("TRN2", target_bir_lowering=False, name=name)
        self.sb = SB()
        self._fc = {}
        self._stack = None
        self._uid = 0

    def uid(self):
        self._uid += 1
        return self._uid

    def sbuf_p(self, name, shape, dtype=F32):
        return self.nc.alloc_sbuf_tensor(name, shape, dtype).ap()

    @contextlib.contextmanager
    def phase(self, name):
        assert self._stack is None, "phases cannot nest"
        self._stack = contextlib.ExitStack()
        self._pstack = contextlib.ExitStack()
        try:
            yield self
        finally:
            self.sb.barrier()
            self._pstack.close()
            self._pstack = None
            self._stack.close()
            self._stack = None

    def psum_flush(self):
        self.sb.barrier()
        self._pstack.close()
        self._pstack = contextlib.ExitStack()

    def sbuf(self, name, shape, dtype=F32):
        h = self._stack.enter_context(
            self.nc.sbuf_tensor(f"{name}_{self.uid()}", shape, dtype))
        return h.ap()

    def psum(self, name, dtype=F32):
        shape = [128, 512] if dtype == F32 else [128, 1024]
        h = self._pstack.enter_context(
            self.nc.psum_tensor(f"{name}_{self.uid()}", shape, dtype))
        return h.ap()

    def inp(self, name, shape, dtype=F32):
        return self.nc.dram_tensor(name, shape, dtype, kind="ExternalInput").ap()

    def out(self, name, shape, dtype=F32):
        return self.nc.dram_tensor(name, shape, dtype, kind="ExternalOutput").ap()

    def dram(self, name, shape, dtype=F32):
        return self.nc.dram_tensor(name, shape, dtype, kind="Internal").ap()

    def const(self, name, arr):
        return self.nc.inline_tensor(np.ascontiguousarray(arr),
                                     f"{name}_{self.uid()}").ap()

    def fconst(self, val):
        if val not in self._fc:
            i = len(self._fc)
            dram = self.const(f"fc{i}", np.full((128, 1), val, dtype=np.float32))
            tile = self.sbuf_p(f"fct{i}", [128, 1])
            key = f"fc{i}"
            self.sb.dma("sync", tile, dram, writes=[key])
            self._fc[val] = (tile, key)
        return self._fc[val]


def alloc_quant_tmp(ctx):
    return {
        "B": ctx.sbuf("q_B", [128, 2048], F32),
        "C": ctx.sbuf("q_C", [128, 2048], F32),
        "D": ctx.sbuf("q_D", [128, 2048], F32),
        "E": ctx.sbuf("q_E", [128, 2048], F32),
        "am": ctx.sbuf("q_am", [128, 128], F32),
        "sc": ctx.sbuf("q_sc", [128, 128], F32),
        "rin": ctx.sbuf("q_rin", [128, 128], F32),
    }


def emit_fp4_quant(ctx, src, dst, tmp, src_key, dst_key):
    sb = ctx.sb
    F = src.shape[-1]
    nb = F // 16
    src3 = src.rearrange("p (a b) -> p a b", b=16)
    am, sc, rin = tmp["am"][:, :nb], tmp["sc"][:, :nb], tmp["rin"][:, :nb]
    Bb, C = tmp["B"][:, :F], tmp["C"][:, :F]
    Dd, E = tmp["D"][:, :F], tmp["E"][:, :F]
    B3 = Bb.rearrange("p (a b) -> p a b", b=16)
    C3 = C.rearrange("p (a b) -> p a b", b=16)
    sb.add("vector", lambda e: e.tensor_reduce(
        am, src3, AX.X, ALU.max, apply_absolute_value=True),
        reads=[src_key], writes=["qam"])
    sb.add("vector", lambda e: e.tensor_scalar(am, am, 1e-30, None, ALU.max),
        reads=["qam"], writes=["qam"])
    sb.add("vector", lambda e: e.tensor_scalar(sc, am, 0.16666667, None, ALU.mult),
        reads=["qam"], writes=["qsc"])
    sb.add("vector", lambda e: e.reciprocal(rin, sc),
        reads=["qsc"], writes=["qrin"])
    sb.add("scalar", lambda e: e.activation(C, src, AF.Abs),
        reads=[src_key], writes=["qC"])
    rb = rin.unsqueeze(2).broadcast_to([128, nb, 16])
    sb.add("vector", lambda e: e.tensor_tensor(B3, C3, rb, ALU.mult),
        reads=["qC", "qrin"], writes=["qB"])
    sb.add("vector", lambda e: e.tensor_scalar(C, Bb, 2.0, None, ALU.is_ge),
        reads=["qB"], writes=["qC"])
    sb.add("vector", lambda e: e.tensor_scalar(Dd, C, -1.0, 2.0, ALU.mult, ALU.add),
        reads=["qC"], writes=["qD"])
    sb.add("vector", lambda e: e.tensor_scalar(E, C, 0.5, 0.5, ALU.mult, ALU.add),
        reads=["qC"], writes=["qE"])
    sb.add("vector", lambda e: e.tensor_scalar(C, Bb, 4.0, None, ALU.is_ge),
        reads=["qB"], writes=["qC"])
    sb.add("vector", lambda e: e.scalar_tensor_tensor(
        Dd, C, -0.5, Dd, ALU.mult, ALU.add), reads=["qC", "qD"], writes=["qD"])
    sb.add("vector", lambda e: e.scalar_tensor_tensor(
        E, C, 1.0, E, ALU.mult, ALU.add), reads=["qC", "qE"], writes=["qE"])
    sb.add("vector", lambda e: e.tensor_tensor(Bb, Bb, Dd, ALU.mult),
        reads=["qB", "qD"], writes=["qB"])
    sb.add("vector", lambda e: e.tensor_scalar(Dd, Bb, 0.5, None, ALU.is_gt),
        reads=["qB"], writes=["qD"])
    for bnd in (1.5, 2.5, 3.5):
        sb.add("vector", lambda e, bnd=bnd: e.scalar_tensor_tensor(
            Dd, Bb, bnd, Dd, ALU.is_gt, ALU.add), reads=["qB", "qD"], writes=["qD"])
    sb.add("vector", lambda e: e.tensor_tensor(Bb, Dd, E, ALU.mult),
        reads=["qD", "qE"], writes=["qB"])
    sb.add("scalar", lambda e: e.activation(C, src, AF.Sign),
        reads=[src_key], writes=["qC"])
    scb = sc.unsqueeze(2).broadcast_to([128, nb, 16])
    sb.add("vector", lambda e: e.tensor_tensor(C3, C3, scb, ALU.mult),
        reads=["qC", "qsc"], writes=["qC"])
    sb.add("vector", lambda e: e.tensor_tensor(dst, Bb, C, ALU.mult),
        reads=["qB", "qC"], writes=[dst_key])


def emit_ln(ctx, xr, scratch, stats, key):
    sb = ctx.sb
    s_sum, s_ssq = stats[:, 0:1], stats[:, 1:2]
    s_mean, s_var = stats[:, 2:3], stats[:, 3:4]
    s_sd, s_rstd, s_m2 = stats[:, 4:5], stats[:, 5:6], stats[:, 6:7]
    sb.add("vector", lambda e: e.reduce_sum(s_sum, xr, AX.X),
        reads=[key], writes=["lnst"])
    sb.add("scalar", lambda e: e.activation(scratch, xr, AF.Square, accum_out=s_ssq),
        reads=[key], writes=["lnscr", "lnst"])
    sb.add("vector", lambda e: e.tensor_scalar(s_mean, s_sum, 1.0 / D, None, ALU.mult),
        reads=["lnst"], writes=["lnst"])
    sb.add("vector", lambda e: e.tensor_tensor(s_m2, s_mean, s_mean, ALU.mult),
        reads=["lnst"], writes=["lnst"])
    sb.add("vector", lambda e: e.tensor_scalar(s_var, s_ssq, 1.0 / D, None, ALU.mult),
        reads=["lnst"], writes=["lnst"])
    sb.add("vector", lambda e: e.tensor_tensor(s_var, s_var, s_m2, ALU.subtract),
        reads=["lnst"], writes=["lnst"])
    epsap, epskey = ctx.fconst(1e-5)
    sb.add("scalar", lambda e: e.activation(s_sd, s_var, AF.Sqrt, bias=epsap),
        reads=["lnst", epskey], writes=["lnst"])
    sb.add("vector", lambda e: e.reciprocal(s_rstd, s_sd),
        reads=["lnst"], writes=["lnst"])
    sb.add("vector", lambda e: e.tensor_scalar(
        xr, xr, s_mean, s_rstd, ALU.subtract, ALU.mult),
        reads=[key, "lnst"], writes=[key])



def ph_wprep1(ctx, w1ts, w2ts, partials_out, pkey):
    sb = ctx.sb
    with ctx.phase("wprep1"):
        onescol = ctx.const("onescol", np.ones((128, 1), dtype=np.float32))
        tile = ctx.sbuf("w1tile", [128, 8192])
        ab = ctx.sbuf("w1ab", [128, 8192])
        red = ctx.sbuf("w1red", [128, 16])
        red2 = ctx.sbuf("w1red2", [128, 2])
        onest = ctx.sbuf("w1ones", [128, 1])
        ps = ctx.psum("w1ps")
        sb.dma("sync", onest, onescol, writes=["on"])
        w1v = w1ts.rearrange("(n p) d -> n p d", p=128)
        w2v = w2ts.rearrange("(n p) d -> n p d", p=128)
        col = 0
        for (v, n, width) in ((w1v, 2, 8192), (w2v, 8, 2048)):
            for i in range(n):
                sb.dma("sync", tile[:, :width], v[i], writes=["w1t"])
                sb.add("scalar", lambda e, w=width: e.activation(
                    ab[:, :w], tile[:, :w], AF.Abs), reads=["w1t"], writes=["w1a"])
                sb.add("vector", lambda e, c=col, w=width: e.tensor_reduce(
                    red[:, c:c + 1], ab[:, :w], AX.X, ALU.add),
                    reads=["w1a"], writes=["w1r"])
                col += 1
        sb.add("vector", lambda e: e.tensor_reduce(
            red2[:, 0:1], red[:, 0:2], AX.X, ALU.add), reads=["w1r"], writes=["w1r2"])
        sb.add("vector", lambda e: e.tensor_reduce(
            red2[:, 1:2], red[:, 2:10], AX.X, ALU.add), reads=["w1r"], writes=["w1r2"])
        sb.add("tensor", lambda e: e.matmul(ps[0:1, 0:2], onest, red2,
                                            start=True, stop=True),
               reads=["w1r2", "on"], writes=["w1ps"])
        pw = partials_out.shape[1]
        outt = ctx.sbuf("w1outt", [1, 256])
        sb.add("vector", lambda e: e.memset(outt, 0.0), writes=["w1o"])
        sb.add("vector", lambda e: e.tensor_copy(outt[:, 0:2], ps[0:1, 0:2]),
               reads=["w1ps"], writes=["w1o"])
        sb.dma("sync", partials_out, outt[:, 0:pw], reads=["w1o"], writes=[pkey])


def ph_wquant(ctx, srcs, qt_outs, ntiles, okeys):
    sb = ctx.sb
    with ctx.phase("wquant"):
        ident = ctx.const("identf", np.eye(128, dtype=np.float32))
        identt = ctx.sbuf("widentt", [128, 128], F32)
        sb.dma("sync", identt, ident, writes=["wid"])
        tmp = alloc_quant_tmp(ctx)
        wt = ctx.sbuf("wqt_in", [128, 2048])
        qw = ctx.sbuf("wqt_q", [128, 2048], F32)
        oqt = ctx.sbuf("wqt_o", [128, 16 * 128], F32)
        pst = ctx.psum("wqt_ps")
        for (src, qt_out, okey) in zip(srcs, qt_outs, okeys):
            wv = src.rearrange("(t p) d -> t p d", p=128)
            ov = qt_out.rearrange("(a p) o -> p a o", p=128)
            for t in range(ntiles):
                sb.dma("sync", wt, wv[t], writes=["wqw"])
                emit_fp4_quant(ctx, wt, qw, tmp, "wqw", "wqq")
                for ic in range(16):
                    sb.add("tensor", lambda e, ic=ic: e.transpose(
                        pst[:, 0:128], qw[:, ic * 128:(ic + 1) * 128], identt),
                        reads=["wqq", "wid"], writes=["wqps"])
                    sb.add("scalar", lambda e, ic=ic: e.copy(
                        oqt[:, ic * 128:(ic + 1) * 128], pst[:, 0:128]),
                        reads=["wqps"], writes=["wqo"])
                sb.dma("sync", ov[:, :, t * 128:(t + 1) * 128],
                       oqt.rearrange("p (a o) -> p a o", o=128),
                       reads=["wqo"], writes=[okey])


def ph_tern(ctx, partials_all, w1ts, w2ts, w1q_out, w2q_out, pkey, okeys):
    sb = ctx.sb
    with ctx.phase("tern"):
        ones8 = ctx.const("ones8", np.ones((8, 1), dtype=np.float32))
        ones1r = ctx.const("ones1r", np.ones((1, 128), dtype=np.float32))
        part_t = ctx.sbuf("tpart", [8, 16])
        on8 = ctx.sbuf("ton8", [8, 1])
        on1r = ctx.sbuf("ton1", [1, 128])
        scal = ctx.sbuf("tscal", [1, 2])
        thr = ctx.sbuf("tthr", [128, 2])
        wt = ctx.sbuf("twt", [128, 2048])
        ab = ctx.sbuf("tab", [128, 2048])
        sg = ctx.sbuf("tsg", [128, 2048])
        tq = ctx.sbuf("ttq", [128, 2048], BF16)
        ps = ctx.psum("tps")
        sb.dma("sync", part_t, partials_all, reads=[pkey], writes=["tpt"])
        sb.dma("sync", on8, ones8, writes=["to8"])
        sb.dma("sync", on1r, ones1r, writes=["to1"])
        sb.add("tensor", lambda e: e.matmul(ps[0:1, 0:2], on8, part_t[:, 0:2],
                                            start=True, stop=True),
               reads=["tpt", "to8"], writes=["tpsk"])
        sb.add("vector", lambda e: e.tensor_copy(scal, ps[0:1, 0:2]),
               reads=["tpsk"], writes=["tsc"])
        sb.add("vector", lambda e: e.tensor_scalar(
            scal, scal, 1.0 / 16777216.0, None, ALU.mult),
            reads=["tsc"], writes=["tsc"])
        sb.add("vector", lambda e: e.tensor_scalar(scal, scal, 1e-8, None, ALU.max),
            reads=["tsc"], writes=["tsc"])
        sb.add("vector", lambda e: e.tensor_scalar(scal, scal, 0.5, None, ALU.mult),
            reads=["tsc"], writes=["tsc"])
        sb.add("tensor", lambda e: e.matmul(ps[0:128, 2:4], on1r, scal,
                                            start=True, stop=True),
               reads=["tsc", "to1"], writes=["tpsk"])
        sb.add("vector", lambda e: e.tensor_copy(thr, ps[0:128, 2:4]),
               reads=["tpsk"], writes=["tth"])
        for (src, o_t, nt_, width, ti, okey) in (
                (w1ts, w1q_out, 2, 8192, 0, okeys[0]),
                (w2ts, w2q_out, 8, 2048, 1, okeys[1])):
            v = src.rearrange("(n p) d -> n p d", p=128)
            o = o_t.rearrange("(n p) d -> n p d", p=128)
            nch = width // 2048
            for i in range(nt_):
                for ch in range(nch):
                    cs = slice(ch * 2048, (ch + 1) * 2048)
                    sb.dma("sync", wt, v[i][:, cs], writes=["twq"])
                    sb.add("scalar", lambda e: e.activation(ab, wt, AF.Abs),
                           reads=["twq"], writes=["tta"])
                    sb.add("scalar", lambda e: e.activation(sg, wt, AF.Sign),
                           reads=["twq"], writes=["tts"])
                    sb.add("vector", lambda e, ti=ti: e.tensor_scalar(
                        ab, ab, thr[:, ti:ti + 1], None, ALU.is_gt),
                        reads=["tta", "tth"], writes=["tta"])
                    sb.add("vector", lambda e: e.tensor_tensor(
                        tq, ab, sg, ALU.mult), reads=["tta", "tts"], writes=["ttr"])
                    sb.dma("sync", o[i][:, cs], tq, reads=["ttr"], writes=[okey])


def _wcols(w, j0, j1):
    if isinstance(w, tuple):
        _, v3, row0, rows = w
        out = []
        j = j0
        while j < j1:
            r = j // 256
            hi = min(j1, (r + 1) * 256)
            out.append((j - j0, v3[r, row0:row0 + rows, j - r * 256:hi - r * 256]))
            j = hi
        return out
    return [(0, w[:, j0:j1])]


def _wshape1(w):
    if isinstance(w, tuple):
        return 8 * 256
    return w.shape[1]


def ph_qkv(ctx, layer, x_in, wq, wk, wv, cos_in, sin_in,
           qT_o, kT_o, v_o, in_keys, out_keys):
    l0 = (layer == 0)
    KD = _wshape1(wk) if not isinstance(wk, tuple) else kT_o.shape[0]
    sb = ctx.sb
    with ctx.phase(f"qkv{layer}"):
        ident = ctx.const("identb", np.eye(128, dtype=BF))
        psgn = ctx.const("psgn", host_psgn())
        xt = ctx.sbuf("kx", [128, 2048])
        xqb = ctx.sbuf("kxq", [128, 2048], BF16)
        xqT = ctx.sbuf("kxqT", [128, NIC * TOK], BF16)
        identt = ctx.sbuf("kid", [128, 128], BF16)
        psgnt = ctx.sbuf("kpg", [128, 128], BF16)
        cost = ctx.sbuf("kcos", [128, TOK])
        sint = ctx.sbuf("ksin", [128, TOK])
        wslab = [ctx.sbuf(f"kws{i}", [128, NIC * 128], BF16) for i in range(2)]
        wvfull = ctx.sbuf("kwv", [128, NIC * min(KD, 2048)], BF16)
        qs = ctx.sbuf("kqs", [128, 512])
        qb = ctx.sbuf("kqb", [128, 512], BF16)
        ra = ctx.sbuf("kra", [128, 512])
        rb_ = ctx.sbuf("krb", [128, 512])
        rout = ctx.sbuf("kro", [128, 512])
        routb = ctx.sbuf("krob", [128, 512], BF16)
        vout = ctx.sbuf("kvo", [128, 512])
        tmp = alloc_quant_tmp(ctx) if l0 else None
        pp = ctx.psum("kpp")
        pr = ctx.psum("kpr")
        pt = ctx.psum("kpt", BF16)

        sb.dma("sync", identt, ident, writes=["kid"])
        sb.dma("sync", psgnt, psgn, writes=["kpg"])
        sb.dma("sync", cost, cos_in, writes=["kcos"])
        sb.dma("sync", sint, sin_in, writes=["ksin"])

        xv = x_in.rearrange("(n p) d -> n p d", p=128)
        xqTv = xqT.rearrange("p (a t) -> p a t", t=TOK)
        for i in range(NT):
            sb.dma("sync", xt, xv[i], reads=[in_keys.get("x", "_")], writes=["kx"])
            if l0:
                emit_fp4_quant(ctx, xt, xqb, tmp, "kx", "kxq")
            else:
                sb.add("vector", lambda e: e.tensor_copy(xqb, xt),
                       reads=["kx"], writes=["kxq"])
            for ic in range(NIC):
                sb.add("tensor", lambda e, ic=ic: e.transpose(
                    pt[:, 0:128], xqb[:, ic * 128:(ic + 1) * 128], identt),
                    reads=["kxq", "kid"], writes=["kpt"])
                sb.add("scalar", lambda e, ic=ic, i=i: e.copy(
                    xqTv[:, ic, i * 128:(i + 1) * 128], pt[:, 0:128]),
                    reads=["kpt"], writes=["kxT"])

        for (w, o_t, od, wkey, okey, is_q) in (
                (wq, qT_o, D, in_keys.get("wq", "_"), out_keys["q"], True),
                (wk, kT_o, KD, in_keys.get("wk", "_"), out_keys["k"], False)):
            ov = o_t.rearrange("(a p) t -> p a t", p=128)
            for dc in range(od // 128):
                buf = wslab[dc % 2]
                kwsl = f"kwsl{dc % 2}"
                for (doff, src) in _wcols(w, dc * 128, (dc + 1) * 128):
                    wsub = src.shape[1]
                    sb.dma("gpsimd",
                           buf.rearrange("p (a o) -> p a o", o=128)
                           [:, :, doff:doff + wsub],
                           src.rearrange("(a p) o -> p a o", p=128),
                           reads=[wkey], writes=[kwsl])
                for tb in range(2):
                    for ic in range(NIC):
                        sb.add("tensor", lambda e, ic=ic, tb=tb, buf=buf: e.matmul(
                            pp[:, 0:512], buf[:, ic * 128:(ic + 1) * 128],
                            xqTv[:, ic, tb * 512:(tb + 1) * 512],
                            start=(ic == 0), stop=(ic == NIC - 1)),
                            reads=[kwsl, "kxT"], writes=["kpp"])
                    sb.add("vector", lambda e: e.tensor_copy(qs, pp[:, 0:512]),
                           reads=["kpp"], writes=["kqs"])
                    sb.add("scalar", lambda e: e.copy(qb, qs),
                           reads=["kqs"], writes=["kqb"])
                    sb.add("tensor", lambda e: e.matmul(
                        pr[:, 0:512], psgnt, qb, start=True, stop=True),
                        reads=["kqb", "kpg"], writes=["kpr"])
                    sb.add("vector", lambda e, tb=tb: e.tensor_tensor(
                        ra, qs, cost[:, tb * 512:(tb + 1) * 512], ALU.mult),
                        reads=["kqs", "kcos"], writes=["kra"])
                    sb.add("vector", lambda e, tb=tb: e.tensor_tensor(
                        rb_, pr[:, 0:512], sint[:, tb * 512:(tb + 1) * 512],
                        ALU.mult), reads=["kpr", "ksin"], writes=["krb"])
                    if is_q:
                        sb.add("vector", lambda e: e.tensor_tensor(
                            routb, ra, rb_, ALU.add),
                            reads=["kra", "krb"], writes=["kro"])
                        sb.dma("sync", ov[:, dc, tb * 512:(tb + 1) * 512], routb,
                               reads=["kro"], writes=[okey])
                    else:
                        sb.add("vector", lambda e: e.tensor_tensor(
                            rout, ra, rb_, ALU.add),
                            reads=["kra", "krb"], writes=["kro"])
                        sb.dma("sync", ov[:, dc, tb * 512:(tb + 1) * 512], rout,
                               reads=["kro"], writes=[okey])

        wvfv = wvfull.rearrange("p (a o) -> p a o", o=min(KD, 2048))
        for (doff, src) in _wcols(wv, 0, KD):
            wsub = src.shape[1]
            sb.dma("gpsimd", wvfv[:, :, doff:doff + wsub],
                   src.rearrange("(a p) o -> p a o", p=128),
                   reads=[in_keys.get("wv", "_")], writes=["kwvf"])
        vv = v_o.rearrange("(n p) d -> n p d", p=128)
        nvb = max(1, KD // 512)
        vw = min(512, KD)
        for i in range(NT):
            for ob in range(nvb):
                for ic in range(NIC):
                    sb.add("tensor", lambda e, ic=ic, i=i, ob=ob: e.matmul(
                        pp[:, 0:vw], xqTv[:, ic, i * 128:(i + 1) * 128],
                        wvfv[:, ic, ob * 512:ob * 512 + vw],
                        start=(ic == 0), stop=(ic == NIC - 1)),
                        reads=["kwvf", "kxT"], writes=["kpp"])
                sb.add("scalar", lambda e: e.copy(vout[:, 0:vw], pp[:, 0:vw]),
                       reads=["kpp"], writes=["kvo"])
                sb.dma("sync", vv[i][:, ob * 512:ob * 512 + vw], vout[:, 0:vw],
                       reads=["kvo"], writes=[out_keys["v"]])


def ph_attn(ctx, layer, qT_in, kTf, vf, masks, x_in, wo, x1_o, in_keys, okey):
    l0 = (layer == 0)
    KD = kTf.shape[0]
    VC = 130
    sb = ctx.sb
    kq = in_keys.get("q", "_")
    kk = in_keys.get("k", "_")
    kv_ = in_keys.get("v", "_")
    kwo = in_keys.get("wo", "_")
    kxk = in_keys.get("x", "_")
    with ctx.phase(f"attn{layer}"):
        ident = ctx.const("identb2", np.eye(128, dtype=BF))
        qTt = ctx.sbuf("aqT", [128, 16 * GSZ], BF16)
        maskt = ctx.sbuf("amask", [128, MASK_W * GSZ], BF16)
        kthp2 = [ctx.sbuf(f"akthp{i}", [128, S], BF16) for i in range(2)]
        vhp2 = [ctx.sbuf(f"avhp{i}", [128, 32 * VC], BF16) for i in range(2)]
        pT2 = [ctx.sbuf(f"apT{i}", [128, GSZ], BF16) for i in range(2)]
        aos = ctx.sbuf("aaos", [128, 4 * 2048])
        aoq = ctx.sbuf("aaoq", [128, 2048], BF16)
        oqT = ctx.sbuf("aoqT", [128, 16 * GSZ], BF16)
        wslb = [ctx.sbuf(f"awsl{i}", [128, NIC * 512], BF16) for i in range(2)]
        rden = ctx.sbuf("arden", [128, 1])
        identt = ctx.sbuf("aid", [128, 128], BF16)
        lnscr = ctx.sbuf("alnscr", [128, 2048])
        stats = ctx.sbuf("astats", [128, 8])
        tmp = alloc_quant_tmp(ctx) if l0 else None
        ps_s = [ctx.psum("aps_s0"), ctx.psum("aps_s1")]
        ps_o = [ctx.psum(f"aps_o{i}") for i in range(4)]
        pp = ctx.psum("app")
        pt = ctx.psum("apt", BF16)

        sb.dma("sync", identt, ident, writes=["aid"])
        vhpv2 = [v.rearrange("p (a c) -> p a c", c=VC) for v in vhp2]
        for i in range(2):
            sb.add("vector", lambda e, i=i: e.memset(vhpv2[i][:, :, 64:65], 1.0),
                   writes=[f"avhp{i}"])
            sb.add("vector", lambda e, i=i: e.memset(vhpv2[i][:, :, 129:130], 1.0),
                   writes=[f"avhp{i}"])

        qTv = qT_in.rearrange("(a p) t -> p a t", p=128)
        x1v = x1_o.rearrange("(n p) d -> n p d", p=128)
        xv = x_in.rearrange("(n p) d -> n p d", p=128)
        qTtv = qTt.rearrange("p (a t) -> p a t", t=GSZ)
        oqTv = oqT.rearrange("p (a t) -> p a t", t=GSZ)
        wslv = [w.rearrange("p (a o) -> p a o", o=512) for w in wslb]

        shap, shkey = ctx.fconst(EXP_SHIFT)
        scap, sckey = ctx.fconst(SCALE_ATT)
        for slot in range(2):
            ktn = KT_PAD[slot]
            sb.dma("sync", qTtv, qTv[:, :, slot * GSZ:(slot + 1) * GSZ],
                   reads=[kq], writes=["aqT"])
            sb.dma("sync", maskt.rearrange("p (m q) -> p m q", q=GSZ),
                   masks[slot * MASK_W:(slot + 1) * MASK_W]
                   .rearrange("m p q -> p m q"),
                   reads=[in_keys.get("masks", "_")], writes=["amask"])
            for hp in range(16 if l0 else 8):
                bi = hp % 2
                kthp = kthp2[bi]
                vhpv = vhpv2[bi]
                kkt, kvh = f"akthp{bi}", f"avhp{bi}"
                if l0:
                    sb.dma("gpsimd", kthp[:, 0:ktn * 128],
                           kTf[hp * 128:(hp + 1) * 128, 0:ktn * 128],
                           reads=[kk], writes=[kkt])
                    for half in range(2):
                        sb.dma("gpsimd",
                               vhpv[:, 0:ktn, 65 * half:65 * half + 64],
                               vf[0:ktn * 128,
                                  hp * 128 + 64 * half:hp * 128 + 64 * half + 64]
                               .rearrange("(t p) c -> p t c", p=128),
                               reads=[kv_], writes=[kvh])
                else:
                    for half in range(2):
                        sb.dma("gpsimd", kthp[64 * half:64 * half + 64, 0:ktn * 128],
                               kTf[hp * 64:(hp + 1) * 64, 0:ktn * 128],
                               reads=[kk], writes=[kkt])
                    sb.dma("gpsimd", vhpv[:, 0:ktn, 0:64],
                           vf[0:ktn * 128, hp * 64:(hp + 1) * 64]
                           .rearrange("(t p) c -> p t c", p=128),
                           reads=[kv_], writes=[kvh])
                for h2 in range(2 if l0 else 4):
                    qh = hp * (2 if l0 else 4) + h2
                    krow = 64 * (qh % 2)
                    vcol = (65 * (qh % 2)) if l0 else 0
                    for kt in range(ktn):
                        pss = ps_s[kt % 2]
                        kss = f"apss{kt % 2}"
                        pT = pT2[kt % 2]
                        kpt = f"apT{kt % 2}"
                        sb.add("tensor", lambda e, kt=kt, qh=qh, krow=krow,
                               pss=pss, kthp=kthp: e.matmul(
                            pss[:, 0:GSZ],
                            kthp[krow:krow + 64, kt * 128:(kt + 1) * 128],
                            qTtv[64 * (qh % 2):64 * (qh % 2) + 64, qh // 2, :],
                            start=True, stop=True),
                            reads=[kkt, "aqT"], writes=[kss])
                        sb.add("scalar", lambda e, pss=pss, pT=pT, shap=shap,
                               scap=scap:
                               e.activation(pT, pss[:, 0:GSZ], AF.Exp,
                                            bias=shap, scale=scap),
                               reads=[kss, shkey, sckey], writes=[kpt])
                        wi = kt - MASK_START[slot]
                        if 0 <= wi < MASK_W:
                            sb.add("vector", lambda e, wi=wi, pT=pT: e.tensor_tensor(
                                pT, pT, maskt[:, wi * GSZ:(wi + 1) * GSZ],
                                ALU.mult),
                                reads=[kpt, "amask"], writes=[kpt])
                        for s4 in range(4):
                            sb.add("tensor", lambda e, s4=s4, kt=kt, vcol=vcol,
                                   ktn=ktn, pT=pT, vhpv=vhpv: e.matmul(
                                ps_o[s4][:, 0:65],
                                pT[:, s4 * 128:(s4 + 1) * 128],
                                vhpv[:, kt, vcol:vcol + 65],
                                start=(kt == 0), stop=(kt == ktn - 1)),
                                reads=[kpt, kvh], writes=[f"apso{s4}"])
                    for s4 in range(4):
                        sb.add("vector", lambda e, s4=s4: e.reciprocal(
                            rden, ps_o[s4][:, 64:65]),
                            reads=[f"apso{s4}"], writes=["arden"])
                        sb.add("vector", lambda e, s4=s4, qh=qh: e.tensor_scalar(
                            aos[:, s4 * 2048 + qh * 64: s4 * 2048 + qh * 64 + 64],
                            ps_o[s4][:, 0:64], rden, None, ALU.mult),
                            reads=[f"apso{s4}", "arden"], writes=["aaos"])

            for s4 in range(4):
                if l0:
                    emit_fp4_quant(ctx, aos[:, s4 * 2048:(s4 + 1) * 2048], aoq,
                                   tmp, "aaos", "aaoq")
                else:
                    sb.add("vector", lambda e, s4=s4: e.tensor_copy(
                        aoq, aos[:, s4 * 2048:(s4 + 1) * 2048]),
                        reads=["aaos"], writes=["aaoq"])
                for ic in range(NIC):
                    sb.add("tensor", lambda e, ic=ic: e.transpose(
                        pt[:, 0:128], aoq[:, ic * 128:(ic + 1) * 128], identt),
                        reads=["aaoq", "aid"], writes=["apt"])
                    sb.add("scalar", lambda e, ic=ic, s4=s4: e.copy(
                        oqTv[:, ic, s4 * 128:(s4 + 1) * 128], pt[:, 0:128]),
                        reads=["apt"], writes=["aoqT"])
            for tt in range(4):
                sb.dma("sync", aos[:, tt * 2048:(tt + 1) * 2048],
                       xv[slot * 4 + tt], reads=[kxk], writes=["aaos"])
            for ob in range(4):
                buf = wslv[ob % 2]
                kwsl = f"awosl{ob % 2}"
                for (doff, src) in _wcols(wo, ob * 512, (ob + 1) * 512):
                    wsub = src.shape[1]
                    sb.dma("gpsimd", buf[:, :, doff:doff + wsub],
                           src.rearrange("(a p) o -> p a o", p=128),
                           reads=[kwo], writes=[kwsl])
                for tt in range(4):
                    for ic in range(NIC):
                        sb.add("tensor", lambda e, ic=ic, tt=tt, buf=buf: e.matmul(
                            pp[:, 0:512], oqTv[:, ic, tt * 128:(tt + 1) * 128],
                            buf[:, ic, :], start=(ic == 0), stop=(ic == NIC - 1)),
                            reads=["aoqT", kwsl], writes=["app"])
                    sb.add("vector", lambda e, tt=tt, ob=ob: e.tensor_tensor(
                        aos[:, tt * 2048 + ob * 512: tt * 2048 + (ob + 1) * 512],
                        aos[:, tt * 2048 + ob * 512: tt * 2048 + (ob + 1) * 512],
                        pp[:, 0:512], ALU.add),
                        reads=["app", "aaos"], writes=["aaos"])
            for tt in range(4):
                xr = aos[:, tt * 2048:(tt + 1) * 2048]
                emit_ln(ctx, xr, lnscr, stats, "aaos")
                sb.dma("sync", x1v[slot * 4 + tt], xr, reads=["aaos"],
                       writes=[okey])


def ph_ffn(ctx, layer, x_in, wg, wu, wd, vec1, vec2, xo, in_keys, okey):
    l0 = (layer == 0)
    MID = INTER0 if l0 else DFF1
    NMC = MID // 128
    PASS_MC = 22 if l0 else 16
    NPASS = NMC // PASS_MC
    sb = ctx.sb
    kx = in_keys.get("x", "_")
    kwg = in_keys.get("wg", "_")
    kwu = in_keys.get("wu", "_")
    kwd = in_keys.get("wd", "_")
    kvec = in_keys.get("vec", "_")
    with ctx.phase(f"ffn{layer}"):
        ident = ctx.const("identb3", np.eye(128, dtype=BF))
        ones1 = ctx.const("ones1f", np.ones((1, 128), dtype=np.float32))
        xT = ctx.sbuf("fxT", [128, NIC * TOK], BF16)
        x2pre = ctx.sbuf("fx2", [128, NT * 2048])
        hfull = ctx.sbuf("fh", [128, PASS_MC * TOK], BF16)
        gsl = [ctx.sbuf(f"fg{i}", [128, NIC * 128], BF16) for i in range(2)]
        usl = [ctx.sbuf(f"fu{i}", [128, NIC * 128], BF16) for i in range(2)]
        dslb = [ctx.sbuf(f"fd{i}", [128, 512], BF16) for i in range(2)]
        dstg = [ctx.sbuf(f"fds{i}", [128, 512]) for i in range(2)] if l0 else None
        bcast = ctx.sbuf("fbc", [128, 2048])
        rowt = ctx.sbuf("frow", [1, 2048])
        hg = ctx.sbuf("fhg", [128, 512])
        hu = ctx.sbuf("fhu", [128, 512])
        xbt = ctx.sbuf("fxb", [128, 2048], BF16)
        xtile = ctx.sbuf("fxt", [128, 2048])
        identt = ctx.sbuf("fid", [128, 128], BF16)
        onest = ctx.sbuf("fon", [1, 128])
        gvec = ctx.sbuf("fgv", [128, 2 * NMC])
        lnscr = ctx.sbuf("flnscr", [128, 2048])
        stats = ctx.sbuf("fstats", [128, 8])
        ps_g = ctx.psum("fps_g")
        ps_u = ctx.psum("fps_u")
        pt = ctx.psum("fpt", BF16)

        sb.dma("sync", identt, ident, writes=["fid"])
        sb.dma("sync", onest, ones1, writes=["fon"])
        if l0:
            sb.dma("sync", gvec[:, 0:NMC], vec1[0], reads=[kvec], writes=["fgv"])
            sb.dma("sync", gvec[:, NMC:2 * NMC], vec1[1], reads=[kvec],
                   writes=["fgv"])
        else:
            sb.dma("sync", gvec[:, 0:NMC], vec1, reads=[kvec], writes=["fgv"])
        sb.dma("sync", rowt, vec2, reads=[kvec], writes=["frow"])

        for j in range(4):
            sb.add("tensor", lambda e, j=j, psg=ps_g: e.matmul(
                psg[:, 0:512], onest, rowt[:, j * 512:(j + 1) * 512],
                start=True, stop=True), reads=["frow", "fon"], writes=["fpsg"])
            sb.add("vector", lambda e, j=j, psg=ps_g: e.tensor_copy(
                bcast[:, j * 512:(j + 1) * 512], psg[:, 0:512]),
                reads=["fpsg"], writes=["fbc"])

        xv = x_in.rearrange("(n p) d -> n p d", p=128)
        xov = xo.rearrange("(n p) d -> n p d", p=128)
        xTv = xT.rearrange("p (a t) -> p a t", t=TOK)
        hv = hfull.rearrange("p (a t) -> p a t", t=TOK)
        x2v = x2pre.rearrange("p (a d) -> p a d", d=2048)

        for i in range(NT):
            sb.dma("sync", xtile, xv[i], reads=[kx], writes=["fxt"])
            if l0:
                sb.add("vector", lambda e, i=i: e.tensor_copy(x2v[:, i, :], xtile),
                       reads=["fxt"], writes=["fx2"])
            else:
                sb.add("vector", lambda e, i=i: e.tensor_tensor(
                    x2v[:, i, :], xtile, bcast, ALU.add),
                    reads=["fxt", "fbc"], writes=["fx2"])
            sb.add("vector", lambda e: e.tensor_copy(xbt, xtile),
                   reads=["fxt"], writes=["fxb"])
            for ic in range(NIC):
                sb.add("tensor", lambda e, ic=ic: e.transpose(
                    pt[:, 0:128], xbt[:, ic * 128:(ic + 1) * 128], identt),
                    reads=["fxb", "fid"], writes=["fpt"])
                sb.add("scalar", lambda e, ic=ic, i=i: e.copy(
                    xTv[:, ic, i * 128:(i + 1) * 128], pt[:, 0:128]),
                    reads=["fpt"], writes=["fxT"])

        wgv = wg.rearrange("(a p) o -> p a o", p=128)
        if l0:
            wuv = wu.rearrange("(a p) o -> p a o", p=128)
        geng = "gpsimd" if l0 else "sync"

        for hp_ in range(NPASS):
            for mcl in range(PASS_MC):
                mc = hp_ * PASS_MC + mcl
                gb = gsl[mcl % 2]
                kg = f"fgsl{mcl % 2}"
                g_eng = geng if l0 else ("sync" if mcl % 2 == 0 else "scalar")
                sb.dma(g_eng, gb.rearrange("p (a o) -> p a o", o=128),
                       wgv[:, :, mc * 128:(mc + 1) * 128],
                       reads=[kwg], writes=[kg])
                if l0:
                    ub = usl[mcl % 2]
                    ku = f"fusl{mcl % 2}"
                    sb.dma("gpsimd", ub.rearrange("p (a o) -> p a o", o=128),
                           wuv[:, :, mc * 128:(mc + 1) * 128],
                           reads=[kwu], writes=[ku])
                for tb in range(2):
                    for ic in range(NIC):
                        sb.add("tensor", lambda e, ic=ic, tb=tb, gb=gb,
                               psg=ps_g: e.matmul(
                            psg[:, 0:512], gb[:, ic * 128:(ic + 1) * 128],
                            xTv[:, ic, tb * 512:(tb + 1) * 512],
                            start=(ic == 0), stop=(ic == NIC - 1)),
                            reads=[kg, "fxT"], writes=["fpsg"])
                    if l0:
                        for ic in range(NIC):
                            sb.add("tensor", lambda e, ic=ic, tb=tb, ub=ub,
                                   psu=ps_u: e.matmul(
                                psu[:, 0:512], ub[:, ic * 128:(ic + 1) * 128],
                                xTv[:, ic, tb * 512:(tb + 1) * 512],
                                start=(ic == 0), stop=(ic == NIC - 1)),
                                reads=[ku, "fxT"], writes=["fpsu"])
                        sb.add("scalar", lambda e, mc=mc, psg=ps_g: e.activation(
                            hg, psg[:, 0:512], AF.Silu, scale=gvec[:, mc:mc + 1]),
                            reads=["fpsg", "fgv"], writes=["fhg"])
                        sb.add("vector", lambda e, mc=mc, psu=ps_u: e.tensor_scalar(
                            hu, psu[:, 0:512], gvec[:, NMC + mc:NMC + mc + 1],
                            None, ALU.mult), reads=["fpsu", "fgv"], writes=["fhu"])
                        sb.add("vector", lambda e, mcl=mcl, tb=tb: e.tensor_tensor(
                            hv[:, mcl, tb * 512:(tb + 1) * 512], hg, hu, ALU.mult),
                            reads=["fhg", "fhu"], writes=["fh"])
                    else:
                        sb.add("scalar", lambda e, mc=mc, mcl=mcl, tb=tb,
                               psg=ps_g:
                               e.activation(
                            hv[:, mcl, tb * 512:(tb + 1) * 512], psg[:, 0:512],
                            AF.Gelu, bias=gvec[:, mc:mc + 1]),
                            reads=["fpsg", "fgv"], writes=["fh"])
            ctx.psum_flush()
            ps8 = [ctx.psum(f"fp8_{i}") for i in range(8)]
            for ob in range(4):
                for mcl in range(PASS_MC):
                    mc = hp_ * PASS_MC + mcl
                    bi = mcl % 2
                    qeng = "sync" if bi == 0 else "scalar"
                    db = dslb[bi]
                    kd = f"fdsl{bi}"
                    wsrc = wd[mc * 128:(mc + 1) * 128, ob * 512:(ob + 1) * 512]
                    if l0:
                        stg = dstg[bi]
                        kstg = f"fstg{bi}"
                        sb.dma(qeng, stg, wsrc, reads=[kwd], writes=[kstg])
                        sb.add("scalar", lambda e, stg=stg, db=db: e.copy(db, stg),
                               reads=[kstg], writes=[kd])
                    else:
                        sb.dma(qeng, db, wsrc, reads=[kwd], writes=[kd])
                    for ti in range(8):
                        sb.add("tensor", lambda e, mcl=mcl, ti=ti, db=db,
                               p8=ps8[ti]: e.matmul(
                            p8[:, 0:512],
                            hv[:, mcl, ti * 128:(ti + 1) * 128],
                            db, start=(mcl == 0), stop=(mcl == PASS_MC - 1)),
                            reads=["fh", kd], writes=[f"fp8_{ti}"])
                for ti in range(8):
                    if l0:
                        sb.add("vector", lambda e, ti=ti, ob=ob,
                               p8=ps8[ti]: e.tensor_tensor(
                            hg, p8[:, 0:512],
                            bcast[:, ob * 512:(ob + 1) * 512],
                            ALU.mult), reads=[f"fp8_{ti}", "fbc"],
                            writes=["fhg"])
                        sb.add("vector", lambda e, ti=ti, ob=ob: e.tensor_tensor(
                            x2v[:, ti, ob * 512:(ob + 1) * 512],
                            x2v[:, ti, ob * 512:(ob + 1) * 512], hg, ALU.add),
                            reads=["fhg", "fx2"], writes=["fx2"])
                    else:
                        sb.add("vector", lambda e, ti=ti, ob=ob,
                               p8=ps8[ti]: e.tensor_tensor(
                            x2v[:, ti, ob * 512:(ob + 1) * 512],
                            x2v[:, ti, ob * 512:(ob + 1) * 512],
                            p8[:, 0:512], ALU.add),
                            reads=[f"fp8_{ti}", "fx2"], writes=["fx2"])
            ctx.psum_flush()
            if hp_ < NPASS - 1:
                ps_g = ctx.psum("fps_g")
                if l0:
                    ps_u = ctx.psum("fps_u")

        for i in range(NT):
            xr = x2v[:, i, :]
            emit_ln(ctx, xr, lnscr, stats, "fx2")
            sb.dma("sync", xov[i], xr, reads=["fx2"], writes=[okey])



def _prefetch_consts(ctx):
    ctx.fconst(1e-5)
    ctx.fconst(EXP_SHIFT)
    ctx.fconst(SCALE_ATT)


def build_launch_a():
    ctx = Ctx("la")
    wq_s = ctx.inp("wq_s", [256, D])
    wk_s = ctx.inp("wk_s", [256, D])
    wv_s = ctx.inp("wv_s", [256, D])
    wo_s = ctx.inp("wo_s", [256, D])
    w1ts = ctx.inp("w1ts", [256, DFF1])
    w2ts = ctx.inp("w2ts", [1024, D])
    psums_o = ctx.out("psums", [1, 16])
    wqqt_o = ctx.out("wqqt_s", [D, 256])
    wkqt_o = ctx.out("wkqt_s", [D, 256])
    wvqt_o = ctx.out("wvqt_s", [D, 256])
    woqt_o = ctx.out("woqt_s", [D, 256])
    _prefetch_consts(ctx)
    ph_wprep1(ctx, w1ts, w2ts, psums_o, "pso")
    ph_wquant(ctx, [wq_s, wk_s, wv_s, wo_s],
              [wqqt_o, wkqt_o, wvqt_o, woqt_o], 2,
              ["wqk", "wkk", "wvk", "wok"])
    run_block(ctx.nc, ctx.sb, "la")
    return ctx.nc


def build_launch_b():
    ctx = Ctx("lb")
    x_in = ctx.inp("x_own", [TOK, D])
    wqqt = ctx.inp("wqqt", [D, D])
    wkqt = ctx.inp("wkqt", [D, D])
    wvqt = ctx.inp("wvqt", [D, D])
    cos_in = ctx.inp("cos_own", [128, TOK])
    sin_in = ctx.inp("sin_own", [128, TOK])
    qT_o = ctx.out("qT", [D, TOK], BF16)
    kT_o = ctx.out("kT", [D, TOK])
    v_o = ctx.out("v", [TOK, D])
    _prefetch_consts(ctx)
    ph_qkv(ctx, 0, x_in, wqqt, wkqt, wvqt, cos_in, sin_in,
           qT_o, kT_o, v_o, {}, {"q": "qo", "k": "ko", "v": "vo"})
    run_block(ctx.nc, ctx.sb, "lb")
    return ctx.nc


def build_launch_c():
    ctx = Ctx("lc")
    x_in = ctx.inp("x_own", [TOK, D])
    qT0 = ctx.inp("qT0", [D, TOK], BF16)
    kTf0 = ctx.inp("kTf0", [D, S])
    vf0 = ctx.inp("vf0", [S, D])
    masks = ctx.inp("masks", [2 * MASK_W, 128, GSZ], BF16)
    woqt = ctx.inp("woqt", [D, D])
    gateT = ctx.inp("gateT", [D, INTER0])
    upT = ctx.inp("upT", [D, INTER0])
    downT = ctx.inp("downT", [INTER0, D])
    gsb = ctx.inp("gsb", [128, INTER0 // 128])
    usb = ctx.inp("usb", [128, INTER0 // 128])
    dsr = ctx.inp("dsr", [1, D])
    partials_all = ctx.inp("partials_all", [8, 16])
    w1ts = ctx.inp("w1ts", [256, DFF1])
    w2ts = ctx.inp("w2ts", [1024, D])
    wq1 = ctx.inp("wq1", [D, D])
    wk1 = ctx.inp("wk1", [D, 512])
    wv1 = ctx.inp("wv1", [D, 512])
    cos_in = ctx.inp("cos_own", [128, TOK])
    sin_in = ctx.inp("sin_own", [128, TOK])
    x2_o = ctx.out("x2", [TOK, D])
    qT1_o = ctx.out("qT1", [D, TOK], BF16)
    kT1_o = ctx.out("kT1", [512, TOK])
    v1_o = ctx.out("v1", [TOK, 512])
    w1q_o = ctx.out("w1q_s", [256, DFF1], BF16)
    w2q_o = ctx.out("w2q_s", [1024, D], BF16)
    x1b = ctx.dram("x1b", [TOK, D])
    _prefetch_consts(ctx)
    ph_attn(ctx, 0, qT0, kTf0, vf0, masks, x_in, woqt, x1b,
            {"q": "_q", "k": "_k", "v": "_v", "wo": "_wo", "x": "_x",
             "masks": "_m"}, "x1k")
    ph_ffn(ctx, 0, x1b, gateT, upT, downT, (gsb, usb), dsr, x2_o,
           {"x": "x1k"}, "x2k")
    ph_tern(ctx, partials_all, w1ts, w2ts, w1q_o, w2q_o, "_pa",
            ["w1k", "w2k"])
    ph_qkv(ctx, 1, x2_o, wq1, wk1, wv1, cos_in, sin_in,
           qT1_o, kT1_o, v1_o, {"x": "x2k"},
           {"q": "q1o", "k": "k1o", "v": "v1o"})
    run_block(ctx.nc, ctx.sb, "lc")
    return ctx.nc


def build_launch_d():
    ctx = Ctx("ld")
    x2_in = ctx.inp("x2_own", [TOK, D])
    qT1 = ctx.inp("qT1", [D, TOK], BF16)
    kTf1 = ctx.inp("kTf1", [512, S])
    vf1 = ctx.inp("vf1", [S, 512])
    masks = ctx.inp("masks", [2 * MASK_W, 128, GSZ], BF16)
    wo1 = ctx.inp("wo1", [D, D])
    w1qT = ctx.inp("w1qT", [D, DFF1], BF16)
    w2qT = ctx.inp("w2qT", [DFF1, D], BF16)
    b1b = ctx.inp("b1b", [128, DFF1 // 128])
    b2r = ctx.inp("b2r", [1, D])
    xo = ctx.out("xo", [TOK, D])
    x3b = ctx.dram("x3b", [TOK, D])
    _prefetch_consts(ctx)
    ph_attn(ctx, 1, qT1, kTf1, vf1, masks, x2_in, wo1, x3b,
            {}, "x3k")
    ph_ffn(ctx, 1, x3b, w1qT, None, w2qT, b1b, b2r, xo,
           {"x": "x3k"}, "xok")
    run_block(ctx.nc, ctx.sb, "ld")
    return ctx.nc



_NC_CACHE = {}


def _run(name, builder, in_maps):
    if name not in _NC_CACHE:
        _NC_CACHE[name] = builder()
    res = run_bass_kernel_spmd(_NC_CACHE[name], in_maps, core_ids=list(range(NCORE)))
    return res.results


def _order_kv(rq, kd, kname, vname):
    f32 = np.float32
    kTfs = [np.zeros((kd, S), dtype=f32) for _ in range(B)]
    vfs = [np.zeros((S, kd), dtype=f32) for _ in range(B)]
    for c in range(NCORE):
        b = c // 4
        (r0, p0), (r1, p1) = core_token_rows(c)
        kT = np.asarray(rq[c][kname])
        v = np.asarray(rq[c][vname])
        for slot, p in ((0, p0), (1, p1)):
            kTfs[b][:, p * GSZ:(p + 1) * GSZ] = kT[:, slot * GSZ:(slot + 1) * GSZ]
            vfs[b][p * GSZ:(p + 1) * GSZ] = v[slot * GSZ:(slot + 1) * GSZ]
    return kTfs, vfs


def kernel_3launch(L0_attn_hidden_states, L0_attn_Wq, L0_attn_Wk, L0_attn_Wv, L0_attn_Wo,
           L0_ffn_gate_w, L0_ffn_up_w, L0_ffn_down_w, L0_ffn_gate_s, L0_ffn_up_s,
           L0_ffn_down_s, L1_attn_wq, L1_attn_wk, L1_attn_wv, L1_attn_wo,
           L1_ffn_w1, L1_ffn_b1, L1_ffn_w2, L1_ffn_b2):
    f32 = np.float32
    x = np.ascontiguousarray(np.asarray(L0_attn_hidden_states, f32).reshape(B * S, D))
    w1T = np.ascontiguousarray(np.asarray(L1_ffn_w1, f32).T)
    w2T = np.ascontiguousarray(np.asarray(L1_ffn_w2, f32).T)
    gateT = np.ascontiguousarray(np.asarray(L0_ffn_gate_w, f32).T)
    upT = np.ascontiguousarray(np.asarray(L0_ffn_up_w, f32).T)
    downT = np.ascontiguousarray(np.asarray(L0_ffn_down_w, f32).T)
    wq0 = np.ascontiguousarray(np.asarray(L0_attn_Wq, f32))
    wk0 = np.ascontiguousarray(np.asarray(L0_attn_Wk, f32))
    wv0 = np.ascontiguousarray(np.asarray(L0_attn_Wv, f32))
    wo0 = np.ascontiguousarray(np.asarray(L0_attn_Wo, f32))
    cos2, sin2 = host_rope_tables()

    def own_rows(c, arr2d):
        (r0, _), (r1, _) = core_token_rows(c)
        return np.ascontiguousarray(
            np.concatenate([arr2d[r0:r0 + GSZ], arr2d[r1:r1 + GSZ]], 0))

    def rope_own(c):
        (_, p0), (_, p1) = core_token_rows(c)
        cosn = np.concatenate([cos2[:, p0 * GSZ:(p0 + 1) * GSZ],
                               cos2[:, p1 * GSZ:(p1 + 1) * GSZ]], 1)
        sinn = np.concatenate([sin2[:, p0 * GSZ:(p0 + 1) * GSZ],
                               sin2[:, p1 * GSZ:(p1 + 1) * GSZ]], 1)
        return np.ascontiguousarray(cosn), np.ascontiguousarray(sinn)

    def scatter_rows(res, key):
        full = np.zeros((B * S, D), dtype=f32)
        for c in range(NCORE):
            (r0, _), (r1, _) = core_token_rows(c)
            full[r0:r0 + GSZ] = np.asarray(res[c][key])[:GSZ]
            full[r1:r1 + GSZ] = np.asarray(res[c][key])[GSZ:]
        return full

    ima = []
    for c in range(NCORE):
        ima.append({"wq_s": np.ascontiguousarray(wq0[c * 256:(c + 1) * 256]),
                    "wk_s": np.ascontiguousarray(wk0[c * 256:(c + 1) * 256]),
                    "wv_s": np.ascontiguousarray(wv0[c * 256:(c + 1) * 256]),
                    "wo_s": np.ascontiguousarray(wo0[c * 256:(c + 1) * 256]),
                    "w1ts": np.ascontiguousarray(w1T[c * 256:(c + 1) * 256]),
                    "w2ts": np.ascontiguousarray(w2T[c * 1024:(c + 1) * 1024])})
    ra = _run("la", build_launch_a, ima)
    partials_all = np.ascontiguousarray(
        np.concatenate([np.asarray(ra[c]["psums"]) for c in range(NCORE)], 0))
    wqQT = np.ascontiguousarray(np.concatenate(
        [np.asarray(ra[c]["wqqt_s"]) for c in range(NCORE)], 1))
    wkQT = np.ascontiguousarray(np.concatenate(
        [np.asarray(ra[c]["wkqt_s"]) for c in range(NCORE)], 1))
    wvQT = np.ascontiguousarray(np.concatenate(
        [np.asarray(ra[c]["wvqt_s"]) for c in range(NCORE)], 1))
    woQT = np.ascontiguousarray(np.concatenate(
        [np.asarray(ra[c]["woqt_s"]) for c in range(NCORE)], 1))

    imb = []
    for c in range(NCORE):
        cosn, sinn = rope_own(c)
        imb.append({"x_own": own_rows(c, x), "wqqt": wqQT, "wkqt": wkQT,
                    "wvqt": wvQT, "cos_own": cosn, "sin_own": sinn})
    rb = _run("lb", build_launch_b, imb)
    kTfs0, vfs0 = _order_kv(rb, D, "kT", "v")

    imc = []
    for c in range(NCORE):
        b = c // 4
        cosn, sinn = rope_own(c)
        imc.append({"x_own": imb[c]["x_own"], "qT0": np.asarray(rb[c]["qT"]),
                    "kTf0": kTfs0[b], "vf0": vfs0[b], "masks": host_masks(c),
                    "woqt": woQT, "gateT": gateT, "upT": upT, "downT": downT,
                    "gsb": host_band_vec(L0_ffn_gate_s),
                    "usb": host_band_vec(L0_ffn_up_s),
                    "dsr": np.ascontiguousarray(np.asarray(L0_ffn_down_s, f32)[None, :]),
                    "partials_all": partials_all,
                    "w1ts": ima[c]["w1ts"], "w2ts": ima[c]["w2ts"],
                    "wq1": np.ascontiguousarray(np.asarray(L1_attn_wq, f32)),
                    "wk1": np.ascontiguousarray(np.asarray(L1_attn_wk, f32)),
                    "wv1": np.ascontiguousarray(np.asarray(L1_attn_wv, f32)),
                    "cos_own": cosn, "sin_own": sinn})
    rc = _run("lc", build_launch_c, imc)
    kTfs1, vfs1 = _order_kv(rc, 512, "kT1", "v1")
    w1qT = np.ascontiguousarray(np.concatenate(
        [np.asarray(rc[c]["w1q_s"]) for c in range(NCORE)], 0))
    w2qT = np.ascontiguousarray(np.concatenate(
        [np.asarray(rc[c]["w2q_s"]) for c in range(NCORE)], 0))

    imd = []
    for c in range(NCORE):
        b = c // 4
        imd.append({"x2_own": np.ascontiguousarray(np.asarray(rc[c]["x2"])),
                    "qT1": np.asarray(rc[c]["qT1"]),
                    "kTf1": kTfs1[b], "vf1": vfs1[b], "masks": host_masks(c),
                    "wo1": np.ascontiguousarray(np.asarray(L1_attn_wo, f32)),
                    "w1qT": w1qT, "w2qT": w2qT,
                    "b1b": host_band_vec(L1_ffn_b1),
                    "b2r": np.ascontiguousarray(np.asarray(L1_ffn_b2, f32)[None, :])})
    rd = _run("ld", build_launch_d, imd)
    x4 = scatter_rows(rd, "xo")
    return x4.reshape(B, S, D)



GRP8 = [[0, 1, 2, 3, 4, 5, 6, 7]]
GRP4 = [[0, 1, 2, 3], [4, 5, 6, 7]]
PAY2KV = 4096
PAYT = 2048
PAY3KV = 1024


def build_fused():
    ctx = Ctx("fz")
    sb = ctx.sb
    nc = ctx.nc
    x_in = ctx.inp("x_own", [TOK, D])
    wq_s = ctx.inp("wq_s", [256, D])
    wk_s = ctx.inp("wk_s", [256, D])
    wv_s = ctx.inp("wv_s", [256, D])
    wo_s = ctx.inp("wo_s", [256, D])
    w1ts = ctx.inp("w1ts", [256, DFF1])
    w2ts = ctx.inp("w2ts", [1024, D])
    gateT = ctx.inp("gateT", [D, INTER0])
    upT = ctx.inp("upT", [D, INTER0])
    downT = ctx.inp("downT", [INTER0, D])
    gsb = ctx.inp("gsb", [128, INTER0 // 128])
    usb = ctx.inp("usb", [128, INTER0 // 128])
    dsr = ctx.inp("dsr", [1, D])
    wq1 = ctx.inp("wq1", [D, D])
    wk1 = ctx.inp("wk1", [D, 512])
    wv1 = ctx.inp("wv1", [D, 512])
    wo1 = ctx.inp("wo1", [D, D])
    b1b = ctx.inp("b1b", [128, DFF1 // 128])
    b2r = ctx.inp("b2r", [1, D])
    masks = ctx.inp("masks", [2 * MASK_W, 128, GSZ], BF16)
    cos_in = ctx.inp("cos_own", [128, TOK])
    sin_in = ctx.inp("sin_own", [128, TOK])
    xo = ctx.out("xo", [TOK, D])

    pay1_in = ctx.dram("pay1_in", [PAY1_ROWS, PAY1W])
    pay1_out = ctx.dram("pay1_out", [8 * PAY1_ROWS, PAY1W])
    pay2_in = ctx.dram("pay2_in", [PAY2KV, PAY2W])
    pay2_out = ctx.dram("pay2_out", [4 * PAY2KV, PAY2W])
    payt_in = ctx.dram("payt_in", [PAYT, PAY2W])
    payt_out = ctx.dram("payt_out", [8 * PAYT, PAY2W])
    pay3_in = ctx.dram("pay3_in", [PAY3KV, PAY3W])
    pay3_out = ctx.dram("pay3_out", [4 * PAY3KV, PAY3W])
    qT0b = ctx.dram("qT0b", [D, TOK], BF16)
    qT1b = ctx.dram("qT1b", [D, TOK], BF16)
    kTf0 = ctx.dram("kTf0", [D, S])
    vf0 = ctx.dram("vf0", [S, D])
    kTf1 = ctx.dram("kTf1", [512, S])
    vf1 = ctx.dram("vf1", [S, 512])
    w1qTb = ctx.dram("w1qTb", [D, DFF1 // 2])
    w2qTb = ctx.dram("w2qTb", [DFF1, D // 2])
    x1b = ctx.dram("x1b", [TOK, D])
    x2b = ctx.dram("x2b", [TOK, D])
    x3b = ctx.dram("x3b", [TOK, D])

    _prefetch_consts(ctx)

    ph_wprep1(ctx, w1ts, w2ts, pay1_in[4 * D:4 * D + 1, :], "p1k")
    ph_wquant(ctx, [wq_s, wk_s, wv_s, wo_s],
              [pay1_in[0:D, :], pay1_in[D:2 * D, :], pay1_in[2 * D:3 * D, :],
               pay1_in[3 * D:4 * D, :]], 2, ["p1k", "p1k", "p1k", "p1k"])
    sb.cc(lambda e: e.collective_compute(
        "AllGather", ALU.bypass, GRP8,
        [pay1_in.opt()], [pay1_out.opt()]), reads=["p1k"], writes=["p1o"])

    p1v3 = pay1_out.rearrange("(r n) w -> r n w", n=PAY1_ROWS)
    wq_full = ("gath", p1v3, 0, D)
    wk_full = ("gath", p1v3, D, D)
    wv_full = ("gath", p1v3, 2 * D, D)
    wo_full = ("gath", p1v3, 3 * D, D)
    partials_all = p1v3[:, 4 * D, 0:16]

    w1q_view = payt_in[0:1024, :].bitcast(BF16).rearrange(
        "(a b) c -> a (b c)", b=4)
    w2q_view = payt_in[1024:2048, :].bitcast(BF16)
    ph_tern(ctx, partials_all, w1ts, w2ts, w1q_view, w2q_view, "p1o",
            ["ptk", "ptk"])
    sb.cc(lambda e: e.collective_compute(
        "AllGather", ALU.bypass, GRP8,
        [payt_in.opt()], [payt_out.opt()]), reads=["ptk"], writes=["pto"])
    with ctx.phase("fixw"):
        for r in range(8):
            sb.add("gpsimd", lambda e, r=r: e.dma_start(
                out=w1qTb[r * 256:(r + 1) * 256, :],
                in_=payt_out[r * PAYT: r * PAYT + 1024, :]
                .rearrange("(a b) c -> a (b c)", b=4)),
                reads=["pto"], writes=["w1f"], q="gd")
            sb.add("gpsimd", lambda e, r=r: e.dma_start(
                out=w2qTb[r * 1024:(r + 1) * 1024, :],
                in_=payt_out[r * PAYT + 1024: (r + 1) * PAYT, :]),
                reads=["pto"], writes=["w2f"], q="gd")

    ph_qkv(ctx, 0, x_in, wq_full, wk_full, wv_full, cos_in, sin_in,
           qT0b, pay2_in[0:2048, :],
           pay2_in[2048:4096, :].rearrange("(a b) c -> a (b c)", b=2),
           {"wq": "p1o", "wk": "p1o", "wv": "p1o"},
           {"q": "q0k", "k": "p2k", "v": "p2k"})
    sb.cc(lambda e: e.collective_compute(
        "AllGather", ALU.bypass, GRP4,
        [pay2_in.opt()], [pay2_out.opt()]), reads=["p2k"], writes=["p2o"])
    with ctx.phase("fix0"):
        for r4 in range(4):
            for slot in range(2):
                p = r4 if slot == 0 else 7 - r4
                sb.add("gpsimd", lambda e, r4=r4, slot=slot, p=p: e.dma_start(
                    out=kTf0[:, p * GSZ:(p + 1) * GSZ],
                    in_=pay2_out[r4 * PAY2KV: r4 * PAY2KV + 2048,
                                 slot * GSZ:(slot + 1) * GSZ]),
                    reads=["p2o"], writes=["ktf0"], q="gd")
                sb.add("gpsimd", lambda e, r4=r4, slot=slot, p=p: e.dma_start(
                    out=vf0[p * GSZ:(p + 1) * GSZ, :]
                    .rearrange("(a b) c -> a (b c)", b=2),
                    in_=pay2_out[r4 * PAY2KV + 2048 + slot * 1024:
                                 r4 * PAY2KV + 2048 + (slot + 1) * 1024, :]),
                    reads=["p2o"], writes=["vf0"], q="gd")

    ph_attn(ctx, 0, qT0b, kTf0, vf0, masks, x_in, wo_full, x1b,
            {"q": "q0k", "k": "ktf0", "v": "vf0", "wo": "p1o",
             "x": "_x", "masks": "_m"}, "x1k")
    ph_ffn(ctx, 0, x1b, gateT, upT, downT, (gsb, usb), dsr, x2b,
           {"x": "x1k"}, "x2k")
    ph_qkv(ctx, 1, x2b, wq1, wk1, wv1, cos_in, sin_in,
           qT1b, pay3_in[0:512, :],
           pay3_in[512:1024, :].rearrange("a (b c) -> (a b) c", b=2),
           {"x": "x2k"}, {"q": "q1k", "k": "p3k", "v": "p3k"})
    sb.cc(lambda e: e.collective_compute(
        "AllGather", ALU.bypass, GRP4,
        [pay3_in.opt()], [pay3_out.opt()]), reads=["p3k"], writes=["p3o"])
    with ctx.phase("fix1"):
        for r4 in range(4):
            for slot in range(2):
                p = r4 if slot == 0 else 7 - r4
                sb.add("gpsimd", lambda e, r4=r4, slot=slot, p=p: e.dma_start(
                    out=kTf1[:, p * GSZ:(p + 1) * GSZ],
                    in_=pay3_out[r4 * PAY3KV: r4 * PAY3KV + 512,
                                 slot * GSZ:(slot + 1) * GSZ]),
                    reads=["p3o"], writes=["ktf1"], q="gd")
                sb.add("gpsimd", lambda e, r4=r4, slot=slot, p=p: e.dma_start(
                    out=vf1[p * GSZ:(p + 1) * GSZ, :],
                    in_=pay3_out[r4 * PAY3KV + 512 + slot * 256:
                                 r4 * PAY3KV + 512 + (slot + 1) * 256, :]
                    .rearrange("a (b c) -> (a b) c", b=2)),
                    reads=["p3o"], writes=["vf1"], q="gd")

    ph_attn(ctx, 1, qT1b, kTf1, vf1, masks, x2b, wo1, x3b,
            {"q": "q1k", "k": "ktf1", "v": "vf1", "x": "x2k"}, "x3k")
    ph_ffn(ctx, 1, x3b, w1qTb.bitcast(BF16), None, w2qTb.bitcast(BF16),
           b1b, b2r, xo, {"x": "x3k", "wg": "w1f", "wd": "w2f"}, "xok")
    run_block(nc, sb, "fz")
    return nc


def kernel_fused_inputs(ins):
    f32 = np.float32
    x = np.ascontiguousarray(np.asarray(ins["L0_attn_hidden_states"], f32)
                             .reshape(B * S, D))
    w1T = np.ascontiguousarray(np.asarray(ins["L1_ffn_w1"], f32).T)
    w2T = np.ascontiguousarray(np.asarray(ins["L1_ffn_w2"], f32).T)
    cos2, sin2 = host_rope_tables()
    ims = []
    for c in range(NCORE):
        (r0, _), (r1, _) = core_token_rows(c)
        (_, p0), (_, p1) = core_token_rows(c)
        cosn = np.ascontiguousarray(np.concatenate(
            [cos2[:, p0 * GSZ:(p0 + 1) * GSZ], cos2[:, p1 * GSZ:(p1 + 1) * GSZ]], 1))
        sinn = np.ascontiguousarray(np.concatenate(
            [sin2[:, p0 * GSZ:(p0 + 1) * GSZ], sin2[:, p1 * GSZ:(p1 + 1) * GSZ]], 1))
        ims.append({
            "x_own": np.ascontiguousarray(
                np.concatenate([x[r0:r0 + GSZ], x[r1:r1 + GSZ]], 0)),
            "wq_s": np.ascontiguousarray(np.asarray(ins["L0_attn_Wq"], f32)[c * 256:(c + 1) * 256]),
            "wk_s": np.ascontiguousarray(np.asarray(ins["L0_attn_Wk"], f32)[c * 256:(c + 1) * 256]),
            "wv_s": np.ascontiguousarray(np.asarray(ins["L0_attn_Wv"], f32)[c * 256:(c + 1) * 256]),
            "wo_s": np.ascontiguousarray(np.asarray(ins["L0_attn_Wo"], f32)[c * 256:(c + 1) * 256]),
            "w1ts": np.ascontiguousarray(w1T[c * 256:(c + 1) * 256]),
            "w2ts": np.ascontiguousarray(w2T[c * 1024:(c + 1) * 1024]),
            "gateT": np.ascontiguousarray(np.asarray(ins["L0_ffn_gate_w"], f32).T),
            "upT": np.ascontiguousarray(np.asarray(ins["L0_ffn_up_w"], f32).T),
            "downT": np.ascontiguousarray(np.asarray(ins["L0_ffn_down_w"], f32).T),
            "gsb": host_band_vec(ins["L0_ffn_gate_s"]),
            "usb": host_band_vec(ins["L0_ffn_up_s"]),
            "dsr": np.ascontiguousarray(np.asarray(ins["L0_ffn_down_s"], f32)[None, :]),
            "wq1": np.ascontiguousarray(np.asarray(ins["L1_attn_wq"], f32)),
            "wk1": np.ascontiguousarray(np.asarray(ins["L1_attn_wk"], f32)),
            "wv1": np.ascontiguousarray(np.asarray(ins["L1_attn_wv"], f32)),
            "wo1": np.ascontiguousarray(np.asarray(ins["L1_attn_wo"], f32)),
            "b1b": host_band_vec(ins["L1_ffn_b1"]),
            "b2r": np.ascontiguousarray(np.asarray(ins["L1_ffn_b2"], f32)[None, :]),
            "masks": host_masks(c),
            "cos_own": cosn, "sin_own": sinn,
        })
    return ims


def kernel_fused(**ins):
    ims = kernel_fused_inputs(ins)
    if "fz" not in _NC_CACHE:
        _NC_CACHE["fz"] = build_fused()
    res = run_bass_kernel_spmd(_NC_CACHE["fz"], ims, core_ids=list(range(NCORE)))
    f32 = np.float32
    x4 = np.zeros((B * S, D), dtype=f32)
    for c in range(NCORE):
        (r0, _), (r1, _) = core_token_rows(c)
        x4[r0:r0 + GSZ] = np.asarray(res.results[c]["xo"])[:GSZ]
        x4[r1:r1 + GSZ] = np.asarray(res.results[c]["xo"])[GSZ:]
    return x4.reshape(B, S, D)


def kernel(**ins):
    return kernel_3launch(**ins)
